# version 04
# baseline (speedup 1.0000x reference)
import contextlib
import numpy as np
import concourse.bass as bass
import concourse.mybir as mybir
from concourse.ap import AP

F32 = mybir.dt.float32
BF16 = mybir.dt.bfloat16
FP16 = mybir.dt.float16
I16 = mybir.dt.int16
AF = mybir.ActivationFunctionType
ALU = mybir.AluOpType
AX = mybir.AxisListType
ENGS = ("sync", "tensor", "vector", "scalar", "gpsimd")
D = 1024
DH = 4096


class Task:
    __slots__ = ("eng", "emit", "deps", "dma", "token", "do_inc", "label")

    def __init__(self, eng, emit, deps, dma):
        self.label = None
        self.eng = eng
        self.emit = emit
        self.deps = [d for d in deps if d is not None]
        self.dma = dma
        self.token = None
        self.do_inc = False


class _HazEng:
    _PASS = ("wait_ge", "dma_start", "engine_nop")

    def __init__(self, e, sem, state):
        self._e = e
        self._sem = sem
        self._st = state
        self.last_r = None
        self.last_short = False

    def __getattr__(self, name):
        real = getattr(self._e, name)
        if name in self._PASS or not callable(real):
            return real

        def wrap(*a, **k):
            r = real(*a, **k)
            short = True
            try:
                short = (a[0].free_size < 512) or (k.get("accum_out") is not None)
            except Exception:
                short = True
            if short:
                self._st[0] += 1
                r.then_inc(self._sem, 1)
                self._e.wait_ge(self._sem, self._st[0])
            self.last_r = r
            self.last_short = short
            return r
        return wrap


class Prog:
    def __init__(self, nc, serial=False):
        self.nc = nc
        self.per_eng = {e: [] for e in ENGS}
        self.esem = {e: nc.alloc_semaphore(f"se_{e}") for e in ENGS}
        self.dsem = {e: nc.alloc_semaphore(f"sd_{e}") for e in ENGS}
        self.hsem = {e: nc.alloc_semaphore(f"hz_{e}")
                     for e in ("vector", "scalar", "gpsimd")}
        self.dummy = nc.alloc_sbuf_tensor("hz_dummy", [1, 1],
                                          mybir.dt.float32).ap()
        self.serial = serial
        self.all = []

    def _add(self, task):
        if self.serial and self.all:
            task.deps = task.deps + [self.all[-1]]
        self.all.append(task)
        self.per_eng[task.eng].append(task)
        return task

    def t(self, eng, emit, deps=()):
        return self._add(Task(eng, emit, deps, dma=False))

    def dma(self, eng, emit, deps=()):
        return self._add(Task(eng, emit, deps, dma=True))

    def tails(self):
        return [self.per_eng[e][-1] for e in ENGS if self.per_eng[e]]

    def finalize(self):
        needed = set()
        for e in ENGS:
            for task in self.per_eng[e]:
                for d in task.deps:
                    if d.dma or d.eng != task.eng:
                        needed.add(id(d))
        ccount = {e: 0 for e in ENGS}
        dcount = {e: 0 for e in ENGS}
        for e in ENGS:
            for task in self.per_eng[e]:
                if task.dma:
                    dcount[e] += 16
                    task.token = (self.dsem[e], dcount[e])
                    task.do_inc = True
                elif id(task) in needed:
                    ccount[e] += 1
                    task.token = (self.esem[e], ccount[e])
                    task.do_inc = True
        with self.nc.Block() as block:
            for ename in ENGS:
                tasks = self.per_eng[ename]
                if not tasks:
                    continue

                def body(eng, tasks=tasks, ename=ename):
                    maxw = {}
                    if ename in self.hsem:
                        heng = _HazEng(eng, self.hsem[ename], [0])
                    else:
                        heng = eng
                    for task in tasks:
                        for d in task.deps:
                            if (not d.dma) and d.eng == ename:
                                continue
                            semh, val = d.token
                            key = id(semh)
                            if maxw.get(key, 0) >= val:
                                continue
                            eng.wait_ge(semh, val)
                            maxw[key] = val
                        last = task.emit(eng if task.dma else heng)
                        if task.do_inc and last is not None:
                            if (heng is not eng and heng.last_r is last
                                    and heng.last_short):
                                if ename == "gpsimd":
                                    last = eng.engine_nop()
                                elif ename == "scalar":
                                    last = eng.copy(self.dummy[0:1, 0:1],
                                                    self.dummy[0:1, 0:1])
                                else:
                                    last = eng.memset(self.dummy[0:1, 0:1], 0.0)
                            semh, _ = task.token
                            last.then_inc(semh, 16 if task.dma else 1)
                getattr(block, ename)(body)


def build(cfg):
    T_EXT = cfg["T_EXT"]
    NT = T_EXT // 128
    SPW = 32 + T_EXT + 32
    ACH = min(512, T_EXT)
    CHUNKS = []
    cc = 0
    while cc < T_EXT:
        CHUNKS.append((cc, min(ACH, T_EXT - cc)))
        cc += ACH
    OWN0 = cfg["OWN0"]
    OWN = cfg["OWN"]
    OSUB0 = OWN0 // 128
    NOSUB = OWN // 128
    mg_first = cfg["maskgrp_first"]
    dev_stop = cfg.get("dev_stop")

    nc = bass.Bass("TRN2", target_bir_lowering=False, detect_race_conditions=False)
    p = Prog(nc, serial=cfg.get("serial", False))
    stack = contextlib.ExitStack()

    x_in = nc.dram_tensor("x_in", [T_EXT, D], F32, kind="ExternalInput").ap()
    w_att = {n: nc.dram_tensor(f"w_{n}", [D, D], F32, kind="ExternalInput").ap()
             for n in ("q", "k", "kp", "v", "o")}
    gamma_i = nc.dram_tensor("gamma", [D], F32, kind="ExternalInput").ap()
    beta_i = nc.dram_tensor("beta", [D], F32, kind="ExternalInput").ap()
    w1_i = nc.dram_tensor("w1", [DH, D], F32, kind="ExternalInput").ap()
    cw_i = nc.dram_tensor("conv_w", [DH, 3], F32, kind="ExternalInput").ap()
    cb_i = nc.dram_tensor("conv_b", [DH], F32, kind="ExternalInput").ap()
    w2_i = nc.dram_tensor("w2", [D, DH], F32, kind="ExternalInput").ap()
    ident_i = nc.dram_tensor("ident", [128, 128], F32, kind="ExternalInput").ap()
    mask_i = nc.dram_tensor("mask", [2, 128, 992], F32, kind="ExternalInput").ap()
    psmask_i = nc.dram_tensor("psmask", [128, 512], F32, kind="ExternalInput").ap()
    out = nc.dram_tensor("out", [OWN, D], F32, kind="ExternalOutput").ap()

    dr = lambda n, s, dt: nc.dram_tensor(n, s, dt, kind="Internal").ap()
    wq1T_d = dr("wq1T_d", [8, 128, DH], FP16)
    wq2T_d = dr("wq2T_d", [32, 128, D], FP16)
    wT_d = {n: dr(f"w{n}T_d", [8, 128, D], BF16) for n in ("q", "k", "kp", "v", "o")}
    xA_d = dr("xA_d", [T_EXT, D], F32)
    xB_d = dr("xB_d", [T_EXT, D], F32)
    spl = {n: dr(f"{n}_d", [8, 128, SPW], BF16) for n in ("q", "k", "kp", "v")}
    xh2_d = dr("xh2_d", [T_EXT, D], F32)
    xh2T_d = dr("xh2T_d", [8, 128, T_EXT], BF16)
    dbg_d = dr("dbg_d", [8, 128, 4096], BF16)

    sb = lambda n, s, dt=F32: nc.alloc_sbuf_tensor(n, s, dt)
    IDF = sb("IDF", [128, 128], F32)
    IDB = sb("IDB", [128, 128], BF16)
    IDH = sb("IDH", [128, 128], FP16)
    X0 = sb("X0", [128, D])
    X1 = sb("X1", [128, D])
    TT_ = sb("TT_", [128, D])
    XQI = sb("XQI", [128, D], I16)
    EE_ = sb("EE_", [128, D])
    XQH = sb("XQH", [128, D], FP16)
    WTMP = sb("WTMP", [128, D], BF16)
    WTMO = sb("WTMO", [128, D], BF16)
    ROWT = sb("ROWT", [1, 132])
    GB = sb("GB", [128, D])
    BB = sb("BB", [128, D])
    WCV = sb("WCV", [128, 32, 4])
    CORR1 = sb("CORR1", [128, 32])
    CORR2 = sb("CORR2", [128, 8])
    SW1 = sb("SW1", [128, 1])
    ISW1 = sb("ISW1", [128, 1])
    SW2 = sb("SW2", [128, 1])
    ISW2 = sb("ISW2", [128, 1])
    ONESC = sb("ONESC", [128, 1], F32)
    ONESR = sb("ONESR", [1, 128], F32)
    ZEROB = sb("ZEROB", [128, 32], BF16)
    names = "SUMX SUMQ MU RST TS0 TS1 TS2 MXC MNC RSC BCC ALC BE2".split()
    C = {n: sb(n, [128, 1]) for n in names}

    PSA = nc.alloc_psum_tensor("PSA", [128, 1024], F32)
    PSB = nc.alloc_psum_tensor("PSB", [128, 512], F32)
    PSC = nc.alloc_psum_tensor("PSC", [128, 512], F32)
    PSD = nc.alloc_psum_tensor("PSD", [128, 512], F32)
    PST2 = nc.alloc_psum_tensor("PST2", [128, 1024], F32)
    PSTB = nc.alloc_psum_tensor("PSTB", [128, 1024], BF16)
    psa, psb, psc, psd = PSA.ap(), PSB.ap(), PSC.ap(), PSD.ap()
    pst2, pstb = PST2.ap(), PSTB.ap()
    psth = pstb.bitcast(FP16)
    psth3 = psth.rearrange("p (a b) -> p a b", b=128)

    def mset(eng, tile_ap, val, deps=()):
        return p.t(eng, lambda e, t=tile_ap, v=val: e.memset(t, v), deps)

    t_ones_c = mset("gpsimd", ONESC[:], 1.0)
    t_ones_r = mset("gpsimd", ONESR[:], 1.0)
    t_zerob = mset("gpsimd", ZEROB[:], 0.0)
    ld_id = p.dma("sync", lambda e: e.dma_start(out=IDF[:], in_=ident_i[:, :]))
    t_idb = p.t("vector", lambda e: e.tensor_copy(IDB[:], IDF[:]), [ld_id])
    t_idh = p.t("vector", lambda e: e.tensor_copy(IDH[:], IDF[:]), [ld_id])
    ld_gb = p.dma("sync", lambda e: e.dma_start(
        out=GB[:], in_=AP(gamma_i.tensor, 0, [[0, 128], [1, D]])))
    ld_bb = p.dma("sync", lambda e: e.dma_start(
        out=BB[:], in_=AP(beta_i.tensor, 0, [[0, 128], [1, D]])))
    ld_cw = p.dma("sync", lambda e: e.dma_start(
        out=AP(WCV, 0, [[128, 128], [4, 32], [1, 3]]),
        in_=AP(cw_i.tensor, 0, [[3, 128], [3 * 128, 32], [1, 3]])))
    ld_cb = p.dma("sync", lambda e: e.dma_start(
        out=AP(WCV, 3, [[128, 128], [4, 32], [1, 1]]),
        in_=AP(cb_i.tensor, 0, [[1, 128], [128, 32], [0, 1]]),
        allow_slow_non_contiguous=True))

    t_ai1 = mset("vector", C["SUMX"][:], 0.0)
    t_ai2 = mset("vector", C["SUMQ"][:], 0.0)
    prev_buf = [None, None]
    w2_flat = w2_i.rearrange("a (b c) -> (a b) c", c=D)
    red31 = red63 = None
    for i in range(64):
        buf = X0 if i % 2 == 0 else X1
        src = w1_i[i * 128:(i + 1) * 128, :] if i < 32 else \
            w2_flat[(i - 32) * 128:(i - 31) * 128, :]
        ld = p.dma("sync", lambda e, b=buf, s=src: e.dma_start(out=b[:], in_=s),
                   [prev_buf[i % 2]])
        acc = C["SUMX"] if i < 32 else C["SUMQ"]
        dep0 = t_ai1 if i < 32 else t_ai2

        def em(e, b=buf, a=acc):
            e.tensor_reduce(C["TS0"][:], b[:], AX.X, ALU.add, apply_absolute_value=True)
            return e.tensor_tensor(a[:], a[:], C["TS0"][:], ALU.add)
        tk = p.t("vector", em, [ld, dep0])
        prev_buf[i % 2] = tk
        if i == 31:
            red31 = tk
        if i == 63:
            red63 = tk

    def em_tot(e):
        e.matmul(psd[0:1, 0:1], ONESC[:, 0:1], C["SUMX"][:, 0:1], start=True, stop=True)
        return e.matmul(psd[0:1, 1:2], ONESC[:, 0:1], C["SUMQ"][:, 0:1],
                        start=True, stop=True, skip_group_check=True)
    t_tot = p.t("tensor", em_tot, [red31, red63, t_ones_c])
    t_mean = p.t("vector", lambda e: e.tensor_scalar(
        ROWT[0:1, 0:2], psd[0:1, 0:2], 1.0 / (DH * D), 1e-5, ALU.mult, ALU.max), [t_tot])
    t_bc = p.t("tensor", lambda e: e.matmul(
        psd[:, 2:4], ONESR[0:1, :], ROWT[0:1, 0:2], start=True, stop=True,
        skip_group_check=True), [t_mean, t_ones_r])

    def em_sw(e):
        e.tensor_copy(ISW1[:], psd[:, 2:3])
        e.tensor_copy(ISW2[:], psd[:, 3:4])
        e.reciprocal(SW1[:], ISW1[:])
        return e.reciprocal(SW2[:], ISW2[:])
    t_sw = p.t("vector", em_sw, [t_bc])

    with nc.sbuf_tensor("P0EV", [128, 1024], FP16) as P0EV:
        prev_buf = [t_sw, t_sw]
        t_ci1 = mset("vector", CORR1[:], 0.0, [t_sw])
        t_ci2 = mset("vector", CORR2[:], 0.0, [t_sw])
        p0_tail = []
        prev_ev = t_idh
        t_cs_last = None
        for i in range(64):
            buf = X0 if i % 2 == 0 else X1
            if i < 32:
                src = w1_i[i * 128:(i + 1) * 128, :]
                sw = SW1
            else:
                r2, cc2 = (i - 32) // 4, (i - 32) % 4
                src = w2_i[128 * r2:128 * (r2 + 1), 1024 * cc2:1024 * (cc2 + 1)]
                sw = SW2
            ld = p.dma("sync", lambda e, b=buf, s=src: e.dma_start(out=b[:], in_=s),
                       [prev_buf[i % 2]])
            t_t = p.t("scalar", lambda e, b=buf, s=sw: e.mul(TT_[:], b[:], s[:, 0:1]),
                      [ld, t_sw])
            prev_buf[i % 2] = t_t

            def em_q(e):
                e.tensor_scalar(EE_[:], TT_[:], 128.0, None, ALU.add)
                e.tensor_copy(XQI[:], EE_[:])
                e.tensor_tensor(EE_[:], EE_[:], XQI[:], ALU.subtract)
                e.scalar_tensor_tensor(TT_[:], EE_[:], 0.5, XQI[:], ALU.is_ge, ALU.add)
                e.tensor_scalar(TT_[:], TT_[:], 129.0, 127.0, ALU.min, ALU.max)
                return e.tensor_scalar(XQH[:], TT_[:], 128.0, None, ALU.subtract)
            t_q = p.t("vector", em_q, [t_t])
            if i < 32:
                def em_cs(e, col=i):
                    e.tensor_reduce(C["TS0"][:], XQH[:], AX.X, ALU.add)
                    return e.tensor_tensor(CORR1[:, col:col + 1], CORR1[:, col:col + 1],
                                           C["TS0"][:], ALU.add)
                t_cs = p.t("vector", em_cs, [t_q, t_ci1])
            else:
                r2 = (i - 32) // 4

                def em_cs(e, col=r2):
                    e.tensor_reduce(C["TS0"][:], XQH[:], AX.X, ALU.add)
                    return e.tensor_tensor(CORR2[:, col:col + 1], CORR2[:, col:col + 1],
                                           C["TS0"][:], ALU.add)
                t_cs = p.t("vector", em_cs, [t_q, t_ci2])
            t_cs_last = t_cs

            def em_tr(e):
                for j in range(8):
                    r = e.transpose(psth[:, j * 128:(j + 1) * 128],
                                    XQH[:, j * 128:(j + 1) * 128], IDH[:])
                return r
            t_tr = p.t("tensor", em_tr, [t_q, t_idh, prev_ev])
            t_ev = p.t("vector", lambda e: e.tensor_copy(P0EV[:], psth[:, 0:1024]),
                       [t_tr])
            prev_ev = t_ev
            if i < 32:
                sp = lambda e, ii=i: e.dma_start(
                    out=AP(wq1T_d.tensor, ii * 128, [[DH, 128], [128 * DH, 8], [1, 128]]),
                    in_=AP(P0EV, 0, [[1024, 128], [128, 8], [1, 128]]))
            else:
                r2, cc2 = (i - 32) // 4, (i - 32) % 4
                sp = lambda e, r2=r2, cc2=cc2: e.dma_start(
                    out=AP(wq2T_d.tensor, (cc2 * 8) * 128 * D + r2 * 128,
                           [[D, 128], [128 * D, 8], [1, 128]]),
                    in_=AP(P0EV, 0, [[1024, 128], [128, 8], [1, 128]]))
            t_sp = p.dma("sync", sp, [t_ev])
            p0_tail.append(t_sp)

        t_corr = p.t("vector", lambda e: (
            e.tensor_scalar(CORR1[:], CORR1[:], 128.0, None, ALU.mult),
            e.tensor_scalar(CORR2[:], CORR2[:], 128.0, None, ALU.mult))[-1],
            [t_cs_last])

        for name in ("q", "k", "kp", "v", "o"):
            w_src = w_att[name]
            for i in range(8):
                buf = X0 if i % 2 == 0 else X1
                ld = p.dma("sync", lambda e, b=buf, s=w_src[i * 128:(i + 1) * 128, :]:
                           e.dma_start(out=b[:], in_=s), [prev_buf[i % 2]])
                t_cast = p.t("vector", lambda e, b=buf: e.tensor_copy(WTMP[:], b[:]),
                             [ld, prev_ev])
                prev_buf[i % 2] = t_cast

                def em_tr(e):
                    for j in range(8):
                        r = e.transpose(pstb[:, j * 128:(j + 1) * 128],
                                        WTMP[:, j * 128:(j + 1) * 128], IDB[:])
                    return r
                t_tr = p.t("tensor", em_tr, [t_cast, t_idb])
                t_ev2 = p.t("vector", lambda e: e.tensor_copy(WTMO[:], pstb[:, 0:1024]),
                            [t_tr])
                prev_ev = t_ev2
                t_sp = p.dma("sync", lambda e, n=name, ii=i: e.dma_start(
                    out=AP(wT_d[n].tensor, ii * 128, [[D, 128], [128 * D, 8], [1, 128]]),
                    in_=AP(WTMO, 0, [[1024, 128], [128, 8], [1, 128]])), [t_ev2])
                p0_tail.append(t_sp)
        p0_done = p0_tail + [t_corr]

    def ffn_phase(x_src, x_dst, own_only, entry_deps, tag=""):
        with contextlib.ExitStack() as fs:
            WQ1T = fs.enter_context(nc.sbuf_tensor("WQ1T" + tag, [128, 8, DH], FP16))
            WQ2T = fs.enter_context(nc.sbuf_tensor("WQ2T" + tag, [128, 32, D], FP16))
            XQT = fs.enter_context(nc.sbuf_tensor("XQT" + tag, [128, 8, 3, 130], FP16))
            YS = fs.enter_context(nc.sbuf_tensor("YS" + tag, [128, 32, 128], F32))
            XQ2 = fs.enter_context(nc.sbuf_tensor("XQ2" + tag, [128, 32, 128], FP16))
            OUT2 = fs.enter_context(nc.sbuf_tensor("OUT2" + tag, [128, D], F32))
            XN = fs.enter_context(nc.sbuf_tensor("XN" + tag, [128, D], F32))
            ALPH = fs.enter_context(nc.sbuf_tensor("ALPH" + tag, [128, 130], F32))
            S2B = fs.enter_context(nc.sbuf_tensor("S2B" + tag, [128, 128], F32))
            H_ = fs.enter_context(nc.sbuf_tensor("H_" + tag, [128, 130], F32))
            TC = fs.enter_context(nc.sbuf_tensor("TC" + tag, [128, 128], F32))
            T2 = fs.enter_context(nc.sbuf_tensor("T2" + tag, [128, 128], F32))
            YY = fs.enter_context(nc.sbuf_tensor("YY" + tag, [128, 128], F32))
            M128 = fs.enter_context(nc.sbuf_tensor("M128" + tag, [128, 128], F32))
            A2 = fs.enter_context(nc.sbuf_tensor("A2" + tag, [128, 128], F32))
            I2 = fs.enter_context(nc.sbuf_tensor("I2" + tag, [128, 128], I16))
            E2 = fs.enter_context(nc.sbuf_tensor("E2" + tag, [128, 128], F32))
            ARW = fs.enter_context(nc.sbuf_tensor("ARW" + tag, [1, 3 * 130], F32))

            ld_wq1 = p.dma("sync", lambda e: e.dma_start(
                out=AP(WQ1T, 0, [[8 * DH, 128], [DH, 8], [1, DH]]),
                in_=AP(wq1T_d.tensor, 0, [[DH, 128], [128 * DH, 8], [1, DH]])),
                entry_deps)
            ld_wq2 = p.dma("sync", lambda e: e.dma_start(
                out=AP(WQ2T, 0, [[32 * D, 128], [D, 32], [1, D]]),
                in_=AP(wq2T_d.tensor, 0, [[D, 128], [128 * D, 32], [1, D]])),
                entry_deps)
            t_xqt0 = mset("gpsimd", XQT[:], 0.0, entry_deps)
            t_arw = mset("gpsimd", ARW[:], 1.0, entry_deps)

            KSLOTS = [(psa, 0), (psa, 512), (psb, 0)]
            t_join = p.t("gpsimd", lambda e: e.engine_nop(), entry_deps)
            deps_x = [t_join, t_join]
            ring_task = [t_xqt0, t_xqt0, t_xqt0]
            tr_prev = [None]
            stat_prev = [None]
            prev_rows = [None]
            prev_tra = [None]
            mm_done_prev = [None]
            last_bcs_read = [None]
            st_tasks = []
            tasks_q = {}
            for j in range(NT + 1):
                if j < NT:
                    buf = X0 if j % 2 == 0 else X1
                    ld = p.dma("sync", lambda e, b=buf,
                               s=x_src[j * 128:(j + 1) * 128, :]:
                               e.dma_start(out=b[:], in_=s), [deps_x[j % 2]])
                    t_sq = p.t("scalar", lambda e, b=buf: e.activation(
                        TT_[:], b[:], AF.Square, accum_out=C["SUMQ"][:, 0:1]),
                        [ld, stat_prev[0]])

                    def em_st1(e, b=buf):
                        e.tensor_reduce(C["SUMX"][:], b[:], AX.X, ALU.add)
                        e.tensor_reduce(C["MXC"][:], b[:], AX.X, ALU.max)
                        e.tensor_reduce(C["MNC"][:], b[:], AX.X, ALU.min)
                        e.tensor_scalar(C["MU"][:], C["SUMX"][:], 1.0 / D, None, ALU.mult)
                        e.tensor_scalar(C["TS0"][:], C["SUMQ"][:], 1.0 / D, None, ALU.mult)
                        e.tensor_tensor(C["TS1"][:], C["MU"][:], C["MU"][:], ALU.mult)
                        e.tensor_tensor(C["TS0"][:], C["TS0"][:], C["TS1"][:], ALU.subtract)
                        return e.tensor_scalar(C["TS0"][:], C["TS0"][:], 1e-5, None, ALU.add)
                    t_st1 = p.t("vector", em_st1, [ld, t_sq])
                    t_sqr = p.t("scalar", lambda e: e.activation(
                        C["TS2"][:], C["TS0"][:], AF.Sqrt), [t_st1])

                    def em_st2(e):
                        e.reciprocal(C["RST"][:], C["TS2"][:])
                        for _ in range(2):
                            e.tensor_tensor(C["TS1"][:], C["RST"][:], C["RST"][:], ALU.mult)
                            e.tensor_tensor(C["TS1"][:], C["TS1"][:], C["TS0"][:], ALU.mult)
                            e.tensor_scalar(C["TS1"][:], C["TS1"][:], -0.5, 1.5,
                                            ALU.mult, ALU.add)
                            e.tensor_tensor(C["RST"][:], C["RST"][:], C["TS1"][:], ALU.mult)
                        e.tensor_tensor(C["TS1"][:], C["MXC"][:], C["MU"][:], ALU.subtract)
                        e.tensor_tensor(C["TS2"][:], C["MU"][:], C["MNC"][:], ALU.subtract)
                        e.tensor_tensor(C["TS1"][:], C["TS1"][:], C["TS2"][:], ALU.max)
                        e.tensor_tensor(C["TS1"][:], C["TS1"][:], C["RST"][:], ALU.mult)
                        e.tensor_scalar(C["TS1"][:], C["TS1"][:], 1e-5, None, ALU.max)
                        e.tensor_scalar(C["ALC"][:], C["TS1"][:], 1.0 / 127.0, None, ALU.mult)
                        e.tensor_tensor(C["ALC"][:], C["ALC"][:], ISW1[:], ALU.mult)
                        e.reciprocal(C["TS2"][:], C["TS1"][:])
                        e.tensor_scalar(C["TS2"][:], C["TS2"][:], 127.0, None, ALU.mult)
                        e.tensor_tensor(C["RSC"][:], C["RST"][:], C["TS2"][:], ALU.mult)
                        e.tensor_tensor(C["BCC"][:], C["MU"][:], C["RSC"][:], ALU.mult)
                        return e.tensor_scalar(C["BCC"][:], C["BCC"][:], -1.0, 128.0,
                                               ALU.mult, ALU.add)
                    t_st2 = p.t("vector", em_st2, [t_sqr, t_sw])
                    t_tq = p.t("scalar", lambda e, b=buf: e.activation(
                        TT_[:], b[:], AF.Identity, bias=C["BCC"][:, 0:1],
                        scale=C["RSC"][:, 0:1]), [t_st2])

                    def em_rnd(e):
                        e.tensor_copy(XQI[:], TT_[:])
                        e.tensor_tensor(EE_[:], TT_[:], XQI[:], ALU.subtract)
                        return e.scalar_tensor_tensor(XQH[:], EE_[:], 0.5, XQI[:],
                                                      ALU.is_ge, ALU.add)
                    t_rnd = p.t("vector", em_rnd, [t_tq])
                    stat_prev[0] = t_rnd

                    def em_trq(e):
                        for jj in range(8):
                            r = e.transpose(psth[:, jj * 128:(jj + 1) * 128],
                                            XQH[:, jj * 128:(jj + 1) * 128], IDH[:])
                        return r
                    t_trq = p.t("tensor", em_trq, [t_rnd, t_idh, tr_prev[0]])

                    def em_evq(e, slot=j % 3, sl=(j - 1) % 3, sr=(j + 1) % 3):
                        e.tensor_copy(AP(XQT, slot * 130 + 1,
                                         [[8 * 3 * 130, 128], [3 * 130, 8], [1, 128]]),
                                      psth3)
                        e.tensor_copy(AP(XQT, sl * 130 + 129,
                                         [[8 * 3 * 130, 128], [3 * 130, 8], [1, 1]]),
                                      psth3[:, :, 0:1])
                        return e.tensor_copy(
                            AP(XQT, sr * 130 + 0,
                               [[8 * 3 * 130, 128], [3 * 130, 8], [1, 1]]),
                            psth3[:, :, 127:128])
                    t_evq = p.t("vector", em_evq, [t_trq, t_xqt0])
                    ring_task[j % 3] = t_evq
                    tr_prev[0] = t_evq
                    t_tra = p.t("tensor", lambda e: e.transpose(
                        psc[0:1, 0:128], C["ALC"][:, 0:1], IDF[:]),
                        [t_st2, ld_id, prev_rows[0]])
                    prev_tra[0] = t_tra

                    def em_eva(e, slot=j % 3, sl=(j - 1) % 3, sr=(j + 1) % 3):
                        e.tensor_copy(ARW[0:1, slot * 130 + 1:slot * 130 + 129],
                                      psc[0:1, 0:128])
                        e.tensor_copy(ARW[0:1, sl * 130 + 129:sl * 130 + 130],
                                      psc[0:1, 0:1])
                        return e.tensor_copy(ARW[0:1, sr * 130:sr * 130 + 1],
                                             psc[0:1, 127:128])
                    t_eva = p.t("vector", em_eva, [t_tra, t_arw])
                    tasks_q[j] = (t_evq, t_eva, ld, buf)

                i = j - 1
                if i < 0 or i >= NT:
                    continue
                buf_i = tasks_q[i][3]
                dep_ring = [ring_task[0], ring_task[1], ring_task[2]]
                deps_bca = [tasks_q[i][1]]
                if i + 1 in tasks_q:
                    deps_bca.append(tasks_q[i + 1][1])
                t_bca = p.t("tensor", lambda e, slot=i % 3: e.matmul(
                    psd[:, 0:130], ONESR[0:1, :], ARW[0:1, slot * 130:slot * 130 + 130],
                    start=True, stop=True, skip_group_check=True),
                    deps_bca + [last_bcs_read[0]])
                t_alph = p.t("vector", lambda e: e.tensor_copy(ALPH[:], psd[:, 0:130]),
                             [t_bca])

                t_ys_last = None
                t_h = None
                t_tp = None
                for cg in range(0, 32, 3):
                    ncg = min(3, 32 - cg)

                    def em_mm1(e, cg=cg, ncg=ncg, ii=i):
                        for ci in range(ncg):
                            c = cg + ci
                            pk, off = KSLOTS[ci]
                            for d8 in range(8):
                                r = e.matmul(
                                    pk[:, off:off + 130],
                                    AP(WQ1T, d8 * DH + c * 128,
                                       [[8 * DH, 128], [1, 128]]),
                                    AP(XQT, d8 * 3 * 130 + (ii % 3) * 130,
                                       [[8 * 3 * 130, 128], [1, 130]]),
                                    start=(d8 == 0), stop=(d8 == 7))
                        return r
                    mm1 = p.t("tensor", em_mm1, dep_ring + [ld_wq1, mm_done_prev[0]])
                    for ci in range(ncg):
                        c = cg + ci
                        pk, off = KSLOTS[ci]
                        t_h = p.t("vector", lambda e, pk=pk, off=off, c=c:
                                  e.scalar_tensor_tensor(H_[:], pk[:, off:off + 130],
                                                         CORR1[:, c:c + 1], ALPH[:],
                                                         ALU.subtract, ALU.mult),
                                  [mm1, t_alph, t_corr, t_tp])
                        t_tc = p.t("scalar", lambda e, c=c: e.activation(
                            TC[:], H_[:, 1:129], AF.Identity, bias=WCV[:, c, 3:4],
                            scale=WCV[:, c, 1:2]), [t_h, ld_cw, ld_cb])

                        def em_taps(e, c=c):
                            e.scalar_tensor_tensor(T2[:], H_[:, 0:128], WCV[:, c, 0:1],
                                                   TC[:], ALU.mult, ALU.add)
                            return e.scalar_tensor_tensor(YY[:], H_[:, 2:130],
                                                          WCV[:, c, 2:3], T2[:],
                                                          ALU.mult, ALU.add)
                        t_tp = p.t("vector", em_taps, [t_tc])
                        t_ys = p.t("scalar", lambda e, c=c: e.activation(
                            YS[:, c, :], YY[:], AF.Silu), [t_tp])
                        t_ys_last = t_ys
                    mm_done_prev[0] = t_h

                t_m = p.t("vector", lambda e: e.tensor_reduce(
                    M128[:], AP(YS, 0, [[32 * 128, 128], [1, 128], [128, 32]]),
                    AX.X, ALU.max, apply_absolute_value=True), [t_ys_last])
                t_trm = p.t("tensor", lambda e: e.transpose(
                    pst2[:, 0:128], M128[:], IDF[:]), [t_m, ld_id, mm_done_prev[0]])

                def em_s2(e):
                    e.tensor_reduce(C["TS0"][:], pst2[:, 0:128], AX.X, ALU.max)
                    e.tensor_scalar(C["TS0"][:], C["TS0"][:], 1e-5, None, ALU.max)
                    e.reciprocal(C["TS1"][:], C["TS0"][:])
                    e.tensor_scalar(C["TS1"][:], C["TS1"][:], 127.0, None, ALU.mult)
                    e.tensor_scalar(C["BE2"][:], C["TS0"][:], 0.5 / 127.0, None, ALU.mult)
                    return e.tensor_tensor(C["BE2"][:], C["BE2"][:], ISW2[:], ALU.mult)
                t_s2 = p.t("vector", em_s2, [t_trm, t_sw])
                t_trs = p.t("tensor", lambda e: e.transpose(
                    psc[0:1, 128:256], C["TS1"][:, 0:1], IDF[:]),
                    [t_s2, ld_id, prev_tra[0]])
                t_rows = p.t("vector", lambda e: e.tensor_copy(
                    ROWT[0:1, 0:128], psc[0:1, 128:256]), [t_trs])
                prev_rows[0] = t_rows
                t_bcs = p.t("tensor", lambda e: e.matmul(
                    psd[:, 256:384], ONESR[0:1, :], ROWT[0:1, 0:128],
                    start=True, stop=True, skip_group_check=True),
                    [t_rows, t_alph])
                t_s2b = p.t("vector", lambda e: e.tensor_copy(S2B[:], psd[:, 256:384]),
                            [t_bcs])
                last_bcs_read[0] = t_s2b

                t_q2_last = t_s2b
                for c in range(32):
                    def em_q2(e, c=c):
                        e.tensor_tensor(A2[:], YS[:, c, :], S2B[:], ALU.mult)
                        e.tensor_scalar(I2[:], A2[:], 128.0, None, ALU.add)
                        e.scalar_tensor_tensor(E2[:], A2[:], 128.0, I2[:],
                                               ALU.add, ALU.subtract)
                        return e.scalar_tensor_tensor(XQ2[:, c, :], E2[:], 0.5, I2[:],
                                                      ALU.is_ge, ALU.add)
                    t_q2_last = p.t("vector", em_q2, [t_q2_last])

                def em_mm2(e):
                    for bank in range(2):
                        first = True
                        for d8 in range(bank * 4, bank * 4 + 4):
                            for c in range(32):
                                r = e.matmul(pst2[:, d8 * 128:(d8 + 1) * 128],
                                             AP(WQ2T, c * D + d8 * 128,
                                                [[32 * D, 128], [1, 128]]),
                                             XQ2[:, c, :],
                                             start=first,
                                             stop=(d8 == bank * 4 + 3 and c == 31),
                                             skip_group_check=True)
                                first = False
                    return r
                mm2 = p.t("tensor", em_mm2, [t_q2_last, ld_wq2, t_trm])

                def em_o2(e):
                    for d8 in range(8):
                        r = e.tensor_scalar(OUT2[:, d8 * 128:(d8 + 1) * 128],
                                            pst2[:, d8 * 128:(d8 + 1) * 128],
                                            CORR2[:, d8:d8 + 1], None, ALU.subtract)
                    return r
                t_o2 = p.t("vector", em_o2, [mm2, t_corr])

                def em_tro(e):
                    for jj in range(8):
                        r = e.transpose(pst2[:, jj * 128:(jj + 1) * 128],
                                        OUT2[:, jj * 128:(jj + 1) * 128], IDF[:])
                    return r
                t_tro = p.t("tensor", em_tro, [t_o2, ld_id])
                t_xn = p.t("vector", lambda e, b=buf_i: e.scalar_tensor_tensor(
                    XN[:], pst2[:, 0:1024], C["BE2"][:, 0:1], b[:], ALU.mult, ALU.add),
                    [t_tro, t_s2])
                mm_done_prev[0] = t_xn
                if (not own_only) or (OSUB0 <= i < OSUB0 + NOSUB):
                    r0 = (i - OSUB0) if own_only else i
                    dst = x_dst[r0 * 128:(r0 + 1) * 128, :]
                    st = p.dma("sync", lambda e, d=dst: e.dma_start(out=d, in_=XN[:]),
                               [t_xn])
                    st_tasks.append(st)
                    deps_x[i % 2] = st
                else:
                    deps_x[i % 2] = t_xn
            barrier = p.tails()
        return st_tasks, barrier

    def proj_phase(entry_deps):
        with contextlib.ExitStack() as fs:
            PW = fs.enter_context(nc.sbuf_tensor("PW", [128, 8, D], BF16))
            XHT = fs.enter_context(nc.sbuf_tensor("XHT", [128, 8, 128], BF16))
            QF = fs.enter_context(nc.sbuf_tensor("QF", [128, D], BF16))
            XHB = fs.enter_context(nc.sbuf_tensor("XHB", [128, D], BF16))
            t_join = p.t("gpsimd", lambda e: e.engine_nop(), entry_deps)
            deps_x = [t_join, t_join]
            stat_prev = [None]
            tr_prev = [t_join]
            sp_tasks = []
            for j in range(NT):
                buf = X0 if j % 2 == 0 else X1
                ld = p.dma("sync", lambda e, b=buf, s=xA_d[j * 128:(j + 1) * 128, :]:
                           e.dma_start(out=b[:], in_=s), [deps_x[j % 2]])
                t_sq = p.t("scalar", lambda e, b=buf: e.activation(
                    TT_[:], b[:], AF.Square, accum_out=C["SUMQ"][:, 0:1]),
                    [ld, stat_prev[0]])

                def em_st1(e, b=buf):
                    e.tensor_reduce(C["SUMX"][:], b[:], AX.X, ALU.add)
                    e.tensor_scalar(C["MU"][:], C["SUMX"][:], 1.0 / D, None, ALU.mult)
                    e.tensor_scalar(C["TS0"][:], C["SUMQ"][:], 1.0 / D, None, ALU.mult)
                    e.tensor_tensor(C["TS1"][:], C["MU"][:], C["MU"][:], ALU.mult)
                    e.tensor_tensor(C["TS0"][:], C["TS0"][:], C["TS1"][:], ALU.subtract)
                    return e.tensor_scalar(C["TS0"][:], C["TS0"][:], 1e-5, None, ALU.add)
                t_st1 = p.t("vector", em_st1, [ld, t_sq])
                t_sqr = p.t("scalar", lambda e: e.activation(
                    C["TS2"][:], C["TS0"][:], AF.Sqrt), [t_st1])

                def em_st2(e):
                    e.reciprocal(C["RST"][:], C["TS2"][:])
                    for _ in range(2):
                        e.tensor_tensor(C["TS1"][:], C["RST"][:], C["RST"][:], ALU.mult)
                        e.tensor_tensor(C["TS1"][:], C["TS1"][:], C["TS0"][:], ALU.mult)
                        e.tensor_scalar(C["TS1"][:], C["TS1"][:], -0.5, 1.5,
                                        ALU.mult, ALU.add)
                        e.tensor_tensor(C["RST"][:], C["RST"][:], C["TS1"][:], ALU.mult)
                    e.tensor_tensor(C["BCC"][:], C["MU"][:], C["RST"][:], ALU.mult)
                    return e.tensor_scalar(C["BCC"][:], C["BCC"][:], -1.0, None, ALU.mult)
                t_st2 = p.t("vector", em_st2, [t_sqr])
                stat_prev[0] = t_st2
                t_ap = p.t("scalar", lambda e, b=buf: e.activation(
                    EE_[:], b[:], AF.Identity, bias=C["BCC"][:, 0:1],
                    scale=C["RST"][:, 0:1]), [t_st2])
                t_cb = p.t("vector", lambda e: e.tensor_copy(XHB[:], EE_[:]), [t_ap])
                st1 = p.dma("sync", lambda e, jj=j: e.dma_start(
                    out=xh2_d[jj * 128:(jj + 1) * 128, :], in_=EE_[:]), [t_cb])
                deps_x[j % 2] = st1

                def em_tr(e):
                    for jj in range(8):
                        r = e.transpose(pstb[:, jj * 128:(jj + 1) * 128],
                                        XHB[:, jj * 128:(jj + 1) * 128], IDB[:])
                    return r
                t_tr = p.t("tensor", em_tr, [t_cb, t_idb, tr_prev[0]])
                t_ev = p.t("vector", lambda e: e.tensor_copy(QF[:], pstb[:, 0:1024]),
                           [t_tr])
                tr_prev[0] = t_ev
                st2 = p.dma("sync", lambda e, jj=j: e.dma_start(
                    out=AP(xh2T_d.tensor, jj * 128, [[T_EXT, 128], [128 * T_EXT, 8],
                                                     [1, 128]]),
                    in_=AP(QF, 0, [[1024, 128], [128, 8], [1, 128]])), [t_ev])
                sp_tasks.append(st1)
                sp_tasks.append(st2)

            def zero_halo(dst_t, width):
                tk = []
                for q8 in range(8):
                    for side in (0, width - 32):
                        tk.append(p.dma("gpsimd", lambda e, q8=q8, s=side, d=dst_t:
                                        e.dma_start(out=AP(d.tensor, q8 * 128 * width + s,
                                                           [[width, 128], [1, 32]]),
                                                    in_=ZEROB[:]), [t_zerob]))
                return tk
            zh = []
            for n in ("q", "k", "kp", "v"):
                zh += zero_halo(spl[n], SPW)

            for n in ("q", "k", "kp", "v"):
                ldw = p.dma("sync", lambda e, n=n: e.dma_start(
                    out=AP(PW, 0, [[8 * D, 128], [D, 8], [1, D]]),
                    in_=AP(wT_d[n].tensor, 0, [[D, 128], [128 * D, 8], [1, D]])),
                    sp_tasks + entry_deps)
                prev_mm = [None]
                for j in range(NT):
                    ldx = p.dma("sync", lambda e, jj=j: e.dma_start(
                        out=AP(XHT, 0, [[8 * 128, 128], [128, 8], [1, 128]]),
                        in_=AP(xh2T_d.tensor, jj * 128, [[T_EXT, 128],
                                                         [128 * T_EXT, 8], [1, 128]])),
                        [sp_tasks[2 * j + 1], prev_mm[0]])

                    def em_mm(e):
                        for do8 in range(8):
                            for di in range(8):
                                r = e.matmul(pst2[:, do8 * 128:(do8 + 1) * 128],
                                             PW[:, di, do8 * 128:(do8 + 1) * 128],
                                             XHT[:, di, :],
                                             start=(di == 0), stop=(di == 7))
                        return r
                    mm = p.t("tensor", em_mm, [ldx, ldw, prev_mm[0]])
                    scl = 0.125 if n == "q" else 1.0
                    t_ev = p.t("vector", lambda e, s=scl: e.tensor_scalar(
                        QF[:], pst2[:, 0:1024], s, None, ALU.mult), [mm])
                    prev_mm[0] = t_ev
                    st = p.dma("sync", lambda e, n=n, jj=j: e.dma_start(
                        out=AP(spl[n].tensor, 32 + jj * 128,
                               [[SPW, 128], [128 * SPW, 8], [1, 128]]),
                        in_=AP(QF, 0, [[1024, 128], [128, 8], [1, 128]])), [t_ev])
                    sp_tasks.append(st)
            barrier = p.tails()
        return sp_tasks, barrier, zh

    def attn_phase(entry_deps):
        with contextlib.ExitStack() as fs:
            KB = fs.enter_context(nc.sbuf_tensor("KB", [128, ACH + 32], BF16))
            KPB = fs.enter_context(nc.sbuf_tensor("KPB", [128, ACH + 64], BF16))
            VB = fs.enter_context(nc.sbuf_tensor("VB", [128, ACH + 64], BF16))
            QBD = fs.enter_context(nc.sbuf_tensor("QBD", [128, 2 * ACH], BF16))
            KKB = fs.enter_context(nc.sbuf_tensor("KKB", [128, 31, ACH + 32], BF16))
            SC1 = fs.enter_context(nc.sbuf_tensor("SC1", [128, 992], F32))
            MSK0 = fs.enter_context(nc.sbuf_tensor("MSK0", [128, 992], F32))
            MSK1 = fs.enter_context(nc.sbuf_tensor("MSK1", [128, 992], F32))
            PSM = fs.enter_context(nc.sbuf_tensor("PSM", [128, 512], BF16))
            DBF = fs.enter_context(nc.sbuf_tensor("DBF", [128, 992], BF16)) \
                if cfg.get("dbg") else None
            EB = fs.enter_context(nc.sbuf_tensor("EB", [128, 992], BF16))
            PB_ = fs.enter_context(nc.sbuf_tensor("PB_", [128, 1504], BF16))
            PSR = fs.enter_context(nc.sbuf_tensor("PSR", [128, 512], BF16))
            PSTS = fs.enter_context(nc.sbuf_tensor("PSTS",
                                                   [128, ACH // 64, 4, 128], BF16))
            UF = fs.enter_context(nc.sbuf_tensor("UF", [128, 16, ACH + 32], BF16))
            UT = fs.enter_context(nc.sbuf_tensor("UT",
                                                 [128, (ACH + 32) // 8, 128], BF16))
            ZB = fs.enter_context(nc.sbuf_tensor("ZB", [128, 8, ACH], BF16))
            WOT = fs.enter_context(nc.sbuf_tensor("WOT", [128, 8, D], BF16))
            WOU = fs.enter_context(nc.sbuf_tensor("WOU", [128, 8, ACH], F32))
            XH2 = fs.enter_context(nc.sbuf_tensor("XH2", [128, D], F32))
            WS = fs.enter_context(nc.sbuf_tensor("WS", [128, D], F32))
            VV_ = fs.enter_context(nc.sbuf_tensor("VV_", [128, D], F32))
            NMX = fs.enter_context(nc.sbuf_tensor("NMX", [128, 1], F32))
            SUS = fs.enter_context(nc.sbuf_tensor("SUS", [128, 1], F32))
            RSS = fs.enter_context(nc.sbuf_tensor("RSS", [128, 1], F32))

            ld_m0 = p.dma("sync", lambda e: e.dma_start(out=MSK0[:], in_=mask_i[0]),
                          entry_deps)
            ld_m1 = p.dma("sync", lambda e: e.dma_start(out=MSK1[:], in_=mask_i[1]),
                          entry_deps)
            ld_pmf = p.dma("sync", lambda e: e.dma_start(out=SC1[:, 0:512],
                                                         in_=psmask_i[:, :]), entry_deps)
            t_pm = p.t("vector", lambda e: e.tensor_copy(PSM[:], SC1[:, 0:512]),
                       [ld_pmf])
            ld_wo = p.dma("sync", lambda e: e.dma_start(
                out=AP(WOT, 0, [[8 * D, 128], [D, 8], [1, D]]),
                in_=AP(wT_d["o"].tensor, 0, [[D, 128], [128 * D, 8], [1, D]])),
                entry_deps)
            t_qbd0 = mset("gpsimd", QBD[:], 0.0, entry_deps)
            t_pb0 = mset("gpsimd", PB_[:, 0:512], 0.0, entry_deps)

            st_tasks = []
            prev_chunk_tail = [None]
            t_zevac = [None, None]
            prev_kk = [None]
            prev_amm = [None]
            prev_utt = [None]
            prev_z = [None]
            for (c0, CL) in CHUNKS:
                NG = CL // 64
                NBLK = CL // 16
                NUTT = (CL + 32) // 8
                q_score_done = []
                t_zb = {}
                for q in range(8):
                    ld_k = p.dma("sync", lambda e, q=q, c0=c0, CL=CL: e.dma_start(
                        out=KB[:, 0:CL + 32],
                        in_=spl["k"][q][:, 16 + c0:16 + c0 + CL + 32]),
                        [prev_chunk_tail[0], prev_kk[0]])
                    ld_kp = p.dma("sync", lambda e, q=q, c0=c0, CL=CL: e.dma_start(
                        out=KPB[:, 0:CL + 64], in_=spl["kp"][q][:, c0:c0 + CL + 64]),
                        [prev_chunk_tail[0], prev_kk[0]])
                    ld_q1 = p.dma("sync", lambda e, q=q, c0=c0, CL=CL: e.dma_start(
                        out=AP(QBD, 0, [[2 * ACH, 64], [32, CL // 16], [1, 16]]),
                        in_=AP(spl["q"].tensor, q * 128 * SPW + 32 + c0,
                               [[SPW, 64], [16, CL // 16], [1, 16]])),
                        [t_qbd0, prev_chunk_tail[0], prev_amm[0]])
                    ld_q2 = p.dma("sync", lambda e, q=q, c0=c0, CL=CL: e.dma_start(
                        out=AP(QBD, 64 * 2 * ACH + 16,
                               [[2 * ACH, 64], [32, CL // 16], [1, 16]]),
                        in_=AP(spl["q"].tensor, q * 128 * SPW + 64 * SPW + 32 + c0,
                               [[SPW, 64], [16, CL // 16], [1, 16]])),
                        [t_qbd0, prev_chunk_tail[0], prev_amm[0]])

                    def em_kk(e, CL=CL):
                        return e.tensor_tensor(
                            KKB[:, :, 0:CL + 32].opt(),
                            AP(KB, 0, [[ACH + 32, 128], [0, 31], [1, CL + 32]]),
                            AP(KPB, 31, [[ACH + 64, 128], [-1, 31], [1, CL + 32]]),
                            ALU.mult)
                    t_kk = p.t("vector", em_kk, [ld_k, ld_kp, prev_amm[0]])
                    prev_kk[0] = t_kk

                    grp_tasks = []
                    prev_soft = [prev_z[0]]
                    for grp in range(NG):
                        def em_amm(e, grp=grp, q=q, CL=CL):
                            for b4 in range(4):
                                blk = grp * 4 + b4
                                lhs = AP(QBD, blk * 32, [[2 * ACH, 128], [1, 32]])
                                rhs1 = AP(KKB, 16 * blk,
                                          [[31 * (ACH + 32), 128], [ACH + 32, 16],
                                           [1, 32]])
                                rhs2 = AP(KKB, 16 * (ACH + 32) + 16 * blk,
                                          [[31 * (ACH + 32), 128], [ACH + 32, 15],
                                           [1, 32]])
                                e.matmul(psa[b4 * 32:b4 * 32 + 32, 0:512], lhs, rhs1,
                                         start=True, stop=True,
                                         tile_position=(0, b4 * 32),
                                         skip_group_check=True)
                                r = e.matmul(psa[b4 * 32:b4 * 32 + 32, 512:992], lhs,
                                             rhs2, start=True, stop=True,
                                             tile_position=(0, b4 * 32),
                                             skip_group_check=True)
                            return r
                        t_amm = p.t("tensor", em_amm,
                                    [t_kk, ld_q1, ld_q2, prev_soft[0]])
                        prev_amm[0] = t_amm
                        gidx = (c0 + grp * 64) // 64
                        msk = MSK0 if gidx == mg_first else MSK1

                        def em_soft1(e, msk=msk):
                            e.tensor_tensor(SC1[:], psa[:, 0:992], msk[:], ALU.add)
                            e.reduce_max(NMX[:, 0:1], SC1[:], AX.X)
                            return e.tensor_scalar(NMX[:], NMX[:], -1.0, None, ALU.mult)
                        t_sf1 = p.t("vector", em_soft1, [t_amm, ld_m0, ld_m1, t_pm])
                        t_exp = p.t("scalar", lambda e: e.activation(
                            EB[:], SC1[:], AF.Exp, bias=NMX[:, 0:1],
                            accum_out=SUS[:, 0:1]), [t_sf1])

                        def em_nrm(e):
                            e.reciprocal(RSS[:], SUS[:])
                            return e.tensor_scalar(PB_[:, 512:1504], EB[:],
                                                   RSS[:, 0:1], None, ALU.mult)
                        t_nrm = p.t("vector", em_nrm, [t_exp, t_pb0])

                        def em_fold(e):
                            e.tensor_copy(AP(PSR, 0, [[512, 128], [16, 32], [1, 1]]),
                                          AP(PB_, 512 + 480, [[1504, 128], [1, 32], [32, 1]]))
                            e.tensor_tensor(
                                AP(PSR, 1, [[512, 128], [16, 32], [1, 15]]),
                                AP(PB_, 512 + 512, [[1504, 128], [1, 32], [32, 15]]),
                                AP(PB_, 512 + 447, [[1504, 128], [1, 32], [-33, 15]]),
                                ALU.add)
                            return e.tensor_tensor(PSR[:], PSR[:], PSM[:], ALU.mult)
                        t_fold = p.t("vector", em_fold, [t_nrm, t_pm])

                        def em_pst(e):
                            for qh in range(4):
                                r = e.transpose(pstb[:, qh * 128:(qh + 1) * 128],
                                                PSR[:, qh * 128:(qh + 1) * 128],
                                                IDB[:])
                            return r
                        t_pst = p.t("tensor", em_pst, [t_fold, t_idb])
                        t_pse = p.t("vector", lambda e, grp=grp: e.tensor_copy(
                            AP(PSTS, grp * 4 * 128,
                               [[(ACH // 64) * 4 * 128, 128], [1, 512]]),
                            pstb[:, 0:512]), [t_pst])
                        prev_soft[0] = t_pse
                        grp_tasks.append(t_pse)
                        if cfg.get("dbg") and c0 == 0 and q == 0 and grp == 2:
                            dbf = p.t("vector", lambda e: e.tensor_copy(
                                DBF[:], SC1[:]), [t_sf1, t_exp])
                            p.dma("sync", lambda e: e.dma_start(
                                out=dbg_d[0][:, 1024:2016], in_=DBF[:]), [dbf])
                            p.dma("sync", lambda e: e.dma_start(
                                out=dbg_d[0][:, 0:992], in_=EB[:]), [t_exp, t_nrm])
                            p.dma("sync", lambda e: e.dma_start(
                                out=dbg_d[1][:, 0:1504], in_=PB_[:]),
                                [t_nrm, t_fold])
                            p.dma("sync", lambda e: e.dma_start(
                                out=dbg_d[1][:, 2048:2560], in_=PSM[:]), [t_pm])
                            p.dma("sync", lambda e: e.dma_start(
                                out=dbg_d[1][:, 3000:3512], in_=PSR[:]),
                                [t_fold, t_pst])

                    if cfg.get("dbg") and c0 == 0 and q == 0:
                        p.dma("sync", lambda e: e.dma_start(
                            out=dbg_d[2][:, 0:4096],
                            in_=AP(PSTS, 0, [[(ACH // 64) * 4 * 128, 128],
                                             [1, 4096]])), grp_tasks[-1:])
                        p.dma("sync", lambda e: e.dma_start(
                            out=dbg_d[5][:, 0:2 * ACH],
                            in_=AP(QBD, 0, [[2 * ACH, 128], [1, 2 * ACH]])),
                            [ld_q1, ld_q2])
                        p.dma("sync", lambda e: e.dma_start(
                            out=dbg_d[6][:, 0:544],
                            in_=AP(KKB, 15 * (ACH + 32),
                                   [[31 * (ACH + 32), 128], [1, 544]])), [t_kk])
                        p.dma("sync", lambda e: e.dma_start(
                            out=dbg_d[6][:, 1024:1024 + ACH + 32],
                            in_=KB[:, 0:ACH + 32]), [t_kk])
                        p.dma("sync", lambda e: e.dma_start(
                            out=dbg_d[6][:, 2048:2048 + ACH + 64],
                            in_=KPB[:, 0:ACH + 64]), [t_kk])

                    ld_v = p.dma("sync", lambda e, q=q, c0=c0, CL=CL: e.dma_start(
                        out=VB[:, 0:CL + 64], in_=spl["v"][q][:, c0:c0 + CL + 64]),
                        [prev_chunk_tail[0], prev_utt[0]])

                    def em_uf(e, CL=CL):
                        return e.tensor_tensor(
                            AP(UF, 0, [[16 * (ACH + 32), 128], [1, 16], [16, CL + 32]]),
                            AP(VB, 16, [[ACH + 64, 128], [0, 16], [1, CL + 32]]),
                            AP(VB, 16, [[ACH + 64, 128], [-1, 16], [1, CL + 32]]),
                            ALU.mult)
                    t_uf = p.t("vector", em_uf, [ld_v, prev_utt[0]])
                    ut_ev = []
                    for kt0 in range(0, NUTT, 8):
                        nk = min(8, NUTT - kt0)

                        def em_utt(e, kt0=kt0, nk=nk, CL=CL):
                            for kk_ in range(nk):
                                r = e.transpose(
                                    pstb[:, kk_ * 128:(kk_ + 1) * 128],
                                    AP(UF, (kt0 + kk_) * 128,
                                       [[16 * (ACH + 32), 128], [1, 128]]),
                                    IDB[:])
                            return r
                        t_utt = p.t("tensor", em_utt,
                                    [t_uf, t_idb, prev_z[0]] +
                                    ([ut_ev[-1]] if ut_ev else grp_tasks[-1:]))
                        t_ute = p.t("vector", lambda e, kt0=kt0, nk=nk: e.tensor_copy(
                            AP(UT, kt0 * 128,
                            [[((ACH + 32) // 8) * 128, 128], [1, nk * 128]]),
                            pstb[:, 0:nk * 128]), [t_utt])
                        ut_ev.append(t_ute)
                    prev_utt[0] = ut_ev[-1]
                    if cfg.get("dbg") and c0 == 0 and q == 0:
                        p.dma("sync", lambda e: e.dma_start(
                            out=dbg_d[3][:, 0:4096],
                            in_=AP(UT, 0, [[((ACH + 32) // 8) * 128, 128],
                                           [1, 4096]])), [ut_ev[-1]])
                        p.dma("sync", lambda e: e.dma_start(
                            out=dbg_d[4][:, 0:NUTT * 128 - 4096],
                            in_=AP(UT, 4096, [[((ACH + 32) // 8) * 128, 128],
                                              [1, NUTT * 128 - 4096]])),
                            [ut_ev[-1]])
                        p.dma("sync", lambda e: e.dma_start(
                            out=dbg_d[7][:, 1024:1024 + ACH + 64],
                            in_=VB[:, 0:ACH + 64]), [t_uf])

                    psz = psb if q % 2 == 0 else psc
                    def em_z(e, q=q, CL=CL, psz=psz, NBLK=NBLK):
                        for blk in range(NBLK):
                            grp, sub = blk // 4, blk % 4
                            for h in range(2):
                                for qh in range(4):
                                    r = e.matmul(
                                        psz[64 * h:64 * h + 64,
                                            16 * blk:16 * blk + 16],
                                        AP(UT, (2 * blk + qh) * 128 + 64 * h,
                                           [[((ACH + 32) // 8) * 128, 128],
                                            [1, 64]]),
                                        AP(PSTS, (grp * 4 + qh) * 128 + 32 * sub
                                           + 16 * h,
                                           [[(ACH // 64) * 4 * 128, 128],
                                            [1, 16]]),
                                        start=(blk == 0 and qh == 0),
                                        stop=(blk == NBLK - 1 and qh == 3),
                                        tile_position=(0, 64 * h),
                                        skip_group_check=True)
                        return r
                    t_z = p.t("tensor", em_z,
                              ut_ev + grp_tasks + [t_zevac[q % 2]])
                    if c0 == 0 and q == 0:
                        t_z.label = "t_z c0 q0"
                        for _i, _t in enumerate(ut_ev):
                            _t.label = f"ut_ev[{_i}] c0 q0" 
                    prev_z[0] = t_z
                    t_ze = p.t("vector", lambda e, q=q, psz=psz, CL=CL: e.tensor_copy(
                        ZB[:, q, 0:CL], psz[:, 0:CL]), [t_z])
                    t_zevac[q % 2] = t_ze
                    t_zb[q] = t_ze
                    q_score_done.append(t_ze)
                    if c0 == 0 and q == 0:
                        t_ze.label = "t_ze c0 q0"
                    if cfg.get("dbg") and c0 == 0 and q == 0:
                        p.dma("sync", lambda e, CL=CL: e.dma_start(
                            out=dbg_d[7][:, 0:CL], in_=ZB[:, 0, 0:CL]), [t_ze])

                wo_ev = []
                for pp in range(4):
                    def em_wo(e, pp=pp, CL=CL):
                        for dd in range(2):
                            d8 = pp * 2 + dd
                            for di in range(8):
                                r = e.matmul(pst2[:, dd * 512:dd * 512 + CL],
                                             WOT[:, di, d8 * 128:(d8 + 1) * 128],
                                             ZB[:, di, 0:CL],
                                             start=(di == 0), stop=(di == 7))
                        return r
                    mmw = p.t("tensor", em_wo,
                              list(t_zb.values()) + [ld_wo] + wo_ev[-1:])
                    t_we = p.t("vector", lambda e, pp=pp, CL=CL: (
                        e.tensor_copy(WOU[:, pp * 2, 0:CL], pst2[:, 0:CL]),
                        e.tensor_copy(WOU[:, pp * 2 + 1, 0:CL],
                                      pst2[:, 512:512 + CL]))[-1], [mmw])
                    wo_ev.append(t_we)

                prev_w = [None]
                prev_fin = [None, None]
                for si in range(CL // 128):
                    sub = (c0 // 128) + si
                    buf = X0 if sub % 2 == 0 else X1
                    ldx = p.dma("sync", lambda e, sub=sub, b=buf: e.dma_start(
                        out=b[:], in_=xA_d[sub * 128:(sub + 1) * 128, :]),
                        wo_ev + [prev_fin[sub % 2]])
                    ldh = p.dma("sync", lambda e, sub=sub: e.dma_start(
                        out=XH2[:], in_=xh2_d[sub * 128:(sub + 1) * 128, :]),
                        wo_ev + [prev_chunk_tail[0], prev_w[0]])

                    def em_trw(e, si=si, CL=CL):
                        for d8 in range(8):
                            r = e.transpose(pst2[:, d8 * 128:(d8 + 1) * 128],
                                            WOU[:, d8, si * 128:si * 128 + 128],
                                            IDF[:])
                        return r
                    t_trw = p.t("tensor", em_trw, wo_ev + [ld_id, prev_w[0]])
                    t_w = p.t("vector", lambda e: e.tensor_tensor(
                        WS[:], pst2[:, 0:1024], XH2[:], ALU.add), [t_trw, ldh])
                    prev_w[0] = t_w
                    t_sq = p.t("scalar", lambda e: e.activation(
                        TT_[:], WS[:], AF.Square, accum_out=C["SUMQ"][:, 0:1]), [t_w])

                    def em_st1(e):
                        e.tensor_reduce(C["SUMX"][:], WS[:], AX.X, ALU.add)
                        e.tensor_scalar(C["MU"][:], C["SUMX"][:], 1.0 / D, None,
                                        ALU.mult)
                        e.tensor_scalar(C["TS0"][:], C["SUMQ"][:], 1.0 / D, None,
                                        ALU.mult)
                        e.tensor_tensor(C["TS1"][:], C["MU"][:], C["MU"][:], ALU.mult)
                        e.tensor_tensor(C["TS0"][:], C["TS0"][:], C["TS1"][:],
                                        ALU.subtract)
                        return e.tensor_scalar(C["TS0"][:], C["TS0"][:], 1e-5, None,
                                               ALU.add)
                    t_st1 = p.t("vector", em_st1, [t_w, t_sq])
                    t_sqr = p.t("scalar", lambda e: e.activation(
                        C["TS2"][:], C["TS0"][:], AF.Sqrt), [t_st1])

                    def em_st2(e):
                        e.reciprocal(C["RST"][:], C["TS2"][:])
                        for _ in range(2):
                            e.tensor_tensor(C["TS1"][:], C["RST"][:], C["RST"][:],
                                            ALU.mult)
                            e.tensor_tensor(C["TS1"][:], C["TS1"][:], C["TS0"][:],
                                            ALU.mult)
                            e.tensor_scalar(C["TS1"][:], C["TS1"][:], -0.5, 1.5,
                                            ALU.mult, ALU.add)
                            e.tensor_tensor(C["RST"][:], C["RST"][:], C["TS1"][:],
                                            ALU.mult)
                        e.tensor_tensor(C["BCC"][:], C["MU"][:], C["RST"][:], ALU.mult)
                        return e.tensor_scalar(C["BCC"][:], C["BCC"][:], -1.0, None,
                                               ALU.mult)
                    t_st2 = p.t("vector", em_st2, [t_sqr])
                    t_v = p.t("scalar", lambda e: e.activation(
                        VV_[:], WS[:], AF.Identity, bias=C["BCC"][:, 0:1],
                        scale=C["RST"][:, 0:1]), [t_st2])

                    def em_fin(e, b=buf):
                        e.tensor_tensor(VV_[:], VV_[:], GB[:], ALU.mult)
                        e.tensor_tensor(VV_[:], VV_[:], BB[:], ALU.add)
                        return e.tensor_tensor(WS[:], b[:], VV_[:], ALU.add)
                    t_fin = p.t("vector", em_fin, [t_v, ld_gb, ld_bb, ldx])
                    st = p.dma("sync", lambda e, sub=sub: e.dma_start(
                        out=xB_d[sub * 128:(sub + 1) * 128, :], in_=WS[:]), [t_fin])
                    prev_fin[sub % 2] = st
                    st_tasks.append(st)
                prev_chunk_tail[0] = st_tasks[-1]
            barrier = p.tails()
        return st_tasks, barrier

    if dev_stop == "p0":
        bar = p.tails()
        d1 = p.dma("sync", lambda e: e.dma_start(
            out=AP(out.tensor, 0, [[D, 128], [1, 32]]), in_=CORR1[:]), bar)
        d2 = p.dma("sync", lambda e: e.dma_start(
            out=AP(out.tensor, 32, [[D, 128], [1, 8]]), in_=CORR2[:]), [d1])
        d3 = p.dma("sync", lambda e: e.dma_start(
            out=AP(out.tensor, 40, [[D, 128], [1, 1]]), in_=SW1[:],
            allow_slow_non_contiguous=True), [d2])
        d4 = p.dma("sync", lambda e: e.dma_start(
            out=AP(out.tensor, 41, [[D, 128], [1, 1]]), in_=ISW1[:],
            allow_slow_non_contiguous=True), [d3])
        d5 = p.dma("sync", lambda e: e.dma_start(
            out=AP(out.tensor, 42, [[D, 128], [1, 1]]), in_=SW2[:],
            allow_slow_non_contiguous=True), [d4])
        d6 = p.dma("sync", lambda e: e.dma_start(
            out=AP(out.tensor, 43, [[D, 128], [1, 1]]), in_=ISW2[:],
            allow_slow_non_contiguous=True), [d5])
        d7 = p.dma("sync", lambda e: e.dma_start(
            out=AP(out.tensor, 128 * D, [[D, 128], [1, D]]), in_=GB[:]), [d6])
        d8 = p.dma("sync", lambda e: e.dma_start(
            out=AP(out.tensor, 256 * D, [[D, 128], [1, D]]), in_=BB[:]), [d7])
        d9 = p.dma("sync", lambda e: e.dma_start(
            out=AP(out.tensor, 384 * D, [[D, 128], [1, 128]]),
            in_=AP(WCV, 0, [[128, 128], [1, 128]])), [d8])
        p.t("gpsimd", lambda e: e.engine_nop(), [d9])
        p.finalize()
        return nc
    def _dump_dram(src_ap, ncols, deps):
        th = src_ap.tensor
        rowlen = th.shape[2]
        DBG16x = nc.alloc_sbuf_tensor("DBG16x", [128, ncols], th.dtype)
        DBGFx = nc.alloc_sbuf_tensor("DBGFx", [128, ncols], F32)
        nsub = ncols // D
        prev = None
        for di in range(8):
            ld = p.dma("sync", lambda e, di=di: e.dma_start(
                out=DBG16x.ap()[:],
                in_=AP(th, di * 128 * rowlen, [[rowlen, 128], [1, ncols]])),
                deps + ([prev] if prev else []))
            cv = p.t("vector", lambda e: e.tensor_copy(DBGFx.ap()[:],
                                                       DBG16x.ap()[:]), [ld])
            prev = p.dma("sync", lambda e, di=di: e.dma_start(
                out=AP(out.tensor, di * 128 * nsub * D,
                       [[nsub * D, 128], [1, nsub * D]]),
                in_=DBGFx.ap()[:]), [cv])
        p.t("gpsimd", lambda e: e.engine_nop(), [prev])
        p.finalize()
        return nc

    def _dump_srcs():
        return dict(wq1T=wq1T_d, wq2T=wq2T_d, xh2T=xh2T_d,
                    **{f"w{n}T": wT_d[n] for n in wT_d},
                    **{f"spl{n}": spl[n] for n in spl})

    if dev_stop and dev_stop.startswith("dump0:"):
        nm, ncols = dev_stop.split(":")[1], int(dev_stop.split(":")[2])
        return _dump_dram(_dump_srcs()[nm], ncols, p.tails())
    if dev_stop in ("p0w", "p0v"):
        DBG16 = nc.alloc_sbuf_tensor("DBG16", [128, DH], FP16)
        DBGF = nc.alloc_sbuf_tensor("DBGF", [128, DH], F32)
        bar = p.tails()
        prev = None
        if dev_stop == "p0w":
            for di in range(8):
                ld = p.dma("sync", lambda e, di=di: e.dma_start(
                    out=DBG16[:], in_=wq1T_d[di]),
                    bar + ([prev] if prev else []))
                cv = p.t("vector", lambda e: e.tensor_copy(DBGF[:], DBG16[:]), [ld])
                prev = p.dma("sync", lambda e, di=di: e.dma_start(
                    out=AP(out.tensor, di * 512 * D,
                           [[4 * D, 128], [D, 4], [1, D]]),
                    in_=DBGF[:]), [cv])
        else:
            for s in range(32):
                ld = p.dma("sync", lambda e, s=s: e.dma_start(
                    out=DBG16[:, 0:D], in_=wq2T_d[s]),
                    bar + ([prev] if prev else []))
                cv = p.t("vector", lambda e: e.tensor_copy(DBGF[:, 0:D],
                                                           DBG16[:, 0:D]), [ld])
                prev = p.dma("sync", lambda e, s=s: e.dma_start(
                    out=AP(out.tensor, s * 128 * D, [[D, 128], [1, D]]),
                    in_=DBGF[:, 0:D]), [cv])
        p.t("gpsimd", lambda e: e.engine_nop(), [prev])
        p.finalize()
        return nc
    p1_st, bar1 = ffn_phase(x_in, xA_d, own_only=False, entry_deps=p0_done, tag="a")
    if dev_stop == "p1":
        fin = p.dma("sync", lambda e: e.dma_start(
            out=out[:, :], in_=xA_d[OWN0:OWN0 + OWN, :]), p1_st)
        p.t("gpsimd", lambda e: e.engine_nop(), [fin])
        p.finalize()
        return nc

    if dev_stop and dev_stop.startswith("dump1:"):
        nm, ncols = dev_stop.split(":")[1], int(dev_stop.split(":")[2])
        return _dump_dram(_dump_srcs()[nm], ncols, p.tails())
    p2_st, bar2, zh = proj_phase(bar1)
    if dev_stop and dev_stop.startswith("dump2:"):
        nm, ncols = dev_stop.split(":")[1], int(dev_stop.split(":")[2])
        return _dump_dram(_dump_srcs()[nm], ncols, p.tails())
    if dev_stop == "p2":
        fin = p.dma("sync", lambda e: e.dma_start(
            out=out[:, :], in_=xh2_d[OWN0:OWN0 + OWN, :]), p2_st)
        p.t("gpsimd", lambda e: e.engine_nop(), [fin])
        p.finalize()
        return nc

    p3_st, bar3 = attn_phase(bar2 + zh)
    if dev_stop and dev_stop.startswith("dumpA:"):
        nm, ncols = dev_stop.split(":")[1], int(dev_stop.split(":")[2])
        return _dump_dram(dbg_d, ncols, p.tails())
    if dev_stop == "p3":
        fin = p.dma("sync", lambda e: e.dma_start(
            out=out[:, :], in_=xB_d[OWN0:OWN0 + OWN, :]), p3_st)
        p.t("gpsimd", lambda e: e.engine_nop(), [fin])
        p.finalize()
        return nc

    p4_st, bar4 = ffn_phase(xB_d, out, own_only=True, entry_deps=bar3, tag="b")
    p.t("gpsimd", lambda e: e.engine_nop(), p4_st)
    p.finalize()
    return nc


T_EXT_FULL = 4352
OWN_FULL = 4096
OWN0_FULL = 128


def _build_masks():
    mask = np.zeros((2, 128, 992), np.float32)
    p = np.arange(128)
    blk4 = p // 32
    u = (p % 32) % 16
    gi = np.arange(31)
    mh = np.arange(32)
    g = gi - 15
    MH = mh[None, None, :]
    U = u[:, None, None]
    G = g[None, :, None]
    B4 = blk4[:, None, None]
    valid = ((U + 1 <= MH) & (MH <= U + 16) &
             (U + 1 <= MH - G) & (MH - G <= U + 16))
    kkey = 16 * B4 + MH - 16
    kpkey = kkey - G
    row0 = valid & (kkey >= 0) & (kpkey >= 0)
    mask[1] = np.where(valid, 0.0, -30000.0).reshape(128, 992)
    mask[0] = np.where(row0, 0.0, -30000.0).reshape(128, 992)
    psmask = np.zeros((128, 512), np.float32)
    mm2, gg2 = np.meshgrid(np.arange(32), np.arange(16), indexing="ij")
    psmask[:, :] = (mm2 >= gg2).astype(np.float32).reshape(1, 512)
    return mask, psmask


_NC_CACHE = {}


def _get_nc():
    key = "full"
    if key not in _NC_CACHE:
        cfg = dict(T_EXT=T_EXT_FULL, OWN0=OWN0_FULL, OWN=OWN_FULL,
                   maskgrp_first=OWN0_FULL // 64, dev_stop=None)
        _NC_CACHE[key] = (build(cfg), cfg)
    return _NC_CACHE[key]


def kernel(L0_attn_x, L0_attn_Wq, L0_attn_Wk, L0_attn_Wv, L0_attn_Wkp,
           L0_attn_Wo, L0_attn_gamma, L0_attn_beta, L0_ffn_w1, L0_ffn_conv_w,
           L0_ffn_conv_b, L0_ffn_w2):
    from concourse.bass_utils import run_bass_kernel_spmd
    x = np.ascontiguousarray(np.asarray(L0_attn_x, dtype=np.float32))
    B, S, Dm = x.shape
    assert (B, S, Dm) == (4, 8192, 1024)
    mask, psmask = _build_masks()
    mask_noseq = mask.copy()
    mask_noseq[0] = mask_noseq[1]
    ident = np.eye(128, dtype=np.float32)
    common = {
        "w_q": np.ascontiguousarray(L0_attn_Wq, dtype=np.float32),
        "w_k": np.ascontiguousarray(L0_attn_Wk, dtype=np.float32),
        "w_kp": np.ascontiguousarray(L0_attn_Wkp, dtype=np.float32),
        "w_v": np.ascontiguousarray(L0_attn_Wv, dtype=np.float32),
        "w_o": np.ascontiguousarray(L0_attn_Wo, dtype=np.float32),
        "gamma": np.ascontiguousarray(L0_attn_gamma, dtype=np.float32),
        "beta": np.ascontiguousarray(L0_attn_beta, dtype=np.float32),
        "w1": np.ascontiguousarray(L0_ffn_w1, dtype=np.float32),
        "conv_w": np.ascontiguousarray(
            np.asarray(L0_ffn_conv_w, dtype=np.float32).reshape(4096, 3)),
        "conv_b": np.ascontiguousarray(L0_ffn_conv_b, dtype=np.float32),
        "w2": np.ascontiguousarray(L0_ffn_w2, dtype=np.float32),
        "ident": ident,
        "psmask": psmask,
    }
    in_maps = []
    for core in range(8):
        b, half = core // 2, core % 2
        lo = half * 4096 - 128
        hi = lo + T_EXT_FULL
        xe = np.zeros((T_EXT_FULL, Dm), np.float32)
        s0, s1 = max(lo, 0), min(hi, S)
        xe[s0 - lo:s1 - lo] = x[b, s0:s1]
        m = dict(common)
        m["x_in"] = xe
        m["mask"] = mask if half == 0 else mask_noseq
        in_maps.append(m)
    nc, cfg = _get_nc()
    res = run_bass_kernel_spmd(nc, in_maps, core_ids=list(range(8)))
    outp = np.empty((4, 8192, 1024), np.float32)
    for core in range(8):
        b, half = core // 2, core % 2
        outp[b, half * 4096:(half + 1) * 4096] = np.asarray(
            res.results[core]["out"], dtype=np.float32)
    return outp


# version 08
# speedup vs baseline: 1.0243x; 1.0243x over previous
import contextlib
import numpy as np
import concourse.bass as bass
import concourse.mybir as mybir
from concourse.ap import AP

F32 = mybir.dt.float32
BF16 = mybir.dt.bfloat16
FP16 = mybir.dt.float16
I16 = mybir.dt.int16
AF = mybir.ActivationFunctionType
ALU = mybir.AluOpType
AX = mybir.AxisListType
ENGS = ("sync", "tensor", "vector", "scalar", "gpsimd")
D = 1024
DH = 4096


class Task:
    __slots__ = ("eng", "emit", "deps", "dma", "token", "do_inc", "label")

    def __init__(self, eng, emit, deps, dma):
        self.label = None
        self.eng = eng
        self.emit = emit
        self.deps = [d for d in deps if d is not None]
        self.dma = dma
        self.token = None
        self.do_inc = False


class _HazEng:
    _PASS = ("wait_ge", "dma_start", "engine_nop")

    def __init__(self, e, sem, state, dummy, has_memset, src):
        self._e = e
        self._sem = sem
        self._st = state
        self._dummy = dummy
        self._mset = has_memset
        self._src = src
        self.last_r = None
        self.last_short = False

    def __getattr__(self, name):
        real = getattr(self._e, name)
        if name in self._PASS or not callable(real):
            return real

        def wrap(*a, **k):
            r = real(*a, **k)
            short = True
            try:
                short = (a[0].free_size < 512) or (k.get("accum_out") is not None)
            except Exception:
                short = True
            if short:
                if self._mset:
                    self._e.tensor_reduce(self._dummy, self._src(),
                                          AX.X, ALU.add)
                else:
                    self._st[0] += 1
                    r.then_inc(self._sem, 1)
                    self._e.wait_ge(self._sem, self._st[0])
            self.last_r = r
            self.last_short = short
            return r
        return wrap


class Prog:
    def __init__(self, nc, serial=False):
        self.nc = nc
        self.per_eng = {e: [] for e in ENGS}
        self.esem = {e: nc.alloc_semaphore(f"se_{e}") for e in ENGS}
        self.dsem = {e: nc.alloc_semaphore(f"sd_{e}") for e in ENGS}
        self.hsem = {e: nc.alloc_semaphore(f"hz_{e}")
                     for e in ("vector", "scalar", "gpsimd")}
        self.dummy = nc.alloc_sbuf_tensor("hz_dummy", [128, 1],
                                          mybir.dt.float32).ap()
        self.spacer_src = None
        self.serial = serial
        self.all = []

    def _add(self, task):
        if self.serial and self.all:
            task.deps = task.deps + [self.all[-1]]
        self.all.append(task)
        self.per_eng[task.eng].append(task)
        return task

    def t(self, eng, emit, deps=()):
        return self._add(Task(eng, emit, deps, dma=False))

    def dma(self, eng, emit, deps=()):
        return self._add(Task(eng, emit, deps, dma=True))

    def tails(self):
        return [self.per_eng[e][-1] for e in ENGS if self.per_eng[e]]

    def finalize(self):
        needed = set()
        for e in ENGS:
            for task in self.per_eng[e]:
                for d in task.deps:
                    if d.dma or d.eng != task.eng:
                        needed.add(id(d))
        ccount = {e: 0 for e in ENGS}
        dcount = {e: 0 for e in ENGS}
        for e in ENGS:
            for task in self.per_eng[e]:
                if task.dma:
                    dcount[e] += 16
                    task.token = (self.dsem[e], dcount[e])
                    task.do_inc = True
                elif id(task) in needed:
                    ccount[e] += 1
                    task.token = (self.esem[e], ccount[e])
                    task.do_inc = True
        with self.nc.Block() as block:
            for ename in ENGS:
                tasks = self.per_eng[ename]
                if not tasks:
                    continue

                def body(eng, tasks=tasks, ename=ename):
                    maxw = {}
                    if ename in self.hsem:
                        heng = _HazEng(eng, self.hsem[ename], [0],
                                       self.dummy[0:128, 0:1],
                                       ename == "vector",
                                       lambda: self.spacer_src[0:128, 0:128])
                    else:
                        heng = eng
                    for task in tasks:
                        for d in task.deps:
                            if (not d.dma) and d.eng == ename:
                                continue
                            semh, val = d.token
                            key = id(semh)
                            if maxw.get(key, 0) >= val:
                                continue
                            eng.wait_ge(semh, val)
                            maxw[key] = val
                        last = task.emit(eng if task.dma else heng)
                        if task.do_inc and last is not None:
                            if (heng is not eng and heng.last_r is last
                                    and heng.last_short):
                                if ename == "gpsimd":
                                    last = eng.engine_nop()
                                elif ename == "scalar":
                                    last = eng.copy(self.dummy[0:1, 0:1],
                                                    self.dummy[0:1, 0:1])
                                else:
                                    last = eng.memset(self.dummy[0:1, 0:1], 0.0)
                            semh, _ = task.token
                            last.then_inc(semh, 16 if task.dma else 1)
                getattr(block, ename)(body)


def build(cfg):
    T_EXT = cfg["T_EXT"]
    NT = T_EXT // 128
    SPW = 32 + T_EXT + 32
    ACH = min(512, T_EXT)
    CHUNKS = []
    cc = 0
    while cc < T_EXT:
        CHUNKS.append((cc, min(ACH, T_EXT - cc)))
        cc += ACH
    OWN0 = cfg["OWN0"]
    OWN = cfg["OWN"]
    OSUB0 = OWN0 // 128
    NOSUB = OWN // 128
    mg_first = cfg["maskgrp_first"]
    dev_stop = cfg.get("dev_stop")

    nc = bass.Bass("TRN2", target_bir_lowering=False, detect_race_conditions=False)
    p = Prog(nc, serial=cfg.get("serial", False))
    stack = contextlib.ExitStack()

    x_in = nc.dram_tensor("x_in", [T_EXT, D], F32, kind="ExternalInput").ap()
    w_att = {n: nc.dram_tensor(f"w_{n}", [D, D], F32, kind="ExternalInput").ap()
             for n in ("q", "k", "kp", "v", "o")}
    gamma_i = nc.dram_tensor("gamma", [D], F32, kind="ExternalInput").ap()
    beta_i = nc.dram_tensor("beta", [D], F32, kind="ExternalInput").ap()
    w1_i = nc.dram_tensor("w1", [DH, D], F32, kind="ExternalInput").ap()
    cw_i = nc.dram_tensor("conv_w", [DH, 3], F32, kind="ExternalInput").ap()
    cb_i = nc.dram_tensor("conv_b", [DH], F32, kind="ExternalInput").ap()
    w2_i = nc.dram_tensor("w2", [D, DH], F32, kind="ExternalInput").ap()
    ident_i = nc.dram_tensor("ident", [128, 128], F32, kind="ExternalInput").ap()
    mask_i = nc.dram_tensor("mask", [2, 128, 992], F32, kind="ExternalInput").ap()
    psmask_i = nc.dram_tensor("psmask", [128, 512], F32, kind="ExternalInput").ap()
    out = nc.dram_tensor("out", [OWN, D], F32, kind="ExternalOutput").ap()

    dr = lambda n, s, dt: nc.dram_tensor(n, s, dt, kind="Internal").ap()
    wq1T_d = dr("wq1T_d", [8, 128, DH], FP16)
    wq2T_d = dr("wq2T_d", [32, 128, D], FP16)
    wT_d = {n: dr(f"w{n}T_d", [8, 128, D], BF16) for n in ("q", "k", "kp", "v", "o")}
    xA_d = dr("xA_d", [T_EXT, D], F32)
    xB_d = dr("xB_d", [T_EXT, D], F32)
    spl = {n: dr(f"{n}_d", [8, 128, SPW], BF16) for n in ("q", "k", "kp", "v")}
    xh2_d = dr("xh2_d", [T_EXT, D], F32)
    xh2T_d = dr("xh2T_d", [8, 128, T_EXT], BF16)
    dbg_d = dr("dbg_d", [8, 128, 4096], BF16)

    sb = lambda n, s, dt=F32: nc.alloc_sbuf_tensor(n, s, dt)
    IDF = sb("IDF", [128, 128], F32)
    IDB = sb("IDB", [128, 128], BF16)
    IDH = sb("IDH", [128, 128], FP16)
    X0 = sb("X0", [128, D])
    X1 = sb("X1", [128, D])
    TT_ = sb("TT_", [128, D])
    XQI = sb("XQI", [128, D], I16)
    EE_ = sb("EE_", [128, D])
    XQH = sb("XQH", [128, D], FP16)
    WTMP = sb("WTMP", [128, D], BF16)
    WTMO = sb("WTMO", [128, D], BF16)
    ROWT = sb("ROWT", [1, 132])
    GB = sb("GB", [128, D])
    p.spacer_src = GB
    BB = sb("BB", [128, D])
    WCV = sb("WCV", [128, 32, 4])
    CORR1 = sb("CORR1", [128, 32])
    CORR2 = sb("CORR2", [128, 8])
    SW1 = sb("SW1", [128, 1])
    ISW1 = sb("ISW1", [128, 1])
    SW2 = sb("SW2", [128, 1])
    ISW2 = sb("ISW2", [128, 1])
    ONESC = sb("ONESC", [128, 1], F32)
    ONESR = sb("ONESR", [1, 128], F32)
    ZEROB = sb("ZEROB", [128, 32], BF16)
    names = "SUMX SUMQ MU RST TS0 TS1 TS2 MXC MNC RSC BCC ALC BE2".split()
    C = {n: sb(n, [128, 1]) for n in names}

    PSA = nc.alloc_psum_tensor("PSA", [128, 1024], F32)
    PSB = nc.alloc_psum_tensor("PSB", [128, 512], F32)
    PSC = nc.alloc_psum_tensor("PSC", [128, 512], F32)
    PSD = nc.alloc_psum_tensor("PSD", [128, 512], F32)
    PST2 = nc.alloc_psum_tensor("PST2", [128, 1024], F32)
    PSTB = nc.alloc_psum_tensor("PSTB", [128, 1024], BF16)
    psa, psb, psc, psd = PSA.ap(), PSB.ap(), PSC.ap(), PSD.ap()
    pst2, pstb = PST2.ap(), PSTB.ap()
    psth = pstb.bitcast(FP16)
    psth3 = psth.rearrange("p (a b) -> p a b", b=128)

    def mset(eng, tile_ap, val, deps=()):
        return p.t(eng, lambda e, t=tile_ap, v=val: e.memset(t, v), deps)

    t_ones_c = mset("gpsimd", ONESC[:], 1.0)
    t_ones_r = mset("gpsimd", ONESR[:], 1.0)
    t_zerob = mset("gpsimd", ZEROB[:], 0.0)
    ld_id = p.dma("sync", lambda e: e.dma_start(out=IDF[:], in_=ident_i[:, :]))
    t_idb = p.t("vector", lambda e: e.tensor_copy(IDB[:], IDF[:]), [ld_id])
    t_idh = p.t("vector", lambda e: e.tensor_copy(IDH[:], IDF[:]), [ld_id])
    ld_gb = p.dma("sync", lambda e: e.dma_start(
        out=GB[:], in_=AP(gamma_i.tensor, 0, [[0, 128], [1, D]])))
    ld_bb = p.dma("sync", lambda e: e.dma_start(
        out=BB[:], in_=AP(beta_i.tensor, 0, [[0, 128], [1, D]])))
    ld_cw = p.dma("sync", lambda e: e.dma_start(
        out=AP(WCV, 0, [[128, 128], [4, 32], [1, 3]]),
        in_=AP(cw_i.tensor, 0, [[3, 128], [3 * 128, 32], [1, 3]])))
    ld_cb = p.dma("sync", lambda e: e.dma_start(
        out=AP(WCV, 3, [[128, 128], [4, 32], [1, 1]]),
        in_=AP(cb_i.tensor, 0, [[1, 128], [128, 32], [0, 1]]),
        allow_slow_non_contiguous=True))

    t_ai1 = mset("vector", C["SUMX"][:], 0.0)
    t_ai2 = mset("vector", C["SUMQ"][:], 0.0)
    prev_buf = [None, None]
    w2_flat = w2_i.rearrange("a (b c) -> (a b) c", c=D)
    red31 = red63 = None
    for i in range(64):
        buf = X0 if i % 2 == 0 else X1
        src = w1_i[i * 128:(i + 1) * 128, :] if i < 32 else \
            w2_flat[(i - 32) * 128:(i - 31) * 128, :]
        ld = p.dma("sync", lambda e, b=buf, s=src: e.dma_start(out=b[:], in_=s),
                   [prev_buf[i % 2]])
        acc = C["SUMX"] if i < 32 else C["SUMQ"]
        dep0 = t_ai1 if i < 32 else t_ai2

        def em(e, b=buf, a=acc):
            e.tensor_reduce(C["TS0"][:], b[:], AX.X, ALU.add, apply_absolute_value=True)
            return e.tensor_tensor(a[:], a[:], C["TS0"][:], ALU.add)
        tk = p.t("vector", em, [ld, dep0])
        prev_buf[i % 2] = tk
        if i == 31:
            red31 = tk
        if i == 63:
            red63 = tk

    def em_tot(e):
        e.matmul(psd[0:1, 0:1], ONESC[:, 0:1], C["SUMX"][:, 0:1], start=True, stop=True)
        return e.matmul(psd[0:1, 1:2], ONESC[:, 0:1], C["SUMQ"][:, 0:1],
                        start=True, stop=True, skip_group_check=True)
    t_tot = p.t("tensor", em_tot, [red31, red63, t_ones_c])
    t_mean = p.t("vector", lambda e: e.tensor_scalar(
        ROWT[0:1, 0:2], psd[0:1, 0:2], 1.0 / (DH * D), 1e-5, ALU.mult, ALU.max), [t_tot])
    t_bc = p.t("tensor", lambda e: e.matmul(
        psd[:, 2:4], ONESR[0:1, :], ROWT[0:1, 0:2], start=True, stop=True,
        skip_group_check=True), [t_mean, t_ones_r])

    def em_sw(e):
        e.tensor_copy(ISW1[:], psd[:, 2:3])
        e.tensor_copy(ISW2[:], psd[:, 3:4])
        e.reciprocal(SW1[:], ISW1[:])
        return e.reciprocal(SW2[:], ISW2[:])
    t_sw = p.t("vector", em_sw, [t_bc])

    with nc.sbuf_tensor("P0EV", [128, 1024], FP16) as P0EV:
        prev_buf = [t_sw, t_sw]
        t_ci1 = mset("vector", CORR1[:], 0.0, [t_sw])
        t_ci2 = mset("vector", CORR2[:], 0.0, [t_sw])
        p0_tail = []
        prev_ev = t_idh
        t_cs_last = None
        for i in range(64):
            buf = X0 if i % 2 == 0 else X1
            if i < 32:
                src = w1_i[i * 128:(i + 1) * 128, :]
                sw = SW1
            else:
                r2, cc2 = (i - 32) // 4, (i - 32) % 4
                src = w2_i[128 * r2:128 * (r2 + 1), 1024 * cc2:1024 * (cc2 + 1)]
                sw = SW2
            ld = p.dma("sync", lambda e, b=buf, s=src: e.dma_start(out=b[:], in_=s),
                       [prev_buf[i % 2]])
            t_t = p.t("scalar", lambda e, b=buf, s=sw: e.mul(TT_[:], b[:], s[:, 0:1]),
                      [ld, t_sw])
            prev_buf[i % 2] = t_t

            def em_q(e):
                e.tensor_scalar(EE_[:], TT_[:], 128.0, None, ALU.add)
                e.tensor_copy(XQI[:], EE_[:])
                e.tensor_tensor(EE_[:], EE_[:], XQI[:], ALU.subtract)
                e.scalar_tensor_tensor(TT_[:], EE_[:], 0.5, XQI[:], ALU.is_ge, ALU.add)
                e.tensor_scalar(TT_[:], TT_[:], 129.0, 127.0, ALU.min, ALU.max)
                return e.tensor_scalar(XQH[:], TT_[:], 128.0, None, ALU.subtract)
            t_q = p.t("vector", em_q, [t_t])
            if i < 32:
                def em_cs(e, col=i):
                    e.tensor_reduce(C["TS0"][:], XQH[:], AX.X, ALU.add)
                    return e.tensor_tensor(CORR1[:, col:col + 1], CORR1[:, col:col + 1],
                                           C["TS0"][:], ALU.add)
                t_cs = p.t("vector", em_cs, [t_q, t_ci1])
            else:
                r2 = (i - 32) // 4

                def em_cs(e, col=r2):
                    e.tensor_reduce(C["TS0"][:], XQH[:], AX.X, ALU.add)
                    return e.tensor_tensor(CORR2[:, col:col + 1], CORR2[:, col:col + 1],
                                           C["TS0"][:], ALU.add)
                t_cs = p.t("vector", em_cs, [t_q, t_ci2])
            t_cs_last = t_cs

            def em_tr(e):
                for j in range(8):
                    r = e.transpose(psth[:, j * 128:(j + 1) * 128],
                                    XQH[:, j * 128:(j + 1) * 128], IDH[:])
                return r
            t_tr = p.t("tensor", em_tr, [t_q, t_idh, prev_ev])
            t_ev = p.t("vector", lambda e: e.tensor_copy(P0EV[:], psth[:, 0:1024]),
                       [t_tr])
            prev_ev = t_ev
            if i < 32:
                sp = lambda e, ii=i: e.dma_start(
                    out=AP(wq1T_d.tensor, ii * 128, [[DH, 128], [128 * DH, 8], [1, 128]]),
                    in_=AP(P0EV, 0, [[1024, 128], [128, 8], [1, 128]]))
            else:
                r2, cc2 = (i - 32) // 4, (i - 32) % 4
                sp = lambda e, r2=r2, cc2=cc2: e.dma_start(
                    out=AP(wq2T_d.tensor, (cc2 * 8) * 128 * D + r2 * 128,
                           [[D, 128], [128 * D, 8], [1, 128]]),
                    in_=AP(P0EV, 0, [[1024, 128], [128, 8], [1, 128]]))
            t_sp = p.dma("sync", sp, [t_ev])
            p0_tail.append(t_sp)

        t_corr = p.t("vector", lambda e: (
            e.tensor_scalar(CORR1[:], CORR1[:], 128.0, None, ALU.mult),
            e.tensor_scalar(CORR2[:], CORR2[:], 128.0, None, ALU.mult))[-1],
            [t_cs_last])

        for name in ("q", "k", "kp", "v", "o"):
            w_src = w_att[name]
            for i in range(8):
                buf = X0 if i % 2 == 0 else X1
                ld = p.dma("sync", lambda e, b=buf, s=w_src[i * 128:(i + 1) * 128, :]:
                           e.dma_start(out=b[:], in_=s), [prev_buf[i % 2]])
                t_cast = p.t("vector", lambda e, b=buf: e.tensor_copy(WTMP[:], b[:]),
                             [ld, prev_ev])
                prev_buf[i % 2] = t_cast

                def em_tr(e):
                    for j in range(8):
                        r = e.transpose(pstb[:, j * 128:(j + 1) * 128],
                                        WTMP[:, j * 128:(j + 1) * 128], IDB[:])
                    return r
                t_tr = p.t("tensor", em_tr, [t_cast, t_idb])
                t_ev2 = p.t("vector", lambda e: e.tensor_copy(WTMO[:], pstb[:, 0:1024]),
                            [t_tr])
                prev_ev = t_ev2
                t_sp = p.dma("sync", lambda e, n=name, ii=i: e.dma_start(
                    out=AP(wT_d[n].tensor, ii * 128, [[D, 128], [128 * D, 8], [1, 128]]),
                    in_=AP(WTMO, 0, [[1024, 128], [128, 8], [1, 128]])), [t_ev2])
                p0_tail.append(t_sp)
        p0_done = p0_tail + [t_corr]

    def ffn_phase(x_src, x_dst, own_only, entry_deps, tag=""):
        with contextlib.ExitStack() as fs:
            WQ1T = fs.enter_context(nc.sbuf_tensor("WQ1T" + tag, [128, 8, DH], FP16))
            WQ2T = fs.enter_context(nc.sbuf_tensor("WQ2T" + tag, [128, 32, D], FP16))
            XQT = fs.enter_context(nc.sbuf_tensor("XQT" + tag, [128, 8, 3, 130], FP16))
            YS = fs.enter_context(nc.sbuf_tensor("YS" + tag, [128, 32, 128], F32))
            XQ2 = fs.enter_context(nc.sbuf_tensor("XQ2" + tag, [128, 32, 128], FP16))
            OUT2 = fs.enter_context(nc.sbuf_tensor("OUT2" + tag, [128, D], F32))
            XN = fs.enter_context(nc.sbuf_tensor("XN" + tag, [128, D], F32))
            ALPH = fs.enter_context(nc.sbuf_tensor("ALPH" + tag, [128, 130], F32))
            S2B = fs.enter_context(nc.sbuf_tensor("S2B" + tag, [128, 128], F32))
            H_ = fs.enter_context(nc.sbuf_tensor("H_" + tag, [128, 130], F32))
            TC = fs.enter_context(nc.sbuf_tensor("TC" + tag, [128, 128], F32))
            T2 = fs.enter_context(nc.sbuf_tensor("T2" + tag, [128, 128], F32))
            YY = fs.enter_context(nc.sbuf_tensor("YY" + tag, [128, 128], F32))
            M128 = fs.enter_context(nc.sbuf_tensor("M128" + tag, [128, 128], F32))
            A2 = fs.enter_context(nc.sbuf_tensor("A2" + tag, [128, 128], F32))
            I2 = fs.enter_context(nc.sbuf_tensor("I2" + tag, [128, 128], I16))
            E2 = fs.enter_context(nc.sbuf_tensor("E2" + tag, [128, 128], F32))
            ARW = fs.enter_context(nc.sbuf_tensor("ARW" + tag, [1, 3 * 130], F32))

            ld_wq1 = p.dma("sync", lambda e: e.dma_start(
                out=AP(WQ1T, 0, [[8 * DH, 128], [DH, 8], [1, DH]]),
                in_=AP(wq1T_d.tensor, 0, [[DH, 128], [128 * DH, 8], [1, DH]])),
                entry_deps)
            ld_wq2 = p.dma("sync", lambda e: e.dma_start(
                out=AP(WQ2T, 0, [[32 * D, 128], [D, 32], [1, D]]),
                in_=AP(wq2T_d.tensor, 0, [[D, 128], [128 * D, 32], [1, D]])),
                entry_deps)
            t_xqt0 = mset("gpsimd", XQT[:], 0.0, entry_deps)
            t_arw = mset("gpsimd", ARW[:], 1.0, entry_deps)

            KSLOTS = [(psa, 0), (psa, 512), (psb, 0)]
            t_join = p.t("gpsimd", lambda e: e.engine_nop(), entry_deps)
            deps_x = [t_join, t_join]
            ring_task = [t_xqt0, t_xqt0, t_xqt0]
            tr_prev = [None]
            stat_prev = [None]
            prev_rows = [None]
            prev_tra = [None]
            mm_done_prev = [None]
            last_bcs_read = [None]
            st_tasks = []
            tasks_q = {}
            for j in range(NT + 1):
                if j < NT:
                    buf = X0 if j % 2 == 0 else X1
                    ld = p.dma("sync", lambda e, b=buf,
                               s=x_src[j * 128:(j + 1) * 128, :]:
                               e.dma_start(out=b[:], in_=s), [deps_x[j % 2]])
                    t_sq = p.t("scalar", lambda e, b=buf: e.activation(
                        TT_[:], b[:], AF.Square, accum_out=C["SUMQ"][:, 0:1]),
                        [ld, stat_prev[0]])

                    def em_st1(e, b=buf):
                        e.tensor_reduce(C["SUMX"][:], b[:], AX.X, ALU.add)
                        e.tensor_reduce(C["MXC"][:], b[:], AX.X, ALU.max)
                        e.tensor_reduce(C["MNC"][:], b[:], AX.X, ALU.min)
                        e.tensor_scalar(C["MU"][:], C["SUMX"][:], 1.0 / D, None, ALU.mult)
                        e.tensor_scalar(C["TS0"][:], C["SUMQ"][:], 1.0 / D, None, ALU.mult)
                        e.tensor_tensor(C["TS1"][:], C["MU"][:], C["MU"][:], ALU.mult)
                        e.tensor_tensor(C["TS0"][:], C["TS0"][:], C["TS1"][:], ALU.subtract)
                        return e.tensor_scalar(C["TS0"][:], C["TS0"][:], 1e-5, None, ALU.add)
                    t_st1 = p.t("vector", em_st1, [ld, t_sq])
                    t_sqr = p.t("scalar", lambda e: e.activation(
                        C["TS2"][:], C["TS0"][:], AF.Sqrt), [t_st1])

                    def em_st2(e):
                        e.reciprocal(C["RST"][:], C["TS2"][:])
                        for _ in range(2):
                            e.tensor_tensor(C["TS1"][:], C["RST"][:], C["RST"][:], ALU.mult)
                            e.tensor_tensor(C["TS1"][:], C["TS1"][:], C["TS0"][:], ALU.mult)
                            e.tensor_scalar(C["TS1"][:], C["TS1"][:], -0.5, 1.5,
                                            ALU.mult, ALU.add)
                            e.tensor_tensor(C["RST"][:], C["RST"][:], C["TS1"][:], ALU.mult)
                        e.tensor_tensor(C["TS1"][:], C["MXC"][:], C["MU"][:], ALU.subtract)
                        e.tensor_tensor(C["TS2"][:], C["MU"][:], C["MNC"][:], ALU.subtract)
                        e.tensor_tensor(C["TS1"][:], C["TS1"][:], C["TS2"][:], ALU.max)
                        e.tensor_tensor(C["TS1"][:], C["TS1"][:], C["RST"][:], ALU.mult)
                        e.tensor_scalar(C["TS1"][:], C["TS1"][:], 1e-5, None, ALU.max)
                        e.tensor_scalar(C["ALC"][:], C["TS1"][:], 1.0 / 127.0, None, ALU.mult)
                        e.tensor_tensor(C["ALC"][:], C["ALC"][:], ISW1[:], ALU.mult)
                        e.reciprocal(C["TS2"][:], C["TS1"][:])
                        e.tensor_scalar(C["TS2"][:], C["TS2"][:], 127.0, None, ALU.mult)
                        e.tensor_tensor(C["RSC"][:], C["RST"][:], C["TS2"][:], ALU.mult)
                        e.tensor_tensor(C["BCC"][:], C["MU"][:], C["RSC"][:], ALU.mult)
                        return e.tensor_scalar(C["BCC"][:], C["BCC"][:], -1.0, 128.0,
                                               ALU.mult, ALU.add)
                    t_st2 = p.t("vector", em_st2, [t_sqr, t_sw])
                    t_tq = p.t("scalar", lambda e, b=buf: e.activation(
                        TT_[:], b[:], AF.Identity, bias=C["BCC"][:, 0:1],
                        scale=C["RSC"][:, 0:1]), [t_st2])

                    def em_rnd(e):
                        e.tensor_copy(XQI[:], TT_[:])
                        e.tensor_tensor(EE_[:], TT_[:], XQI[:], ALU.subtract)
                        return e.scalar_tensor_tensor(XQH[:], EE_[:], 0.5, XQI[:],
                                                      ALU.is_ge, ALU.add)
                    t_rnd = p.t("vector", em_rnd, [t_tq])
                    stat_prev[0] = t_rnd

                    def em_trq(e):
                        for jj in range(8):
                            r = e.transpose(psth[:, jj * 128:(jj + 1) * 128],
                                            XQH[:, jj * 128:(jj + 1) * 128], IDH[:])
                        return r
                    t_trq = p.t("tensor", em_trq, [t_rnd, t_idh, tr_prev[0]])

                    def em_evq(e, slot=j % 3, sl=(j - 1) % 3, sr=(j + 1) % 3):
                        e.tensor_copy(AP(XQT, slot * 130 + 1,
                                         [[8 * 3 * 130, 128], [3 * 130, 8], [1, 128]]),
                                      psth3)
                        e.tensor_copy(AP(XQT, sl * 130 + 129,
                                         [[8 * 3 * 130, 128], [3 * 130, 8], [1, 1]]),
                                      psth3[:, :, 0:1])
                        return e.tensor_copy(
                            AP(XQT, sr * 130 + 0,
                               [[8 * 3 * 130, 128], [3 * 130, 8], [1, 1]]),
                            psth3[:, :, 127:128])
                    t_evq = p.t("vector", em_evq, [t_trq, t_xqt0])
                    ring_task[j % 3] = t_evq
                    tr_prev[0] = t_evq
                    t_tra = p.t("tensor", lambda e: e.transpose(
                        psc[0:1, 0:128], C["ALC"][:, 0:1], IDF[:]),
                        [t_st2, ld_id, prev_rows[0]])
                    prev_tra[0] = t_tra

                    def em_eva(e, slot=j % 3, sl=(j - 1) % 3, sr=(j + 1) % 3):
                        e.tensor_copy(ARW[0:1, slot * 130 + 1:slot * 130 + 129],
                                      psc[0:1, 0:128])
                        e.tensor_copy(ARW[0:1, sl * 130 + 129:sl * 130 + 130],
                                      psc[0:1, 0:1])
                        return e.tensor_copy(ARW[0:1, sr * 130:sr * 130 + 1],
                                             psc[0:1, 127:128])
                    t_eva = p.t("vector", em_eva, [t_tra, t_arw])
                    tasks_q[j] = (t_evq, t_eva, ld, buf)

                i = j - 1
                if i < 0 or i >= NT:
                    continue
                buf_i = tasks_q[i][3]
                dep_ring = [ring_task[0], ring_task[1], ring_task[2]]
                deps_bca = [tasks_q[i][1]]
                if i + 1 in tasks_q:
                    deps_bca.append(tasks_q[i + 1][1])
                t_bca = p.t("tensor", lambda e, slot=i % 3: e.matmul(
                    psd[:, 0:130], ONESR[0:1, :], ARW[0:1, slot * 130:slot * 130 + 130],
                    start=True, stop=True, skip_group_check=True),
                    deps_bca + [last_bcs_read[0]])
                t_alph = p.t("vector", lambda e: e.tensor_copy(ALPH[:], psd[:, 0:130]),
                             [t_bca])

                t_ys_last = None
                t_h = None
                t_tp = None
                for cg in range(0, 32, 3):
                    ncg = min(3, 32 - cg)

                    def em_mm1(e, cg=cg, ncg=ncg, ii=i):
                        for ci in range(ncg):
                            c = cg + ci
                            pk, off = KSLOTS[ci]
                            for d8 in range(8):
                                r = e.matmul(
                                    pk[:, off:off + 130],
                                    AP(WQ1T, d8 * DH + c * 128,
                                       [[8 * DH, 128], [1, 128]]),
                                    AP(XQT, d8 * 3 * 130 + (ii % 3) * 130,
                                       [[8 * 3 * 130, 128], [1, 130]]),
                                    start=(d8 == 0), stop=(d8 == 7))
                        return r
                    mm1 = p.t("tensor", em_mm1, dep_ring + [ld_wq1, mm_done_prev[0]])
                    for ci in range(ncg):
                        c = cg + ci
                        pk, off = KSLOTS[ci]
                        t_h = p.t("vector", lambda e, pk=pk, off=off, c=c:
                                  e.scalar_tensor_tensor(H_[:], pk[:, off:off + 130],
                                                         CORR1[:, c:c + 1], ALPH[:],
                                                         ALU.subtract, ALU.mult),
                                  [mm1, t_alph, t_corr, t_tp])
                        t_tc = p.t("scalar", lambda e, c=c: e.activation(
                            TC[:], H_[:, 1:129], AF.Identity, bias=WCV[:, c, 3:4],
                            scale=WCV[:, c, 1:2]), [t_h, ld_cw, ld_cb])

                        def em_taps(e, c=c):
                            e.scalar_tensor_tensor(T2[:], H_[:, 0:128], WCV[:, c, 0:1],
                                                   TC[:], ALU.mult, ALU.add)
                            return e.scalar_tensor_tensor(YY[:], H_[:, 2:130],
                                                          WCV[:, c, 2:3], T2[:],
                                                          ALU.mult, ALU.add)
                        t_tp = p.t("vector", em_taps, [t_tc])
                        t_ys = p.t("scalar", lambda e, c=c: e.activation(
                            YS[:, c, :], YY[:], AF.Silu), [t_tp])
                        t_ys_last = t_ys
                    mm_done_prev[0] = t_h

                t_m = p.t("vector", lambda e: e.tensor_reduce(
                    M128[:], AP(YS, 0, [[32 * 128, 128], [1, 128], [128, 32]]),
                    AX.X, ALU.max, apply_absolute_value=True), [t_ys_last])
                t_trm = p.t("tensor", lambda e: e.transpose(
                    pst2[:, 0:128], M128[:], IDF[:]), [t_m, ld_id, mm_done_prev[0]])

                def em_s2(e):
                    e.tensor_reduce(C["TS0"][:], pst2[:, 0:128], AX.X, ALU.max)
                    e.tensor_scalar(C["TS0"][:], C["TS0"][:], 1e-5, None, ALU.max)
                    e.reciprocal(C["TS1"][:], C["TS0"][:])
                    e.tensor_scalar(C["TS1"][:], C["TS1"][:], 127.0, None, ALU.mult)
                    e.tensor_scalar(C["BE2"][:], C["TS0"][:], 0.5 / 127.0, None, ALU.mult)
                    return e.tensor_tensor(C["BE2"][:], C["BE2"][:], ISW2[:], ALU.mult)
                t_s2 = p.t("vector", em_s2, [t_trm, t_sw])
                t_trs = p.t("tensor", lambda e: e.transpose(
                    psc[0:1, 128:256], C["TS1"][:, 0:1], IDF[:]),
                    [t_s2, ld_id, prev_tra[0]])
                t_rows = p.t("vector", lambda e: e.tensor_copy(
                    ROWT[0:1, 0:128], psc[0:1, 128:256]), [t_trs])
                prev_rows[0] = t_rows
                t_bcs = p.t("tensor", lambda e: e.matmul(
                    psd[:, 256:384], ONESR[0:1, :], ROWT[0:1, 0:128],
                    start=True, stop=True, skip_group_check=True),
                    [t_rows, t_alph])
                t_s2b = p.t("vector", lambda e: e.tensor_copy(S2B[:], psd[:, 256:384]),
                            [t_bcs])
                last_bcs_read[0] = t_s2b

                t_q2_last = t_s2b
                for c in range(32):
                    def em_q2(e, c=c):
                        e.tensor_tensor(A2[:], YS[:, c, :], S2B[:], ALU.mult)
                        e.tensor_scalar(I2[:], A2[:], 128.0, None, ALU.add)
                        e.scalar_tensor_tensor(E2[:], A2[:], 128.0, I2[:],
                                               ALU.add, ALU.subtract)
                        return e.scalar_tensor_tensor(XQ2[:, c, :], E2[:], 0.5, I2[:],
                                                      ALU.is_ge, ALU.add)
                    t_q2_last = p.t("vector", em_q2, [t_q2_last])

                def em_mm2(e):
                    for bank in range(2):
                        first = True
                        for d8 in range(bank * 4, bank * 4 + 4):
                            for c in range(32):
                                r = e.matmul(pst2[:, d8 * 128:(d8 + 1) * 128],
                                             AP(WQ2T, c * D + d8 * 128,
                                                [[32 * D, 128], [1, 128]]),
                                             XQ2[:, c, :],
                                             start=first,
                                             stop=(d8 == bank * 4 + 3 and c == 31),
                                             skip_group_check=True)
                                first = False
                    return r
                mm2 = p.t("tensor", em_mm2, [t_q2_last, ld_wq2, t_trm])

                def em_o2(e):
                    for d8 in range(8):
                        r = e.tensor_scalar(OUT2[:, d8 * 128:(d8 + 1) * 128],
                                            pst2[:, d8 * 128:(d8 + 1) * 128],
                                            CORR2[:, d8:d8 + 1], None, ALU.subtract)
                    return r
                t_o2 = p.t("vector", em_o2, [mm2, t_corr])

                def em_tro(e):
                    for jj in range(8):
                        r = e.transpose(pst2[:, jj * 128:(jj + 1) * 128],
                                        OUT2[:, jj * 128:(jj + 1) * 128], IDF[:])
                    return r
                t_tro = p.t("tensor", em_tro, [t_o2, ld_id])
                t_xn = p.t("vector", lambda e, b=buf_i: e.scalar_tensor_tensor(
                    XN[:], pst2[:, 0:1024], C["BE2"][:, 0:1], b[:], ALU.mult, ALU.add),
                    [t_tro, t_s2])
                mm_done_prev[0] = t_xn
                if (not own_only) or (OSUB0 <= i < OSUB0 + NOSUB):
                    r0 = (i - OSUB0) if own_only else i
                    dst = x_dst[r0 * 128:(r0 + 1) * 128, :]
                    st = p.dma("sync", lambda e, d=dst: e.dma_start(out=d, in_=XN[:]),
                               [t_xn])
                    st_tasks.append(st)
                    deps_x[i % 2] = st
                else:
                    deps_x[i % 2] = t_xn
            barrier = p.tails()
        return st_tasks, barrier

    def proj_phase(entry_deps):
        with contextlib.ExitStack() as fs:
            PW = fs.enter_context(nc.sbuf_tensor("PW", [128, 8, D], BF16))
            XHT = fs.enter_context(nc.sbuf_tensor("XHT", [128, 8, 128], BF16))
            QF = fs.enter_context(nc.sbuf_tensor("QF", [128, D], BF16))
            XHB = fs.enter_context(nc.sbuf_tensor("XHB", [128, D], BF16))
            t_join = p.t("gpsimd", lambda e: e.engine_nop(), entry_deps)
            deps_x = [t_join, t_join]
            stat_prev = [None]
            tr_prev = [t_join]
            sp_tasks = []
            for j in range(NT):
                buf = X0 if j % 2 == 0 else X1
                ld = p.dma("sync", lambda e, b=buf, s=xA_d[j * 128:(j + 1) * 128, :]:
                           e.dma_start(out=b[:], in_=s), [deps_x[j % 2]])
                t_sq = p.t("scalar", lambda e, b=buf: e.activation(
                    TT_[:], b[:], AF.Square, accum_out=C["SUMQ"][:, 0:1]),
                    [ld, stat_prev[0]])

                def em_st1(e, b=buf):
                    e.tensor_reduce(C["SUMX"][:], b[:], AX.X, ALU.add)
                    e.tensor_scalar(C["MU"][:], C["SUMX"][:], 1.0 / D, None, ALU.mult)
                    e.tensor_scalar(C["TS0"][:], C["SUMQ"][:], 1.0 / D, None, ALU.mult)
                    e.tensor_tensor(C["TS1"][:], C["MU"][:], C["MU"][:], ALU.mult)
                    e.tensor_tensor(C["TS0"][:], C["TS0"][:], C["TS1"][:], ALU.subtract)
                    return e.tensor_scalar(C["TS0"][:], C["TS0"][:], 1e-5, None, ALU.add)
                t_st1 = p.t("vector", em_st1, [ld, t_sq])
                t_sqr = p.t("scalar", lambda e: e.activation(
                    C["TS2"][:], C["TS0"][:], AF.Sqrt), [t_st1])

                def em_st2(e):
                    e.reciprocal(C["RST"][:], C["TS2"][:])
                    for _ in range(2):
                        e.tensor_tensor(C["TS1"][:], C["RST"][:], C["RST"][:], ALU.mult)
                        e.tensor_tensor(C["TS1"][:], C["TS1"][:], C["TS0"][:], ALU.mult)
                        e.tensor_scalar(C["TS1"][:], C["TS1"][:], -0.5, 1.5,
                                        ALU.mult, ALU.add)
                        e.tensor_tensor(C["RST"][:], C["RST"][:], C["TS1"][:], ALU.mult)
                    e.tensor_tensor(C["BCC"][:], C["MU"][:], C["RST"][:], ALU.mult)
                    return e.tensor_scalar(C["BCC"][:], C["BCC"][:], -1.0, None, ALU.mult)
                t_st2 = p.t("vector", em_st2, [t_sqr])
                stat_prev[0] = t_st2
                t_ap = p.t("scalar", lambda e, b=buf: e.activation(
                    EE_[:], b[:], AF.Identity, bias=C["BCC"][:, 0:1],
                    scale=C["RST"][:, 0:1]), [t_st2])
                t_cb = p.t("vector", lambda e: e.tensor_copy(XHB[:], EE_[:]), [t_ap])
                st1 = p.dma("sync", lambda e, jj=j: e.dma_start(
                    out=xh2_d[jj * 128:(jj + 1) * 128, :], in_=EE_[:]), [t_cb])
                deps_x[j % 2] = st1

                def em_tr(e):
                    for jj in range(8):
                        r = e.transpose(pstb[:, jj * 128:(jj + 1) * 128],
                                        XHB[:, jj * 128:(jj + 1) * 128], IDB[:])
                    return r
                t_tr = p.t("tensor", em_tr, [t_cb, t_idb, tr_prev[0]])
                t_ev = p.t("vector", lambda e: e.tensor_copy(QF[:], pstb[:, 0:1024]),
                           [t_tr])
                tr_prev[0] = t_ev
                st2 = p.dma("sync", lambda e, jj=j: e.dma_start(
                    out=AP(xh2T_d.tensor, jj * 128, [[T_EXT, 128], [128 * T_EXT, 8],
                                                     [1, 128]]),
                    in_=AP(QF, 0, [[1024, 128], [128, 8], [1, 128]])), [t_ev])
                sp_tasks.append(st1)
                sp_tasks.append(st2)

            def zero_halo(dst_t, width):
                tk = []
                for q8 in range(8):
                    for side in (0, width - 32):
                        tk.append(p.dma("gpsimd", lambda e, q8=q8, s=side, d=dst_t:
                                        e.dma_start(out=AP(d.tensor, q8 * 128 * width + s,
                                                           [[width, 128], [1, 32]]),
                                                    in_=ZEROB[:]), [t_zerob]))
                return tk
            zh = []
            for n in ("q", "k", "kp", "v"):
                zh += zero_halo(spl[n], SPW)

            for n in ("q", "k", "kp", "v"):
                ldw = p.dma("sync", lambda e, n=n: e.dma_start(
                    out=AP(PW, 0, [[8 * D, 128], [D, 8], [1, D]]),
                    in_=AP(wT_d[n].tensor, 0, [[D, 128], [128 * D, 8], [1, D]])),
                    sp_tasks + entry_deps)
                prev_mm = [None]
                for j in range(NT):
                    ldx = p.dma("sync", lambda e, jj=j: e.dma_start(
                        out=AP(XHT, 0, [[8 * 128, 128], [128, 8], [1, 128]]),
                        in_=AP(xh2T_d.tensor, jj * 128, [[T_EXT, 128],
                                                         [128 * T_EXT, 8], [1, 128]])),
                        [sp_tasks[2 * j + 1], prev_mm[0]])

                    def em_mm(e):
                        for do8 in range(8):
                            for di in range(8):
                                r = e.matmul(pst2[:, do8 * 128:(do8 + 1) * 128],
                                             PW[:, di, do8 * 128:(do8 + 1) * 128],
                                             XHT[:, di, :],
                                             start=(di == 0), stop=(di == 7))
                        return r
                    mm = p.t("tensor", em_mm, [ldx, ldw, prev_mm[0]])
                    scl = 0.125 if n == "q" else 1.0
                    t_ev = p.t("vector", lambda e, s=scl: e.tensor_scalar(
                        QF[:], pst2[:, 0:1024], s, None, ALU.mult), [mm])
                    prev_mm[0] = t_ev
                    st = p.dma("sync", lambda e, n=n, jj=j: e.dma_start(
                        out=AP(spl[n].tensor, 32 + jj * 128,
                               [[SPW, 128], [128 * SPW, 8], [1, 128]]),
                        in_=AP(QF, 0, [[1024, 128], [128, 8], [1, 128]])), [t_ev])
                    sp_tasks.append(st)
            barrier = p.tails()
        return sp_tasks, barrier, zh

    def attn_phase(entry_deps):
        with contextlib.ExitStack() as fs:
            KB = fs.enter_context(nc.sbuf_tensor("KB", [128, ACH + 32], BF16))
            KPB = fs.enter_context(nc.sbuf_tensor("KPB", [128, ACH + 64], BF16))
            VB = fs.enter_context(nc.sbuf_tensor("VB", [128, ACH + 64], BF16))
            QBD = fs.enter_context(nc.sbuf_tensor("QBD", [128, 2 * ACH], BF16))
            KKB = fs.enter_context(nc.sbuf_tensor("KKB", [128, 31, ACH + 32], BF16))
            SC1 = fs.enter_context(nc.sbuf_tensor("SC1", [128, 992], F32))
            MSK0 = fs.enter_context(nc.sbuf_tensor("MSK0", [128, 992], F32))
            MSK1 = fs.enter_context(nc.sbuf_tensor("MSK1", [128, 992], F32))
            PSM = fs.enter_context(nc.sbuf_tensor("PSM", [128, 512], BF16))
            DBF = fs.enter_context(nc.sbuf_tensor("DBF", [128, 992], BF16)) \
                if cfg.get("dbg") else None
            EB = fs.enter_context(nc.sbuf_tensor("EB", [128, 992], BF16))
            PB_ = fs.enter_context(nc.sbuf_tensor("PB_", [128, 1504], BF16))
            PSR = fs.enter_context(nc.sbuf_tensor("PSR", [128, 512], BF16))
            PSTS = fs.enter_context(nc.sbuf_tensor("PSTS",
                                                   [128, ACH // 64, 4, 128], BF16))
            UF = fs.enter_context(nc.sbuf_tensor("UF", [128, 16, ACH + 32], BF16))
            UT = fs.enter_context(nc.sbuf_tensor("UT",
                                                 [128, (ACH + 32) // 8, 128], BF16))
            ZB = fs.enter_context(nc.sbuf_tensor("ZB", [128, 8, ACH], BF16))
            WOT = fs.enter_context(nc.sbuf_tensor("WOT", [128, 8, D], BF16))
            WOU = fs.enter_context(nc.sbuf_tensor("WOU", [128, 8, ACH], F32))
            XH2 = fs.enter_context(nc.sbuf_tensor("XH2", [128, D], F32))
            WS = fs.enter_context(nc.sbuf_tensor("WS", [128, D], F32))
            VV_ = fs.enter_context(nc.sbuf_tensor("VV_", [128, D], F32))
            NMX = fs.enter_context(nc.sbuf_tensor("NMX", [128, 1], F32))
            SUS = fs.enter_context(nc.sbuf_tensor("SUS", [128, 1], F32))
            RSS = fs.enter_context(nc.sbuf_tensor("RSS", [128, 1], F32))

            ld_m0 = p.dma("sync", lambda e: e.dma_start(out=MSK0[:], in_=mask_i[0]),
                          entry_deps)
            ld_m1 = p.dma("sync", lambda e: e.dma_start(out=MSK1[:], in_=mask_i[1]),
                          entry_deps)
            ld_pmf = p.dma("sync", lambda e: e.dma_start(out=SC1[:, 0:512],
                                                         in_=psmask_i[:, :]), entry_deps)
            t_pm = p.t("vector", lambda e: e.tensor_copy(PSM[:], SC1[:, 0:512]),
                       [ld_pmf])
            ld_wo = p.dma("sync", lambda e: e.dma_start(
                out=AP(WOT, 0, [[8 * D, 128], [D, 8], [1, D]]),
                in_=AP(wT_d["o"].tensor, 0, [[D, 128], [128 * D, 8], [1, D]])),
                entry_deps)
            t_qbd0 = mset("gpsimd", QBD[:], 0.0, entry_deps)
            t_pb0 = mset("gpsimd", PB_[:, 0:512], 0.0, entry_deps)

            st_tasks = []
            prev_chunk_tail = [None]
            t_zevac = [None, None]
            prev_kk = [None]
            prev_amm = [None]
            prev_utt = [None]
            prev_z = [None]
            for (c0, CL) in CHUNKS:
                NG = CL // 64
                NBLK = CL // 16
                NUTT = (CL + 32) // 8
                q_score_done = []
                t_zb = {}
                for q in range(8):
                    ld_k = p.dma("sync", lambda e, q=q, c0=c0, CL=CL: e.dma_start(
                        out=KB[:, 0:CL + 32],
                        in_=spl["k"][q][:, 16 + c0:16 + c0 + CL + 32]),
                        [prev_chunk_tail[0], prev_kk[0]])
                    ld_kp = p.dma("sync", lambda e, q=q, c0=c0, CL=CL: e.dma_start(
                        out=KPB[:, 0:CL + 64], in_=spl["kp"][q][:, c0:c0 + CL + 64]),
                        [prev_chunk_tail[0], prev_kk[0]])
                    ld_q1 = p.dma("sync", lambda e, q=q, c0=c0, CL=CL: e.dma_start(
                        out=AP(QBD, 0, [[2 * ACH, 64], [32, CL // 16], [1, 16]]),
                        in_=AP(spl["q"].tensor, q * 128 * SPW + 32 + c0,
                               [[SPW, 64], [16, CL // 16], [1, 16]])),
                        [t_qbd0, prev_chunk_tail[0], prev_amm[0]])
                    ld_q2 = p.dma("sync", lambda e, q=q, c0=c0, CL=CL: e.dma_start(
                        out=AP(QBD, 64 * 2 * ACH + 16,
                               [[2 * ACH, 64], [32, CL // 16], [1, 16]]),
                        in_=AP(spl["q"].tensor, q * 128 * SPW + 64 * SPW + 32 + c0,
                               [[SPW, 64], [16, CL // 16], [1, 16]])),
                        [t_qbd0, prev_chunk_tail[0], prev_amm[0]])

                    def em_kk(e, CL=CL):
                        return e.tensor_tensor(
                            KKB[:, :, 0:CL + 32].opt(),
                            AP(KB, 0, [[ACH + 32, 128], [0, 31], [1, CL + 32]]),
                            AP(KPB, 31, [[ACH + 64, 128], [-1, 31], [1, CL + 32]]),
                            ALU.mult)
                    t_kk = p.t("vector", em_kk, [ld_k, ld_kp, prev_amm[0]])
                    prev_kk[0] = t_kk

                    grp_tasks = []
                    prev_soft = [prev_z[0]]
                    for grp in range(NG):
                        def em_amm(e, grp=grp, q=q, CL=CL):
                            for b4 in range(4):
                                blk = grp * 4 + b4
                                lhs = AP(QBD, blk * 32, [[2 * ACH, 128], [1, 32]])
                                rhs1 = AP(KKB, 16 * blk,
                                          [[31 * (ACH + 32), 128], [ACH + 32, 16],
                                           [1, 32]])
                                rhs2 = AP(KKB, 16 * (ACH + 32) + 16 * blk,
                                          [[31 * (ACH + 32), 128], [ACH + 32, 15],
                                           [1, 32]])
                                e.matmul(psa[b4 * 32:b4 * 32 + 32, 0:512], lhs, rhs1,
                                         start=True, stop=True,
                                         tile_position=(0, b4 * 32),
                                         skip_group_check=True)
                                r = e.matmul(psa[b4 * 32:b4 * 32 + 32, 512:992], lhs,
                                             rhs2, start=True, stop=True,
                                             tile_position=(0, b4 * 32),
                                             skip_group_check=True)
                            return r
                        t_amm = p.t("tensor", em_amm,
                                    [t_kk, ld_q1, ld_q2, prev_soft[0]])
                        prev_amm[0] = t_amm
                        gidx = (c0 + grp * 64) // 64
                        msk = MSK0 if gidx == mg_first else MSK1

                        def em_soft1(e, msk=msk):
                            e.tensor_tensor(SC1[:], psa[:, 0:992], msk[:], ALU.add)
                            e.reduce_max(NMX[:, 0:1], SC1[:], AX.X)
                            return e.tensor_scalar(NMX[:], NMX[:], -1.0, None, ALU.mult)
                        t_sf1 = p.t("vector", em_soft1, [t_amm, ld_m0, ld_m1, t_pm])
                        t_exp = p.t("scalar", lambda e: e.activation(
                            EB[:], SC1[:], AF.Exp, bias=NMX[:, 0:1],
                            accum_out=SUS[:, 0:1]), [t_sf1])

                        def em_nrm(e):
                            e.reciprocal(RSS[:], SUS[:])
                            return e.tensor_scalar(PB_[:, 512:1504], EB[:],
                                                   RSS[:, 0:1], None, ALU.mult)
                        t_nrm = p.t("vector", em_nrm, [t_exp, t_pb0])

                        def em_fold(e):
                            e.tensor_copy(AP(PSR, 0, [[512, 128], [16, 32], [1, 1]]),
                                          AP(PB_, 512 + 480, [[1504, 128], [1, 32], [32, 1]]))
                            e.tensor_tensor(
                                AP(PSR, 1, [[512, 128], [16, 32], [1, 15]]),
                                AP(PB_, 512 + 512, [[1504, 128], [1, 32], [32, 15]]),
                                AP(PB_, 512 + 447, [[1504, 128], [1, 32], [-33, 15]]),
                                ALU.add)
                            return e.tensor_tensor(PSR[:], PSR[:], PSM[:], ALU.mult)
                        t_fold = p.t("vector", em_fold, [t_nrm, t_pm])

                        def em_pst(e):
                            for qh in range(4):
                                r = e.transpose(pstb[:, qh * 128:(qh + 1) * 128],
                                                PSR[:, qh * 128:(qh + 1) * 128],
                                                IDB[:])
                            return r
                        t_pst = p.t("tensor", em_pst, [t_fold, t_idb])
                        t_pse = p.t("vector", lambda e, grp=grp: e.tensor_copy(
                            AP(PSTS, grp * 4 * 128,
                               [[(ACH // 64) * 4 * 128, 128], [1, 512]]),
                            pstb[:, 0:512]), [t_pst])
                        prev_soft[0] = t_pse
                        grp_tasks.append(t_pse)
                        if cfg.get("dbg") and c0 == 0 and q == 0 and grp == 2:
                            dbf = p.t("vector", lambda e: e.tensor_copy(
                                DBF[:], SC1[:]), [t_sf1, t_exp])
                            p.dma("sync", lambda e: e.dma_start(
                                out=dbg_d[0][:, 1024:2016], in_=DBF[:]), [dbf])
                            p.dma("sync", lambda e: e.dma_start(
                                out=dbg_d[0][:, 0:992], in_=EB[:]), [t_exp, t_nrm])
                            p.dma("sync", lambda e: e.dma_start(
                                out=dbg_d[1][:, 0:1504], in_=PB_[:]),
                                [t_nrm, t_fold])
                            p.dma("sync", lambda e: e.dma_start(
                                out=dbg_d[1][:, 2048:2560], in_=PSM[:]), [t_pm])
                            p.dma("sync", lambda e: e.dma_start(
                                out=dbg_d[1][:, 3000:3512], in_=PSR[:]),
                                [t_fold, t_pst])

                    if cfg.get("dbg") and c0 == 0 and q == 0:
                        p.dma("sync", lambda e: e.dma_start(
                            out=dbg_d[2][:, 0:4096],
                            in_=AP(PSTS, 0, [[(ACH // 64) * 4 * 128, 128],
                                             [1, 4096]])), grp_tasks[-1:])
                        p.dma("sync", lambda e: e.dma_start(
                            out=dbg_d[5][:, 0:2 * ACH],
                            in_=AP(QBD, 0, [[2 * ACH, 128], [1, 2 * ACH]])),
                            [ld_q1, ld_q2])
                        p.dma("sync", lambda e: e.dma_start(
                            out=dbg_d[6][:, 0:544],
                            in_=AP(KKB, 15 * (ACH + 32),
                                   [[31 * (ACH + 32), 128], [1, 544]])), [t_kk])
                        p.dma("sync", lambda e: e.dma_start(
                            out=dbg_d[6][:, 1024:1024 + ACH + 32],
                            in_=KB[:, 0:ACH + 32]), [t_kk])
                        p.dma("sync", lambda e: e.dma_start(
                            out=dbg_d[6][:, 2048:2048 + ACH + 64],
                            in_=KPB[:, 0:ACH + 64]), [t_kk])

                    ld_v = p.dma("sync", lambda e, q=q, c0=c0, CL=CL: e.dma_start(
                        out=VB[:, 0:CL + 64], in_=spl["v"][q][:, c0:c0 + CL + 64]),
                        [prev_chunk_tail[0], prev_utt[0]])

                    def em_uf(e, CL=CL):
                        return e.tensor_tensor(
                            AP(UF, 0, [[16 * (ACH + 32), 128], [1, 16], [16, CL + 32]]),
                            AP(VB, 16, [[ACH + 64, 128], [0, 16], [1, CL + 32]]),
                            AP(VB, 16, [[ACH + 64, 128], [-1, 16], [1, CL + 32]]),
                            ALU.mult)
                    t_uf = p.t("vector", em_uf, [ld_v, prev_utt[0]])
                    ut_ev = []
                    for kt0 in range(0, NUTT, 8):
                        nk = min(8, NUTT - kt0)

                        def em_utt(e, kt0=kt0, nk=nk, CL=CL):
                            for kk_ in range(nk):
                                r = e.transpose(
                                    pstb[:, kk_ * 128:(kk_ + 1) * 128],
                                    AP(UF, (kt0 + kk_) * 128,
                                       [[16 * (ACH + 32), 128], [1, 128]]),
                                    IDB[:])
                            return r
                        t_utt = p.t("tensor", em_utt,
                                    [t_uf, t_idb, prev_z[0]] +
                                    ([ut_ev[-1]] if ut_ev else grp_tasks[-1:]))
                        t_ute = p.t("vector", lambda e, kt0=kt0, nk=nk: e.tensor_copy(
                            AP(UT, kt0 * 128,
                            [[((ACH + 32) // 8) * 128, 128], [1, nk * 128]]),
                            pstb[:, 0:nk * 128]), [t_utt])
                        ut_ev.append(t_ute)
                    prev_utt[0] = ut_ev[-1]
                    if cfg.get("dbg") and c0 == 0 and q == 0:
                        p.dma("sync", lambda e: e.dma_start(
                            out=dbg_d[3][:, 0:4096],
                            in_=AP(UT, 0, [[((ACH + 32) // 8) * 128, 128],
                                           [1, 4096]])), [ut_ev[-1]])
                        p.dma("sync", lambda e: e.dma_start(
                            out=dbg_d[4][:, 0:NUTT * 128 - 4096],
                            in_=AP(UT, 4096, [[((ACH + 32) // 8) * 128, 128],
                                              [1, NUTT * 128 - 4096]])),
                            [ut_ev[-1]])
                        p.dma("sync", lambda e: e.dma_start(
                            out=dbg_d[7][:, 1024:1024 + ACH + 64],
                            in_=VB[:, 0:ACH + 64]), [t_uf])

                    psz = psb if q % 2 == 0 else psc
                    def em_z(e, q=q, CL=CL, psz=psz, NBLK=NBLK):
                        for blk in range(NBLK):
                            grp, sub = blk // 4, blk % 4
                            for h in range(2):
                                for qh in range(4):
                                    r = e.matmul(
                                        psz[64 * h:64 * h + 64,
                                            16 * blk:16 * blk + 16],
                                        AP(UT, (2 * blk + qh) * 128 + 64 * h,
                                           [[((ACH + 32) // 8) * 128, 128],
                                            [1, 64]]),
                                        AP(PSTS, (grp * 4 + qh) * 128 + 32 * sub
                                           + 16 * h,
                                           [[(ACH // 64) * 4 * 128, 128],
                                            [1, 16]]),
                                        start=(blk == 0 and qh == 0),
                                        stop=(blk == NBLK - 1 and qh == 3),
                                        tile_position=(0, 64 * h),
                                        skip_group_check=True)
                        return r
                    t_z = p.t("tensor", em_z,
                              ut_ev + grp_tasks + [t_zevac[q % 2]])
                    if c0 == 0 and q == 0:
                        t_z.label = "t_z c0 q0"
                        for _i, _t in enumerate(ut_ev):
                            _t.label = f"ut_ev[{_i}] c0 q0" 
                    prev_z[0] = t_z
                    t_ze = p.t("vector", lambda e, q=q, psz=psz, CL=CL: e.tensor_copy(
                        ZB[:, q, 0:CL], psz[:, 0:CL]), [t_z])
                    t_zevac[q % 2] = t_ze
                    t_zb[q] = t_ze
                    q_score_done.append(t_ze)
                    if c0 == 0 and q == 0:
                        t_ze.label = "t_ze c0 q0"
                    if cfg.get("dbg") and c0 == 0 and q == 0:
                        p.dma("sync", lambda e, CL=CL: e.dma_start(
                            out=dbg_d[7][:, 0:CL], in_=ZB[:, 0, 0:CL]), [t_ze])

                wo_ev = []
                for pp in range(4):
                    def em_wo(e, pp=pp, CL=CL):
                        for dd in range(2):
                            d8 = pp * 2 + dd
                            for di in range(8):
                                r = e.matmul(pst2[:, dd * 512:dd * 512 + CL],
                                             WOT[:, di, d8 * 128:(d8 + 1) * 128],
                                             ZB[:, di, 0:CL],
                                             start=(di == 0), stop=(di == 7))
                        return r
                    mmw = p.t("tensor", em_wo,
                              list(t_zb.values()) + [ld_wo] + wo_ev[-1:])
                    t_we = p.t("vector", lambda e, pp=pp, CL=CL: (
                        e.tensor_copy(WOU[:, pp * 2, 0:CL], pst2[:, 0:CL]),
                        e.tensor_copy(WOU[:, pp * 2 + 1, 0:CL],
                                      pst2[:, 512:512 + CL]))[-1], [mmw])
                    wo_ev.append(t_we)

                prev_w = [None]
                prev_fin = [None, None]
                for si in range(CL // 128):
                    sub = (c0 // 128) + si
                    buf = X0 if sub % 2 == 0 else X1
                    ldx = p.dma("sync", lambda e, sub=sub, b=buf: e.dma_start(
                        out=b[:], in_=xA_d[sub * 128:(sub + 1) * 128, :]),
                        wo_ev + [prev_fin[sub % 2]])
                    ldh = p.dma("sync", lambda e, sub=sub: e.dma_start(
                        out=XH2[:], in_=xh2_d[sub * 128:(sub + 1) * 128, :]),
                        wo_ev + [prev_chunk_tail[0], prev_w[0]])

                    def em_trw(e, si=si, CL=CL):
                        for d8 in range(8):
                            r = e.transpose(pst2[:, d8 * 128:(d8 + 1) * 128],
                                            WOU[:, d8, si * 128:si * 128 + 128],
                                            IDF[:])
                        return r
                    t_trw = p.t("tensor", em_trw, wo_ev + [ld_id, prev_w[0]])
                    t_w = p.t("vector", lambda e: e.tensor_tensor(
                        WS[:], pst2[:, 0:1024], XH2[:], ALU.add), [t_trw, ldh])
                    prev_w[0] = t_w
                    t_sq = p.t("scalar", lambda e: e.activation(
                        TT_[:], WS[:], AF.Square, accum_out=C["SUMQ"][:, 0:1]), [t_w])

                    def em_st1(e):
                        e.tensor_reduce(C["SUMX"][:], WS[:], AX.X, ALU.add)
                        e.tensor_scalar(C["MU"][:], C["SUMX"][:], 1.0 / D, None,
                                        ALU.mult)
                        e.tensor_scalar(C["TS0"][:], C["SUMQ"][:], 1.0 / D, None,
                                        ALU.mult)
                        e.tensor_tensor(C["TS1"][:], C["MU"][:], C["MU"][:], ALU.mult)
                        e.tensor_tensor(C["TS0"][:], C["TS0"][:], C["TS1"][:],
                                        ALU.subtract)
                        return e.tensor_scalar(C["TS0"][:], C["TS0"][:], 1e-5, None,
                                               ALU.add)
                    t_st1 = p.t("vector", em_st1, [t_w, t_sq])
                    t_sqr = p.t("scalar", lambda e: e.activation(
                        C["TS2"][:], C["TS0"][:], AF.Sqrt), [t_st1])

                    def em_st2(e):
                        e.reciprocal(C["RST"][:], C["TS2"][:])
                        for _ in range(2):
                            e.tensor_tensor(C["TS1"][:], C["RST"][:], C["RST"][:],
                                            ALU.mult)
                            e.tensor_tensor(C["TS1"][:], C["TS1"][:], C["TS0"][:],
                                            ALU.mult)
                            e.tensor_scalar(C["TS1"][:], C["TS1"][:], -0.5, 1.5,
                                            ALU.mult, ALU.add)
                            e.tensor_tensor(C["RST"][:], C["RST"][:], C["TS1"][:],
                                            ALU.mult)
                        e.tensor_tensor(C["BCC"][:], C["MU"][:], C["RST"][:], ALU.mult)
                        return e.tensor_scalar(C["BCC"][:], C["BCC"][:], -1.0, None,
                                               ALU.mult)
                    t_st2 = p.t("vector", em_st2, [t_sqr])
                    t_v = p.t("scalar", lambda e: e.activation(
                        VV_[:], WS[:], AF.Identity, bias=C["BCC"][:, 0:1],
                        scale=C["RST"][:, 0:1]), [t_st2])

                    def em_fin(e, b=buf):
                        e.tensor_tensor(VV_[:], VV_[:], GB[:], ALU.mult)
                        e.tensor_tensor(VV_[:], VV_[:], BB[:], ALU.add)
                        return e.tensor_tensor(WS[:], b[:], VV_[:], ALU.add)
                    t_fin = p.t("vector", em_fin, [t_v, ld_gb, ld_bb, ldx])
                    st = p.dma("sync", lambda e, sub=sub: e.dma_start(
                        out=xB_d[sub * 128:(sub + 1) * 128, :], in_=WS[:]), [t_fin])
                    prev_fin[sub % 2] = st
                    st_tasks.append(st)
                prev_chunk_tail[0] = st_tasks[-1]
            barrier = p.tails()
        return st_tasks, barrier

    if dev_stop == "p0":
        bar = p.tails()
        d1 = p.dma("sync", lambda e: e.dma_start(
            out=AP(out.tensor, 0, [[D, 128], [1, 32]]), in_=CORR1[:]), bar)
        d2 = p.dma("sync", lambda e: e.dma_start(
            out=AP(out.tensor, 32, [[D, 128], [1, 8]]), in_=CORR2[:]), [d1])
        d3 = p.dma("sync", lambda e: e.dma_start(
            out=AP(out.tensor, 40, [[D, 128], [1, 1]]), in_=SW1[:],
            allow_slow_non_contiguous=True), [d2])
        d4 = p.dma("sync", lambda e: e.dma_start(
            out=AP(out.tensor, 41, [[D, 128], [1, 1]]), in_=ISW1[:],
            allow_slow_non_contiguous=True), [d3])
        d5 = p.dma("sync", lambda e: e.dma_start(
            out=AP(out.tensor, 42, [[D, 128], [1, 1]]), in_=SW2[:],
            allow_slow_non_contiguous=True), [d4])
        d6 = p.dma("sync", lambda e: e.dma_start(
            out=AP(out.tensor, 43, [[D, 128], [1, 1]]), in_=ISW2[:],
            allow_slow_non_contiguous=True), [d5])
        d7 = p.dma("sync", lambda e: e.dma_start(
            out=AP(out.tensor, 128 * D, [[D, 128], [1, D]]), in_=GB[:]), [d6])
        d8 = p.dma("sync", lambda e: e.dma_start(
            out=AP(out.tensor, 256 * D, [[D, 128], [1, D]]), in_=BB[:]), [d7])
        d9 = p.dma("sync", lambda e: e.dma_start(
            out=AP(out.tensor, 384 * D, [[D, 128], [1, 128]]),
            in_=AP(WCV, 0, [[128, 128], [1, 128]])), [d8])
        p.t("gpsimd", lambda e: e.engine_nop(), [d9])
        p.finalize()
        return nc
    def _dump_dram(src_ap, ncols, deps):
        th = src_ap.tensor
        rowlen = th.shape[2]
        DBG16x = nc.alloc_sbuf_tensor("DBG16x", [128, ncols], th.dtype)
        DBGFx = nc.alloc_sbuf_tensor("DBGFx", [128, ncols], F32)
        nsub = ncols // D
        prev = None
        for di in range(8):
            ld = p.dma("sync", lambda e, di=di: e.dma_start(
                out=DBG16x.ap()[:],
                in_=AP(th, di * 128 * rowlen, [[rowlen, 128], [1, ncols]])),
                deps + ([prev] if prev else []))
            cv = p.t("vector", lambda e: e.tensor_copy(DBGFx.ap()[:],
                                                       DBG16x.ap()[:]), [ld])
            prev = p.dma("sync", lambda e, di=di: e.dma_start(
                out=AP(out.tensor, di * 128 * nsub * D,
                       [[nsub * D, 128], [1, nsub * D]]),
                in_=DBGFx.ap()[:]), [cv])
        p.t("gpsimd", lambda e: e.engine_nop(), [prev])
        p.finalize()
        return nc

    def _dump_srcs():
        return dict(wq1T=wq1T_d, wq2T=wq2T_d, xh2T=xh2T_d,
                    **{f"w{n}T": wT_d[n] for n in wT_d},
                    **{f"spl{n}": spl[n] for n in spl})

    if dev_stop and dev_stop.startswith("dump0:"):
        nm, ncols = dev_stop.split(":")[1], int(dev_stop.split(":")[2])
        return _dump_dram(_dump_srcs()[nm], ncols, p.tails())
    if dev_stop in ("p0w", "p0v"):
        DBG16 = nc.alloc_sbuf_tensor("DBG16", [128, DH], FP16)
        DBGF = nc.alloc_sbuf_tensor("DBGF", [128, DH], F32)
        bar = p.tails()
        prev = None
        if dev_stop == "p0w":
            for di in range(8):
                ld = p.dma("sync", lambda e, di=di: e.dma_start(
                    out=DBG16[:], in_=wq1T_d[di]),
                    bar + ([prev] if prev else []))
                cv = p.t("vector", lambda e: e.tensor_copy(DBGF[:], DBG16[:]), [ld])
                prev = p.dma("sync", lambda e, di=di: e.dma_start(
                    out=AP(out.tensor, di * 512 * D,
                           [[4 * D, 128], [D, 4], [1, D]]),
                    in_=DBGF[:]), [cv])
        else:
            for s in range(32):
                ld = p.dma("sync", lambda e, s=s: e.dma_start(
                    out=DBG16[:, 0:D], in_=wq2T_d[s]),
                    bar + ([prev] if prev else []))
                cv = p.t("vector", lambda e: e.tensor_copy(DBGF[:, 0:D],
                                                           DBG16[:, 0:D]), [ld])
                prev = p.dma("sync", lambda e, s=s: e.dma_start(
                    out=AP(out.tensor, s * 128 * D, [[D, 128], [1, D]]),
                    in_=DBGF[:, 0:D]), [cv])
        p.t("gpsimd", lambda e: e.engine_nop(), [prev])
        p.finalize()
        return nc
    p1_st, bar1 = ffn_phase(x_in, xA_d, own_only=False, entry_deps=p0_done, tag="a")
    if dev_stop == "p1":
        fin = p.dma("sync", lambda e: e.dma_start(
            out=out[:, :], in_=xA_d[OWN0:OWN0 + OWN, :]), p1_st)
        p.t("gpsimd", lambda e: e.engine_nop(), [fin])
        p.finalize()
        return nc

    if dev_stop and dev_stop.startswith("dump1:"):
        nm, ncols = dev_stop.split(":")[1], int(dev_stop.split(":")[2])
        return _dump_dram(_dump_srcs()[nm], ncols, p.tails())
    p2_st, bar2, zh = proj_phase(bar1)
    if dev_stop and dev_stop.startswith("dump2:"):
        nm, ncols = dev_stop.split(":")[1], int(dev_stop.split(":")[2])
        return _dump_dram(_dump_srcs()[nm], ncols, p.tails())
    if dev_stop == "p2":
        fin = p.dma("sync", lambda e: e.dma_start(
            out=out[:, :], in_=xh2_d[OWN0:OWN0 + OWN, :]), p2_st)
        p.t("gpsimd", lambda e: e.engine_nop(), [fin])
        p.finalize()
        return nc

    p3_st, bar3 = attn_phase(bar2 + zh)
    if dev_stop and dev_stop.startswith("dumpA:"):
        nm, ncols = dev_stop.split(":")[1], int(dev_stop.split(":")[2])
        return _dump_dram(dbg_d, ncols, p.tails())
    if dev_stop == "p3":
        fin = p.dma("sync", lambda e: e.dma_start(
            out=out[:, :], in_=xB_d[OWN0:OWN0 + OWN, :]), p3_st)
        p.t("gpsimd", lambda e: e.engine_nop(), [fin])
        p.finalize()
        return nc

    p4_st, bar4 = ffn_phase(xB_d, out, own_only=True, entry_deps=bar3, tag="b")
    p.t("gpsimd", lambda e: e.engine_nop(), p4_st)
    p.finalize()
    return nc


T_EXT_FULL = 4352
OWN_FULL = 4096
OWN0_FULL = 128


def _build_masks():
    mask = np.zeros((2, 128, 992), np.float32)
    p = np.arange(128)
    blk4 = p // 32
    u = (p % 32) % 16
    gi = np.arange(31)
    mh = np.arange(32)
    g = gi - 15
    MH = mh[None, None, :]
    U = u[:, None, None]
    G = g[None, :, None]
    B4 = blk4[:, None, None]
    valid = ((U + 1 <= MH) & (MH <= U + 16) &
             (U + 1 <= MH - G) & (MH - G <= U + 16))
    kkey = 16 * B4 + MH - 16
    kpkey = kkey - G
    row0 = valid & (kkey >= 0) & (kpkey >= 0)
    mask[1] = np.where(valid, 0.0, -30000.0).reshape(128, 992)
    mask[0] = np.where(row0, 0.0, -30000.0).reshape(128, 992)
    psmask = np.zeros((128, 512), np.float32)
    mm2, gg2 = np.meshgrid(np.arange(32), np.arange(16), indexing="ij")
    psmask[:, :] = (mm2 >= gg2).astype(np.float32).reshape(1, 512)
    return mask, psmask


_NC_CACHE = {}


def _get_nc():
    key = "full"
    if key not in _NC_CACHE:
        cfg = dict(T_EXT=T_EXT_FULL, OWN0=OWN0_FULL, OWN=OWN_FULL,
                   maskgrp_first=OWN0_FULL // 64, dev_stop=None)
        _NC_CACHE[key] = (build(cfg), cfg)
    return _NC_CACHE[key]


def kernel(L0_attn_x, L0_attn_Wq, L0_attn_Wk, L0_attn_Wv, L0_attn_Wkp,
           L0_attn_Wo, L0_attn_gamma, L0_attn_beta, L0_ffn_w1, L0_ffn_conv_w,
           L0_ffn_conv_b, L0_ffn_w2):
    from concourse.bass_utils import run_bass_kernel_spmd
    x = np.ascontiguousarray(np.asarray(L0_attn_x, dtype=np.float32))
    B, S, Dm = x.shape
    assert (B, S, Dm) == (4, 8192, 1024)
    mask, psmask = _build_masks()
    mask_noseq = mask.copy()
    mask_noseq[0] = mask_noseq[1]
    ident = np.eye(128, dtype=np.float32)
    common = {
        "w_q": np.ascontiguousarray(L0_attn_Wq, dtype=np.float32),
        "w_k": np.ascontiguousarray(L0_attn_Wk, dtype=np.float32),
        "w_kp": np.ascontiguousarray(L0_attn_Wkp, dtype=np.float32),
        "w_v": np.ascontiguousarray(L0_attn_Wv, dtype=np.float32),
        "w_o": np.ascontiguousarray(L0_attn_Wo, dtype=np.float32),
        "gamma": np.ascontiguousarray(L0_attn_gamma, dtype=np.float32),
        "beta": np.ascontiguousarray(L0_attn_beta, dtype=np.float32),
        "w1": np.ascontiguousarray(L0_ffn_w1, dtype=np.float32),
        "conv_w": np.ascontiguousarray(
            np.asarray(L0_ffn_conv_w, dtype=np.float32).reshape(4096, 3)),
        "conv_b": np.ascontiguousarray(L0_ffn_conv_b, dtype=np.float32),
        "w2": np.ascontiguousarray(L0_ffn_w2, dtype=np.float32),
        "ident": ident,
        "psmask": psmask,
    }
    in_maps = []
    for core in range(8):
        b, half = core // 2, core % 2
        lo = half * 4096 - 128
        hi = lo + T_EXT_FULL
        xe = np.zeros((T_EXT_FULL, Dm), np.float32)
        s0, s1 = max(lo, 0), min(hi, S)
        xe[s0 - lo:s1 - lo] = x[b, s0:s1]
        m = dict(common)
        m["x_in"] = xe
        m["mask"] = mask if half == 0 else mask_noseq
        in_maps.append(m)
    nc, cfg = _get_nc()
    res = run_bass_kernel_spmd(nc, in_maps, core_ids=list(range(8)))
    outp = np.empty((4, 8192, 1024), np.float32)
    for core in range(8):
        b, half = core // 2, core % 2
        outp[b, half * 4096:(half + 1) * 4096] = np.asarray(
            res.results[core]["out"], dtype=np.float32)
    return outp


# version 09 (final)
# speedup vs baseline: 1.0360x; 1.0114x over previous
import contextlib
import numpy as np
import concourse.bass as bass
import concourse.mybir as mybir
from concourse.ap import AP

F32 = mybir.dt.float32
BF16 = mybir.dt.bfloat16
FP16 = mybir.dt.float16
I16 = mybir.dt.int16
AF = mybir.ActivationFunctionType
ALU = mybir.AluOpType
AX = mybir.AxisListType
ENGS = ("sync", "tensor", "vector", "scalar", "gpsimd")
D = 1024
DH = 4096


class Task:
    __slots__ = ("eng", "emit", "deps", "dma", "token", "do_inc", "label")

    def __init__(self, eng, emit, deps, dma):
        self.label = None
        self.eng = eng
        self.emit = emit
        self.deps = [d for d in deps if d is not None]
        self.dma = dma
        self.token = None
        self.do_inc = False


class _HazEng:
    _PASS = ("wait_ge", "dma_start", "engine_nop")

    def __init__(self, e, sem, state, dummy, has_memset, src):
        self._e = e
        self._sem = sem
        self._st = state
        self._dummy = dummy
        self._mset = has_memset
        self._src = src
        self.last_r = None
        self.last_short = False

    def __getattr__(self, name):
        real = getattr(self._e, name)
        if name in self._PASS or not callable(real):
            return real

        def wrap(*a, **k):
            r = real(*a, **k)
            short = True
            try:
                short = (a[0].free_size < 512) or (k.get("accum_out") is not None)
            except Exception:
                short = True
            if short:
                if self._mset:
                    self._e.tensor_reduce(self._dummy, self._src(),
                                          AX.X, ALU.add)
                else:
                    self._st[0] += 1
                    r.then_inc(self._sem, 1)
                    self._e.wait_ge(self._sem, self._st[0])
            self.last_r = r
            self.last_short = short
            return r
        return wrap


class Prog:
    def __init__(self, nc, serial=False):
        self.nc = nc
        self.per_eng = {e: [] for e in ENGS}
        self.esem = {e: nc.alloc_semaphore(f"se_{e}") for e in ENGS}
        self.dsem = {e: nc.alloc_semaphore(f"sd_{e}") for e in ENGS}
        self.hsem = {e: nc.alloc_semaphore(f"hz_{e}")
                     for e in ("vector", "scalar", "gpsimd")}
        self.dummy = nc.alloc_sbuf_tensor("hz_dummy", [128, 1],
                                          mybir.dt.float32).ap()
        self.spacer_src = None
        self.serial = serial
        self.all = []

    def _add(self, task):
        if self.serial and self.all:
            task.deps = task.deps + [self.all[-1]]
        self.all.append(task)
        self.per_eng[task.eng].append(task)
        return task

    def t(self, eng, emit, deps=()):
        return self._add(Task(eng, emit, deps, dma=False))

    def dma(self, eng, emit, deps=()):
        return self._add(Task(eng, emit, deps, dma=True))

    def tails(self):
        return [self.per_eng[e][-1] for e in ENGS if self.per_eng[e]]

    def finalize(self):
        needed = set()
        for e in ENGS:
            for task in self.per_eng[e]:
                for d in task.deps:
                    if d.dma or d.eng != task.eng:
                        needed.add(id(d))
        ccount = {e: 0 for e in ENGS}
        dcount = {e: 0 for e in ENGS}
        for e in ENGS:
            for task in self.per_eng[e]:
                if task.dma:
                    dcount[e] += 16
                    task.token = (self.dsem[e], dcount[e])
                    task.do_inc = True
                elif id(task) in needed:
                    ccount[e] += 1
                    task.token = (self.esem[e], ccount[e])
                    task.do_inc = True
        with self.nc.Block() as block:
            for ename in ENGS:
                tasks = self.per_eng[ename]
                if not tasks:
                    continue

                def body(eng, tasks=tasks, ename=ename):
                    maxw = {}
                    if ename in self.hsem:
                        heng = _HazEng(eng, self.hsem[ename], [0],
                                       self.dummy[0:128, 0:1],
                                       ename == "vector",
                                       lambda: self.spacer_src[0:128, 0:128])
                    else:
                        heng = eng
                    for task in tasks:
                        for d in task.deps:
                            if (not d.dma) and d.eng == ename:
                                continue
                            semh, val = d.token
                            key = id(semh)
                            if maxw.get(key, 0) >= val:
                                continue
                            eng.wait_ge(semh, val)
                            maxw[key] = val
                        last = task.emit(eng if task.dma else heng)
                        if task.do_inc and last is not None:
                            if (heng is not eng and heng.last_r is last
                                    and heng.last_short):
                                if ename == "gpsimd":
                                    last = eng.engine_nop()
                                elif ename == "scalar":
                                    last = eng.copy(self.dummy[0:1, 0:1],
                                                    self.dummy[0:1, 0:1])
                                else:
                                    last = eng.memset(self.dummy[0:1, 0:1], 0.0)
                            semh, _ = task.token
                            last.then_inc(semh, 16 if task.dma else 1)
                getattr(block, ename)(body)


def build(cfg):
    T_EXT = cfg["T_EXT"]
    NT = T_EXT // 128
    SPW = 32 + T_EXT + 32
    ACH = min(512, T_EXT)
    CHUNKS = []
    cc = 0
    while cc < T_EXT:
        CHUNKS.append((cc, min(ACH, T_EXT - cc)))
        cc += ACH
    OWN0 = cfg["OWN0"]
    OWN = cfg["OWN"]
    OSUB0 = OWN0 // 128
    NOSUB = OWN // 128
    mg_first = cfg["maskgrp_first"]
    dev_stop = cfg.get("dev_stop")

    nc = bass.Bass("TRN2", target_bir_lowering=False, detect_race_conditions=False)
    p = Prog(nc, serial=cfg.get("serial", False))
    stack = contextlib.ExitStack()

    x_in = nc.dram_tensor("x_in", [T_EXT, D], F32, kind="ExternalInput").ap()
    w_att = {n: nc.dram_tensor(f"w_{n}", [D, D], F32, kind="ExternalInput").ap()
             for n in ("q", "k", "kp", "v", "o")}
    gamma_i = nc.dram_tensor("gamma", [D], F32, kind="ExternalInput").ap()
    beta_i = nc.dram_tensor("beta", [D], F32, kind="ExternalInput").ap()
    w1_i = nc.dram_tensor("w1", [DH, D], F32, kind="ExternalInput").ap()
    cw_i = nc.dram_tensor("conv_w", [DH, 3], F32, kind="ExternalInput").ap()
    cb_i = nc.dram_tensor("conv_b", [DH], F32, kind="ExternalInput").ap()
    w2_i = nc.dram_tensor("w2", [D, DH], F32, kind="ExternalInput").ap()
    ident_i = nc.dram_tensor("ident", [128, 128], F32, kind="ExternalInput").ap()
    mask_i = nc.dram_tensor("mask", [2, 128, 992], F32, kind="ExternalInput").ap()
    psmask_i = nc.dram_tensor("psmask", [128, 512], F32, kind="ExternalInput").ap()
    out = nc.dram_tensor("out", [OWN, D], F32, kind="ExternalOutput").ap()

    dr = lambda n, s, dt: nc.dram_tensor(n, s, dt, kind="Internal").ap()
    wq1T_d = dr("wq1T_d", [8, 128, DH], FP16)
    wq2T_d = dr("wq2T_d", [32, 128, D], FP16)
    wT_d = {n: dr(f"w{n}T_d", [8, 128, D], BF16) for n in ("q", "k", "kp", "v", "o")}
    xA_d = dr("xA_d", [T_EXT, D], F32)
    xB_d = dr("xB_d", [T_EXT, D], F32)
    spl = {n: dr(f"{n}_d", [8, 128, SPW], BF16) for n in ("q", "k", "kp", "v")}
    xh2_d = dr("xh2_d", [T_EXT, D], F32)
    xh2T_d = dr("xh2T_d", [8, 128, T_EXT], BF16)
    dbg_d = dr("dbg_d", [8, 128, 4096], BF16)

    sb = lambda n, s, dt=F32: nc.alloc_sbuf_tensor(n, s, dt)
    IDF = sb("IDF", [128, 128], F32)
    IDB = sb("IDB", [128, 128], BF16)
    IDH = sb("IDH", [128, 128], FP16)
    X0 = sb("X0", [128, D])
    X1 = sb("X1", [128, D])
    TT_ = sb("TT_", [128, D])
    XQI = sb("XQI", [128, D], I16)
    EE_ = sb("EE_", [128, D])
    XQH = sb("XQH", [128, D], FP16)
    WTMP = sb("WTMP", [128, D], BF16)
    WTMO = sb("WTMO", [128, D], BF16)
    ROWT = sb("ROWT", [1, 132])
    GB = sb("GB", [128, D])
    p.spacer_src = GB
    BB = sb("BB", [128, D])
    WCV = sb("WCV", [128, 32, 4])
    CORR1 = sb("CORR1", [128, 32])
    CORR2 = sb("CORR2", [128, 8])
    SW1 = sb("SW1", [128, 1])
    ISW1 = sb("ISW1", [128, 1])
    SW2 = sb("SW2", [128, 1])
    ISW2 = sb("ISW2", [128, 1])
    ONESC = sb("ONESC", [128, 1], F32)
    ONESR = sb("ONESR", [1, 128], F32)
    ZEROB = sb("ZEROB", [128, 32], BF16)
    names = "SUMX SUMQ MU RST TS0 TS1 TS2 MXC MNC RSC BCC ALC BE2".split()
    C = {n: sb(n, [128, 1]) for n in names}

    PSA = nc.alloc_psum_tensor("PSA", [128, 1024], F32)
    PSB = nc.alloc_psum_tensor("PSB", [128, 512], F32)
    PSC = nc.alloc_psum_tensor("PSC", [128, 512], F32)
    PSD = nc.alloc_psum_tensor("PSD", [128, 512], F32)
    PST2 = nc.alloc_psum_tensor("PST2", [128, 1024], F32)
    PSTB = nc.alloc_psum_tensor("PSTB", [128, 1024], BF16)
    psa, psb, psc, psd = PSA.ap(), PSB.ap(), PSC.ap(), PSD.ap()
    pst2, pstb = PST2.ap(), PSTB.ap()
    psth = pstb.bitcast(FP16)
    psth3 = psth.rearrange("p (a b) -> p a b", b=128)

    def mset(eng, tile_ap, val, deps=()):
        return p.t(eng, lambda e, t=tile_ap, v=val: e.memset(t, v), deps)

    t_ones_c = mset("gpsimd", ONESC[:], 1.0)
    t_ones_r = mset("gpsimd", ONESR[:], 1.0)
    t_zerob = mset("gpsimd", ZEROB[:], 0.0)
    ld_id = p.dma("sync", lambda e: e.dma_start(out=IDF[:], in_=ident_i[:, :]))
    t_idb = p.t("vector", lambda e: e.tensor_copy(IDB[:], IDF[:]), [ld_id])
    t_idh = p.t("vector", lambda e: e.tensor_copy(IDH[:], IDF[:]), [ld_id])
    ld_gb = p.dma("sync", lambda e: e.dma_start(
        out=GB[:], in_=AP(gamma_i.tensor, 0, [[0, 128], [1, D]])))
    ld_bb = p.dma("sync", lambda e: e.dma_start(
        out=BB[:], in_=AP(beta_i.tensor, 0, [[0, 128], [1, D]])))
    ld_cw = p.dma("sync", lambda e: e.dma_start(
        out=AP(WCV, 0, [[128, 128], [4, 32], [1, 3]]),
        in_=AP(cw_i.tensor, 0, [[3, 128], [3 * 128, 32], [1, 3]])))
    ld_cb = p.dma("sync", lambda e: e.dma_start(
        out=AP(WCV, 3, [[128, 128], [4, 32], [1, 1]]),
        in_=AP(cb_i.tensor, 0, [[1, 128], [128, 32], [0, 1]]),
        allow_slow_non_contiguous=True))

    t_ai1 = mset("vector", C["SUMX"][:], 0.0)
    t_ai2 = mset("vector", C["SUMQ"][:], 0.0)
    prev_buf = [None, None]
    w2_flat = w2_i.rearrange("a (b c) -> (a b) c", c=D)
    red31 = red63 = None
    for i in range(64):
        buf = X0 if i % 2 == 0 else X1
        src = w1_i[i * 128:(i + 1) * 128, :] if i < 32 else \
            w2_flat[(i - 32) * 128:(i - 31) * 128, :]
        ld = p.dma("sync", lambda e, b=buf, s=src: e.dma_start(out=b[:], in_=s),
                   [prev_buf[i % 2]])
        acc = C["SUMX"] if i < 32 else C["SUMQ"]
        dep0 = t_ai1 if i < 32 else t_ai2

        def em(e, b=buf, a=acc):
            e.tensor_reduce(C["TS0"][:], b[:], AX.X, ALU.add, apply_absolute_value=True)
            return e.tensor_tensor(a[:], a[:], C["TS0"][:], ALU.add)
        tk = p.t("vector", em, [ld, dep0])
        prev_buf[i % 2] = tk
        if i == 31:
            red31 = tk
        if i == 63:
            red63 = tk

    def em_tot(e):
        e.matmul(psd[0:1, 0:1], ONESC[:, 0:1], C["SUMX"][:, 0:1], start=True, stop=True)
        return e.matmul(psd[0:1, 1:2], ONESC[:, 0:1], C["SUMQ"][:, 0:1],
                        start=True, stop=True, skip_group_check=True)
    t_tot = p.t("tensor", em_tot, [red31, red63, t_ones_c])
    t_mean = p.t("vector", lambda e: e.tensor_scalar(
        ROWT[0:1, 0:2], psd[0:1, 0:2], 1.0 / (DH * D), 1e-5, ALU.mult, ALU.max), [t_tot])
    t_bc = p.t("tensor", lambda e: e.matmul(
        psd[:, 2:4], ONESR[0:1, :], ROWT[0:1, 0:2], start=True, stop=True,
        skip_group_check=True), [t_mean, t_ones_r])

    def em_sw(e):
        e.tensor_copy(ISW1[:], psd[:, 2:3])
        e.tensor_copy(ISW2[:], psd[:, 3:4])
        e.reciprocal(SW1[:], ISW1[:])
        return e.reciprocal(SW2[:], ISW2[:])
    t_sw = p.t("vector", em_sw, [t_bc])

    with nc.sbuf_tensor("P0EV", [128, 1024], FP16) as P0EV:
        prev_buf = [t_sw, t_sw]
        t_ci1 = mset("vector", CORR1[:], 0.0, [t_sw])
        t_ci2 = mset("vector", CORR2[:], 0.0, [t_sw])
        p0_tail = []
        prev_ev = t_idh
        t_cs_last = None
        for i in range(64):
            buf = X0 if i % 2 == 0 else X1
            if i < 32:
                src = w1_i[i * 128:(i + 1) * 128, :]
                sw = SW1
            else:
                r2, cc2 = (i - 32) // 4, (i - 32) % 4
                src = w2_i[128 * r2:128 * (r2 + 1), 1024 * cc2:1024 * (cc2 + 1)]
                sw = SW2
            ld = p.dma("sync", lambda e, b=buf, s=src: e.dma_start(out=b[:], in_=s),
                       [prev_buf[i % 2]])
            t_t = p.t("scalar", lambda e, b=buf, s=sw: e.mul(TT_[:], b[:], s[:, 0:1]),
                      [ld, t_sw])
            prev_buf[i % 2] = t_t

            def em_q(e):
                e.tensor_scalar(EE_[:], TT_[:], 128.0, None, ALU.add)
                e.tensor_copy(XQI[:], EE_[:])
                e.tensor_tensor(EE_[:], EE_[:], XQI[:], ALU.subtract)
                e.scalar_tensor_tensor(TT_[:], EE_[:], 0.5, XQI[:], ALU.is_ge, ALU.add)
                e.tensor_scalar(TT_[:], TT_[:], 129.0, 127.0, ALU.min, ALU.max)
                return e.tensor_scalar(XQH[:], TT_[:], 128.0, None, ALU.subtract)
            t_q = p.t("vector", em_q, [t_t])
            if i < 32:
                def em_cs(e, col=i):
                    e.tensor_reduce(C["TS0"][:], XQH[:], AX.X, ALU.add)
                    return e.tensor_tensor(CORR1[:, col:col + 1], CORR1[:, col:col + 1],
                                           C["TS0"][:], ALU.add)
                t_cs = p.t("vector", em_cs, [t_q, t_ci1])
            else:
                r2 = (i - 32) // 4

                def em_cs(e, col=r2):
                    e.tensor_reduce(C["TS0"][:], XQH[:], AX.X, ALU.add)
                    return e.tensor_tensor(CORR2[:, col:col + 1], CORR2[:, col:col + 1],
                                           C["TS0"][:], ALU.add)
                t_cs = p.t("vector", em_cs, [t_q, t_ci2])
            t_cs_last = t_cs

            def em_tr(e):
                for j in range(8):
                    r = e.transpose(psth[:, j * 128:(j + 1) * 128],
                                    XQH[:, j * 128:(j + 1) * 128], IDH[:])
                return r
            t_tr = p.t("tensor", em_tr, [t_q, t_idh, prev_ev])
            t_ev = p.t("vector", lambda e: e.tensor_copy(P0EV[:], psth[:, 0:1024]),
                       [t_tr])
            prev_ev = t_ev
            if i < 32:
                sp = lambda e, ii=i: e.dma_start(
                    out=AP(wq1T_d.tensor, ii * 128, [[DH, 128], [128 * DH, 8], [1, 128]]),
                    in_=AP(P0EV, 0, [[1024, 128], [128, 8], [1, 128]]))
            else:
                r2, cc2 = (i - 32) // 4, (i - 32) % 4
                sp = lambda e, r2=r2, cc2=cc2: e.dma_start(
                    out=AP(wq2T_d.tensor, (cc2 * 8) * 128 * D + r2 * 128,
                           [[D, 128], [128 * D, 8], [1, 128]]),
                    in_=AP(P0EV, 0, [[1024, 128], [128, 8], [1, 128]]))
            t_sp = p.dma("sync", sp, [t_ev])
            p0_tail.append(t_sp)

        t_corr = p.t("vector", lambda e: (
            e.tensor_scalar(CORR1[:], CORR1[:], 128.0, None, ALU.mult),
            e.tensor_scalar(CORR2[:], CORR2[:], 128.0, None, ALU.mult))[-1],
            [t_cs_last])

        for name in ("q", "k", "kp", "v", "o"):
            w_src = w_att[name]
            for i in range(8):
                buf = X0 if i % 2 == 0 else X1
                ld = p.dma("sync", lambda e, b=buf, s=w_src[i * 128:(i + 1) * 128, :]:
                           e.dma_start(out=b[:], in_=s), [prev_buf[i % 2]])
                t_cast = p.t("vector", lambda e, b=buf: e.tensor_copy(WTMP[:], b[:]),
                             [ld, prev_ev])
                prev_buf[i % 2] = t_cast

                def em_tr(e):
                    for j in range(8):
                        r = e.transpose(pstb[:, j * 128:(j + 1) * 128],
                                        WTMP[:, j * 128:(j + 1) * 128], IDB[:])
                    return r
                t_tr = p.t("tensor", em_tr, [t_cast, t_idb])
                t_ev2 = p.t("vector", lambda e: e.tensor_copy(WTMO[:], pstb[:, 0:1024]),
                            [t_tr])
                prev_ev = t_ev2
                t_sp = p.dma("sync", lambda e, n=name, ii=i: e.dma_start(
                    out=AP(wT_d[n].tensor, ii * 128, [[D, 128], [128 * D, 8], [1, 128]]),
                    in_=AP(WTMO, 0, [[1024, 128], [128, 8], [1, 128]])), [t_ev2])
                p0_tail.append(t_sp)
        p0_done = p0_tail + [t_corr]

    def ffn_phase(x_src, x_dst, own_only, entry_deps, tag=""):
        with contextlib.ExitStack() as fs:
            WQ1T = fs.enter_context(nc.sbuf_tensor("WQ1T" + tag, [128, 8, DH], FP16))
            WQ2T = fs.enter_context(nc.sbuf_tensor("WQ2T" + tag, [128, 32, D], FP16))
            XQT = fs.enter_context(nc.sbuf_tensor("XQT" + tag, [128, 8, 3, 130], FP16))
            YS = fs.enter_context(nc.sbuf_tensor("YS" + tag, [128, 32, 128], F32))
            XQ2 = fs.enter_context(nc.sbuf_tensor("XQ2" + tag, [128, 32, 128], FP16))
            OUT2 = fs.enter_context(nc.sbuf_tensor("OUT2" + tag, [128, D], F32))
            XN = fs.enter_context(nc.sbuf_tensor("XN" + tag, [128, D], F32))
            ALPH = fs.enter_context(nc.sbuf_tensor("ALPH" + tag, [128, 130], F32))
            S2B = fs.enter_context(nc.sbuf_tensor("S2B" + tag, [128, 128], F32))
            H_ = fs.enter_context(nc.sbuf_tensor("H_" + tag, [128, 130], F32))
            TC = fs.enter_context(nc.sbuf_tensor("TC" + tag, [128, 128], F32))
            T2 = fs.enter_context(nc.sbuf_tensor("T2" + tag, [128, 128], F32))
            YY = fs.enter_context(nc.sbuf_tensor("YY" + tag, [128, 128], F32))
            M128 = fs.enter_context(nc.sbuf_tensor("M128" + tag, [128, 128], F32))
            A2 = fs.enter_context(nc.sbuf_tensor("A2" + tag, [128, 128], F32))
            I2 = fs.enter_context(nc.sbuf_tensor("I2" + tag, [128, 128], I16))
            E2 = fs.enter_context(nc.sbuf_tensor("E2" + tag, [128, 128], F32))
            ARW = fs.enter_context(nc.sbuf_tensor("ARW" + tag, [1, 3 * 130], F32))

            ld_wq1 = p.dma("sync", lambda e: e.dma_start(
                out=AP(WQ1T, 0, [[8 * DH, 128], [DH, 8], [1, DH]]),
                in_=AP(wq1T_d.tensor, 0, [[DH, 128], [128 * DH, 8], [1, DH]])),
                entry_deps)
            ld_wq2 = p.dma("sync", lambda e: e.dma_start(
                out=AP(WQ2T, 0, [[32 * D, 128], [D, 32], [1, D]]),
                in_=AP(wq2T_d.tensor, 0, [[D, 128], [128 * D, 32], [1, D]])),
                entry_deps)
            t_xqt0 = mset("gpsimd", XQT[:], 0.0, entry_deps)
            t_arw = mset("gpsimd", ARW[:], 1.0, entry_deps)

            KSLOTS = [(psa, 0), (psa, 512), (psb, 0)]
            t_join = p.t("gpsimd", lambda e: e.engine_nop(), entry_deps)
            deps_x = [t_join, t_join]
            ring_task = [t_xqt0, t_xqt0, t_xqt0]
            tr_prev = [None]
            stat_prev = [None]
            prev_rows = [None]
            prev_tra = [None]
            mm_done_prev = [None]
            last_bcs_read = [None]
            st_tasks = []
            tasks_q = {}
            for j in range(NT + 1):
                if j < NT:
                    buf = X0 if j % 2 == 0 else X1
                    ld = p.dma("sync", lambda e, b=buf,
                               s=x_src[j * 128:(j + 1) * 128, :]:
                               e.dma_start(out=b[:], in_=s), [deps_x[j % 2]])
                    t_sq = p.t("scalar", lambda e, b=buf: e.activation(
                        TT_[:], b[:], AF.Square, accum_out=C["SUMQ"][:, 0:1]),
                        [ld, stat_prev[0]])

                    def em_st1(e, b=buf):
                        e.tensor_reduce(C["SUMX"][:], b[:], AX.X, ALU.add)
                        e.tensor_reduce(C["MXC"][:], b[:], AX.X, ALU.max)
                        e.tensor_reduce(C["MNC"][:], b[:], AX.X, ALU.min)
                        e.tensor_scalar(C["MU"][:], C["SUMX"][:], 1.0 / D, None, ALU.mult)
                        e.tensor_scalar(C["TS0"][:], C["SUMQ"][:], 1.0 / D, None, ALU.mult)
                        e.tensor_tensor(C["TS1"][:], C["MU"][:], C["MU"][:], ALU.mult)
                        e.tensor_tensor(C["TS0"][:], C["TS0"][:], C["TS1"][:], ALU.subtract)
                        return e.tensor_scalar(C["TS0"][:], C["TS0"][:], 1e-5, None, ALU.add)
                    t_st1 = p.t("vector", em_st1, [ld, t_sq])
                    t_sqr = p.t("scalar", lambda e: e.activation(
                        C["TS2"][:], C["TS0"][:], AF.Sqrt), [t_st1])

                    def em_st2(e):
                        e.reciprocal(C["RST"][:], C["TS2"][:])
                        for _ in range(2):
                            e.tensor_tensor(C["TS1"][:], C["RST"][:], C["RST"][:], ALU.mult)
                            e.tensor_tensor(C["TS1"][:], C["TS1"][:], C["TS0"][:], ALU.mult)
                            e.tensor_scalar(C["TS1"][:], C["TS1"][:], -0.5, 1.5,
                                            ALU.mult, ALU.add)
                            e.tensor_tensor(C["RST"][:], C["RST"][:], C["TS1"][:], ALU.mult)
                        e.tensor_tensor(C["TS1"][:], C["MXC"][:], C["MU"][:], ALU.subtract)
                        e.tensor_tensor(C["TS2"][:], C["MU"][:], C["MNC"][:], ALU.subtract)
                        e.tensor_tensor(C["TS1"][:], C["TS1"][:], C["TS2"][:], ALU.max)
                        e.tensor_tensor(C["TS1"][:], C["TS1"][:], C["RST"][:], ALU.mult)
                        e.tensor_scalar(C["TS1"][:], C["TS1"][:], 1e-5, None, ALU.max)
                        e.tensor_scalar(C["ALC"][:], C["TS1"][:], 1.0 / 127.0, None, ALU.mult)
                        e.tensor_tensor(C["ALC"][:], C["ALC"][:], ISW1[:], ALU.mult)
                        e.reciprocal(C["TS2"][:], C["TS1"][:])
                        e.tensor_scalar(C["TS2"][:], C["TS2"][:], 127.0, None, ALU.mult)
                        e.tensor_tensor(C["RSC"][:], C["RST"][:], C["TS2"][:], ALU.mult)
                        e.tensor_tensor(C["BCC"][:], C["MU"][:], C["RSC"][:], ALU.mult)
                        return e.tensor_scalar(C["BCC"][:], C["BCC"][:], -1.0, 128.0,
                                               ALU.mult, ALU.add)
                    t_st2 = p.t("vector", em_st2, [t_sqr, t_sw])
                    t_tq = p.t("scalar", lambda e, b=buf: e.activation(
                        TT_[:], b[:], AF.Identity, bias=C["BCC"][:, 0:1],
                        scale=C["RSC"][:, 0:1]), [t_st2])

                    def em_rnd(e):
                        e.tensor_copy(XQI[:], TT_[:])
                        e.tensor_tensor(EE_[:], TT_[:], XQI[:], ALU.subtract)
                        return e.scalar_tensor_tensor(XQH[:], EE_[:], 0.5, XQI[:],
                                                      ALU.is_ge, ALU.add)
                    t_rnd = p.t("vector", em_rnd, [t_tq])
                    stat_prev[0] = t_rnd

                    def em_trq(e):
                        for jj in range(8):
                            r = e.transpose(psth[:, jj * 128:(jj + 1) * 128],
                                            XQH[:, jj * 128:(jj + 1) * 128], IDH[:])
                        return r
                    t_trq = p.t("tensor", em_trq, [t_rnd, t_idh, tr_prev[0]])

                    def em_evq(e, slot=j % 3, sl=(j - 1) % 3, sr=(j + 1) % 3):
                        e.tensor_copy(AP(XQT, slot * 130 + 1,
                                         [[8 * 3 * 130, 128], [3 * 130, 8], [1, 128]]),
                                      psth3)
                        e.tensor_copy(AP(XQT, sl * 130 + 129,
                                         [[8 * 3 * 130, 128], [3 * 130, 8], [1, 1]]),
                                      psth3[:, :, 0:1])
                        return e.tensor_copy(
                            AP(XQT, sr * 130 + 0,
                               [[8 * 3 * 130, 128], [3 * 130, 8], [1, 1]]),
                            psth3[:, :, 127:128])
                    t_evq = p.t("vector", em_evq, [t_trq, t_xqt0])
                    ring_task[j % 3] = t_evq
                    tr_prev[0] = t_evq
                    t_tra = p.t("tensor", lambda e: e.transpose(
                        psc[0:1, 0:128], C["ALC"][:, 0:1], IDF[:]),
                        [t_st2, ld_id, prev_rows[0]])
                    prev_tra[0] = t_tra

                    def em_eva(e, slot=j % 3, sl=(j - 1) % 3, sr=(j + 1) % 3):
                        e.tensor_copy(ARW[0:1, slot * 130 + 1:slot * 130 + 129],
                                      psc[0:1, 0:128])
                        e.tensor_copy(ARW[0:1, sl * 130 + 129:sl * 130 + 130],
                                      psc[0:1, 0:1])
                        return e.tensor_copy(ARW[0:1, sr * 130:sr * 130 + 1],
                                             psc[0:1, 127:128])
                    t_eva = p.t("vector", em_eva, [t_tra, t_arw])
                    tasks_q[j] = (t_evq, t_eva, ld, buf)

                i = j - 1
                if i < 0 or i >= NT:
                    continue
                buf_i = tasks_q[i][3]
                dep_ring = [ring_task[0], ring_task[1], ring_task[2]]
                deps_bca = [tasks_q[i][1]]
                if i + 1 in tasks_q:
                    deps_bca.append(tasks_q[i + 1][1])
                t_bca = p.t("tensor", lambda e, slot=i % 3: e.matmul(
                    psd[:, 0:130], ONESR[0:1, :], ARW[0:1, slot * 130:slot * 130 + 130],
                    start=True, stop=True, skip_group_check=True),
                    deps_bca + [last_bcs_read[0]])
                t_alph = p.t("vector", lambda e: e.tensor_copy(ALPH[:], psd[:, 0:130]),
                             [t_bca])

                t_ys_last = None
                t_h = None
                t_tp = None
                for cg in range(0, 32, 3):
                    ncg = min(3, 32 - cg)

                    def em_mm1(e, cg=cg, ncg=ncg, ii=i):
                        for ci in range(ncg):
                            c = cg + ci
                            pk, off = KSLOTS[ci]
                            for d8 in range(8):
                                r = e.matmul(
                                    pk[:, off:off + 130],
                                    AP(WQ1T, d8 * DH + c * 128,
                                       [[8 * DH, 128], [1, 128]]),
                                    AP(XQT, d8 * 3 * 130 + (ii % 3) * 130,
                                       [[8 * 3 * 130, 128], [1, 130]]),
                                    start=(d8 == 0), stop=(d8 == 7))
                        return r
                    mm1 = p.t("tensor", em_mm1, dep_ring + [ld_wq1, mm_done_prev[0]])
                    for ci in range(ncg):
                        c = cg + ci
                        pk, off = KSLOTS[ci]
                        t_h = p.t("vector", lambda e, pk=pk, off=off, c=c:
                                  e.scalar_tensor_tensor(H_[:], pk[:, off:off + 130],
                                                         CORR1[:, c:c + 1], ALPH[:],
                                                         ALU.subtract, ALU.mult),
                                  [mm1, t_alph, t_corr, t_tp])
                        t_tc = p.t("scalar", lambda e, c=c: e.activation(
                            TC[:], H_[:, 1:129], AF.Identity, bias=WCV[:, c, 3:4],
                            scale=WCV[:, c, 1:2]), [t_h, ld_cw, ld_cb])

                        def em_taps(e, c=c):
                            e.scalar_tensor_tensor(T2[:], H_[:, 0:128], WCV[:, c, 0:1],
                                                   TC[:], ALU.mult, ALU.add)
                            return e.scalar_tensor_tensor(YY[:], H_[:, 2:130],
                                                          WCV[:, c, 2:3], T2[:],
                                                          ALU.mult, ALU.add)
                        t_tp = p.t("vector", em_taps, [t_tc])
                        t_ys = p.t("scalar", lambda e, c=c: e.activation(
                            YS[:, c, :], YY[:], AF.Silu), [t_tp])
                        t_ys_last = t_ys
                    mm_done_prev[0] = t_h

                t_m = p.t("vector", lambda e: e.tensor_reduce(
                    M128[:], AP(YS, 0, [[32 * 128, 128], [1, 128], [128, 32]]),
                    AX.X, ALU.max, apply_absolute_value=True), [t_ys_last])
                t_trm = p.t("tensor", lambda e: e.transpose(
                    pst2[:, 0:128], M128[:], IDF[:]), [t_m, ld_id, mm_done_prev[0]])

                def em_s2(e):
                    e.tensor_reduce(C["TS0"][:], pst2[:, 0:128], AX.X, ALU.max)
                    e.tensor_scalar(C["TS0"][:], C["TS0"][:], 1e-5, None, ALU.max)
                    e.reciprocal(C["TS1"][:], C["TS0"][:])
                    e.tensor_scalar(C["TS1"][:], C["TS1"][:], 127.0, None, ALU.mult)
                    e.tensor_scalar(C["BE2"][:], C["TS0"][:], 0.5 / 127.0, None, ALU.mult)
                    return e.tensor_tensor(C["BE2"][:], C["BE2"][:], ISW2[:], ALU.mult)
                t_s2 = p.t("vector", em_s2, [t_trm, t_sw])
                t_trs = p.t("tensor", lambda e: e.transpose(
                    psc[0:1, 128:256], C["TS1"][:, 0:1], IDF[:]),
                    [t_s2, ld_id, prev_tra[0]])
                t_rows = p.t("vector", lambda e: e.tensor_copy(
                    ROWT[0:1, 0:128], psc[0:1, 128:256]), [t_trs])
                prev_rows[0] = t_rows
                t_bcs = p.t("tensor", lambda e: e.matmul(
                    psd[:, 256:384], ONESR[0:1, :], ROWT[0:1, 0:128],
                    start=True, stop=True, skip_group_check=True),
                    [t_rows, t_alph])
                t_s2b = p.t("vector", lambda e: e.tensor_copy(S2B[:], psd[:, 256:384]),
                            [t_bcs])
                last_bcs_read[0] = t_s2b

                t_q2_last = t_s2b
                for c in range(32):
                    def em_q2(e, c=c):
                        e.tensor_tensor(A2[:], YS[:, c, :], S2B[:], ALU.mult)
                        e.tensor_scalar(I2[:], A2[:], 128.0, None, ALU.add)
                        e.scalar_tensor_tensor(E2[:], A2[:], 128.0, I2[:],
                                               ALU.add, ALU.subtract)
                        return e.scalar_tensor_tensor(XQ2[:, c, :], E2[:], 0.5, I2[:],
                                                      ALU.is_ge, ALU.add)
                    t_q2_last = p.t("vector", em_q2, [t_q2_last])

                def em_mm2(e):
                    for bank in range(2):
                        first = True
                        for d8 in range(bank * 4, bank * 4 + 4):
                            for c in range(32):
                                r = e.matmul(pst2[:, d8 * 128:(d8 + 1) * 128],
                                             AP(WQ2T, c * D + d8 * 128,
                                                [[32 * D, 128], [1, 128]]),
                                             XQ2[:, c, :],
                                             start=first,
                                             stop=(d8 == bank * 4 + 3 and c == 31),
                                             skip_group_check=True)
                                first = False
                    return r
                mm2 = p.t("tensor", em_mm2, [t_q2_last, ld_wq2, t_trm])

                def em_o2(e):
                    for d8 in range(8):
                        r = e.tensor_scalar(OUT2[:, d8 * 128:(d8 + 1) * 128],
                                            pst2[:, d8 * 128:(d8 + 1) * 128],
                                            CORR2[:, d8:d8 + 1], None, ALU.subtract)
                    return r
                t_o2 = p.t("vector", em_o2, [mm2, t_corr])

                def em_tro(e):
                    for jj in range(8):
                        r = e.transpose(pst2[:, jj * 128:(jj + 1) * 128],
                                        OUT2[:, jj * 128:(jj + 1) * 128], IDF[:])
                    return r
                t_tro = p.t("tensor", em_tro, [t_o2, ld_id])
                t_xn = p.t("vector", lambda e, b=buf_i: e.scalar_tensor_tensor(
                    XN[:], pst2[:, 0:1024], C["BE2"][:, 0:1], b[:], ALU.mult, ALU.add),
                    [t_tro, t_s2])
                mm_done_prev[0] = t_xn
                if (not own_only) or (OSUB0 <= i < OSUB0 + NOSUB):
                    r0 = (i - OSUB0) if own_only else i
                    dst = x_dst[r0 * 128:(r0 + 1) * 128, :]
                    st = p.dma("sync", lambda e, d=dst: e.dma_start(out=d, in_=XN[:]),
                               [t_xn])
                    st_tasks.append(st)
                    deps_x[i % 2] = st
                else:
                    deps_x[i % 2] = t_xn
            barrier = p.tails()
        return st_tasks, barrier

    def proj_phase(entry_deps):
        with contextlib.ExitStack() as fs:
            PW = fs.enter_context(nc.sbuf_tensor("PW", [128, 8, D], BF16))
            XHT = fs.enter_context(nc.sbuf_tensor("XHT", [128, 8, 128], BF16))
            QF = fs.enter_context(nc.sbuf_tensor("QF", [128, D], BF16))
            XHB = fs.enter_context(nc.sbuf_tensor("XHB", [128, D], BF16))
            t_join = p.t("gpsimd", lambda e: e.engine_nop(), entry_deps)
            deps_x = [t_join, t_join]
            stat_prev = [None]
            tr_prev = [t_join]
            sp_tasks = []
            for j in range(NT):
                buf = X0 if j % 2 == 0 else X1
                ld = p.dma("sync", lambda e, b=buf, s=xA_d[j * 128:(j + 1) * 128, :]:
                           e.dma_start(out=b[:], in_=s), [deps_x[j % 2]])
                t_sq = p.t("scalar", lambda e, b=buf: e.activation(
                    TT_[:], b[:], AF.Square, accum_out=C["SUMQ"][:, 0:1]),
                    [ld, stat_prev[0]])

                def em_st1(e, b=buf):
                    e.tensor_reduce(C["SUMX"][:], b[:], AX.X, ALU.add)
                    e.tensor_scalar(C["MU"][:], C["SUMX"][:], 1.0 / D, None, ALU.mult)
                    e.tensor_scalar(C["TS0"][:], C["SUMQ"][:], 1.0 / D, None, ALU.mult)
                    e.tensor_tensor(C["TS1"][:], C["MU"][:], C["MU"][:], ALU.mult)
                    e.tensor_tensor(C["TS0"][:], C["TS0"][:], C["TS1"][:], ALU.subtract)
                    return e.tensor_scalar(C["TS0"][:], C["TS0"][:], 1e-5, None, ALU.add)
                t_st1 = p.t("vector", em_st1, [ld, t_sq])
                t_sqr = p.t("scalar", lambda e: e.activation(
                    C["TS2"][:], C["TS0"][:], AF.Sqrt), [t_st1])

                def em_st2(e):
                    e.reciprocal(C["RST"][:], C["TS2"][:])
                    for _ in range(2):
                        e.tensor_tensor(C["TS1"][:], C["RST"][:], C["RST"][:], ALU.mult)
                        e.tensor_tensor(C["TS1"][:], C["TS1"][:], C["TS0"][:], ALU.mult)
                        e.tensor_scalar(C["TS1"][:], C["TS1"][:], -0.5, 1.5,
                                        ALU.mult, ALU.add)
                        e.tensor_tensor(C["RST"][:], C["RST"][:], C["TS1"][:], ALU.mult)
                    e.tensor_tensor(C["BCC"][:], C["MU"][:], C["RST"][:], ALU.mult)
                    return e.tensor_scalar(C["BCC"][:], C["BCC"][:], -1.0, None, ALU.mult)
                t_st2 = p.t("vector", em_st2, [t_sqr])
                stat_prev[0] = t_st2
                t_ap = p.t("scalar", lambda e, b=buf: e.activation(
                    EE_[:], b[:], AF.Identity, bias=C["BCC"][:, 0:1],
                    scale=C["RST"][:, 0:1]), [t_st2])
                t_cb = p.t("vector", lambda e: e.tensor_copy(XHB[:], EE_[:]), [t_ap])
                st1 = p.dma("sync", lambda e, jj=j: e.dma_start(
                    out=xh2_d[jj * 128:(jj + 1) * 128, :], in_=EE_[:]), [t_cb])
                deps_x[j % 2] = st1

                def em_tr(e):
                    for jj in range(8):
                        r = e.transpose(pstb[:, jj * 128:(jj + 1) * 128],
                                        XHB[:, jj * 128:(jj + 1) * 128], IDB[:])
                    return r
                t_tr = p.t("tensor", em_tr, [t_cb, t_idb, tr_prev[0]])
                t_ev = p.t("vector", lambda e: e.tensor_copy(QF[:], pstb[:, 0:1024]),
                           [t_tr])
                tr_prev[0] = t_ev
                st2 = p.dma("sync", lambda e, jj=j: e.dma_start(
                    out=AP(xh2T_d.tensor, jj * 128, [[T_EXT, 128], [128 * T_EXT, 8],
                                                     [1, 128]]),
                    in_=AP(QF, 0, [[1024, 128], [128, 8], [1, 128]])), [t_ev])
                sp_tasks.append(st1)
                sp_tasks.append(st2)

            def zero_halo(dst_t, width):
                tk = []
                for q8 in range(8):
                    for side in (0, width - 32):
                        tk.append(p.dma("gpsimd", lambda e, q8=q8, s=side, d=dst_t:
                                        e.dma_start(out=AP(d.tensor, q8 * 128 * width + s,
                                                           [[width, 128], [1, 32]]),
                                                    in_=ZEROB[:]), [t_zerob]))
                return tk
            zh = []
            for n in ("q", "k", "kp", "v"):
                zh += zero_halo(spl[n], SPW)

            for n in ("q", "k", "kp", "v"):
                ldw = p.dma("sync", lambda e, n=n: e.dma_start(
                    out=AP(PW, 0, [[8 * D, 128], [D, 8], [1, D]]),
                    in_=AP(wT_d[n].tensor, 0, [[D, 128], [128 * D, 8], [1, D]])),
                    sp_tasks + entry_deps)
                prev_mm = [None]
                for j in range(NT):
                    ldx = p.dma("sync", lambda e, jj=j: e.dma_start(
                        out=AP(XHT, 0, [[8 * 128, 128], [128, 8], [1, 128]]),
                        in_=AP(xh2T_d.tensor, jj * 128, [[T_EXT, 128],
                                                         [128 * T_EXT, 8], [1, 128]])),
                        [sp_tasks[2 * j + 1], prev_mm[0]])

                    def em_mm(e):
                        for do8 in range(8):
                            for di in range(8):
                                r = e.matmul(pst2[:, do8 * 128:(do8 + 1) * 128],
                                             PW[:, di, do8 * 128:(do8 + 1) * 128],
                                             XHT[:, di, :],
                                             start=(di == 0), stop=(di == 7))
                        return r
                    mm = p.t("tensor", em_mm, [ldx, ldw, prev_mm[0]])
                    scl = 0.125 if n == "q" else 1.0
                    t_ev = p.t("vector", lambda e, s=scl: e.tensor_scalar(
                        QF[:], pst2[:, 0:1024], s, None, ALU.mult), [mm])
                    prev_mm[0] = t_ev
                    st = p.dma("sync", lambda e, n=n, jj=j: e.dma_start(
                        out=AP(spl[n].tensor, 32 + jj * 128,
                               [[SPW, 128], [128 * SPW, 8], [1, 128]]),
                        in_=AP(QF, 0, [[1024, 128], [128, 8], [1, 128]])), [t_ev])
                    sp_tasks.append(st)
            barrier = p.tails()
        return sp_tasks, barrier, zh

    def attn_phase(entry_deps):
        with contextlib.ExitStack() as fs:
            KB = fs.enter_context(nc.sbuf_tensor("KB", [128, ACH + 32], BF16))
            KPB = fs.enter_context(nc.sbuf_tensor("KPB", [128, ACH + 64], BF16))
            VB = fs.enter_context(nc.sbuf_tensor("VB", [128, ACH + 64], BF16))
            QBD = fs.enter_context(nc.sbuf_tensor("QBD", [128, 2 * ACH], BF16))
            KKB = fs.enter_context(nc.sbuf_tensor("KKB", [128, 31, ACH + 32], BF16))
            SC1 = fs.enter_context(nc.sbuf_tensor("SC1", [128, 992], F32))
            MSK0 = fs.enter_context(nc.sbuf_tensor("MSK0", [128, 992], F32))
            MSK1 = fs.enter_context(nc.sbuf_tensor("MSK1", [128, 992], F32))
            PSM = fs.enter_context(nc.sbuf_tensor("PSM", [128, 512], BF16))
            DBF = fs.enter_context(nc.sbuf_tensor("DBF", [128, 992], BF16)) \
                if cfg.get("dbg") else None
            EB = fs.enter_context(nc.sbuf_tensor("EB", [128, 992], BF16))
            PB_ = fs.enter_context(nc.sbuf_tensor("PB_", [128, 1504], BF16))
            PSR = fs.enter_context(nc.sbuf_tensor("PSR", [128, 512], BF16))
            PSTS = fs.enter_context(nc.sbuf_tensor("PSTS",
                                                   [128, ACH // 64, 4, 128], BF16))
            UF = fs.enter_context(nc.sbuf_tensor("UF", [128, 16, ACH + 32], BF16))
            UT = fs.enter_context(nc.sbuf_tensor("UT",
                                                 [128, (ACH + 32) // 8, 128], BF16))
            ZB = fs.enter_context(nc.sbuf_tensor("ZB", [128, 8, ACH], BF16))
            WOT = fs.enter_context(nc.sbuf_tensor("WOT", [128, 8, D], BF16))
            WOU = fs.enter_context(nc.sbuf_tensor("WOU", [128, 8, ACH], F32))
            XH2 = fs.enter_context(nc.sbuf_tensor("XH2", [128, D], F32))
            WS = fs.enter_context(nc.sbuf_tensor("WS", [128, D], F32))
            VV_ = fs.enter_context(nc.sbuf_tensor("VV_", [128, D], F32))
            NMX = fs.enter_context(nc.sbuf_tensor("NMX", [128, 1], F32))
            SUS = fs.enter_context(nc.sbuf_tensor("SUS", [128, 1], F32))
            RSS = fs.enter_context(nc.sbuf_tensor("RSS", [128, 1], F32))

            ld_m0 = p.dma("sync", lambda e: e.dma_start(out=MSK0[:], in_=mask_i[0]),
                          entry_deps)
            ld_m1 = p.dma("sync", lambda e: e.dma_start(out=MSK1[:], in_=mask_i[1]),
                          entry_deps)
            ld_pmf = p.dma("sync", lambda e: e.dma_start(out=SC1[:, 0:512],
                                                         in_=psmask_i[:, :]), entry_deps)
            t_pm = p.t("vector", lambda e: e.tensor_copy(PSM[:], SC1[:, 0:512]),
                       [ld_pmf])
            ld_wo = p.dma("sync", lambda e: e.dma_start(
                out=AP(WOT, 0, [[8 * D, 128], [D, 8], [1, D]]),
                in_=AP(wT_d["o"].tensor, 0, [[D, 128], [128 * D, 8], [1, D]])),
                entry_deps)
            t_qbd0 = mset("gpsimd", QBD[:], 0.0, entry_deps)
            t_pb0 = mset("gpsimd", PB_[:, 0:512], 0.0, entry_deps)

            st_tasks = []
            prev_chunk_tail = [None]
            t_zevac = [None, None]
            prev_kk = [None]
            prev_amm = [None]
            prev_utt = [None]
            prev_z = [None]
            for (c0, CL) in CHUNKS:
                NG = CL // 64
                NBLK = CL // 16
                NUTT = (CL + 32) // 8
                q_score_done = []
                t_zb = {}
                for q in range(8):
                    ld_k = p.dma("sync", lambda e, q=q, c0=c0, CL=CL: e.dma_start(
                        out=KB[:, 0:CL + 32],
                        in_=spl["k"][q][:, 16 + c0:16 + c0 + CL + 32]),
                        [prev_chunk_tail[0], prev_kk[0]])
                    ld_kp = p.dma("sync", lambda e, q=q, c0=c0, CL=CL: e.dma_start(
                        out=KPB[:, 0:CL + 64], in_=spl["kp"][q][:, c0:c0 + CL + 64]),
                        [prev_chunk_tail[0], prev_kk[0]])
                    ld_q1 = p.dma("sync", lambda e, q=q, c0=c0, CL=CL: e.dma_start(
                        out=AP(QBD, 0, [[2 * ACH, 64], [32, CL // 16], [1, 16]]),
                        in_=AP(spl["q"].tensor, q * 128 * SPW + 32 + c0,
                               [[SPW, 64], [16, CL // 16], [1, 16]])),
                        [t_qbd0, prev_chunk_tail[0], prev_amm[0]])
                    ld_q2 = p.dma("sync", lambda e, q=q, c0=c0, CL=CL: e.dma_start(
                        out=AP(QBD, 64 * 2 * ACH + 16,
                               [[2 * ACH, 64], [32, CL // 16], [1, 16]]),
                        in_=AP(spl["q"].tensor, q * 128 * SPW + 64 * SPW + 32 + c0,
                               [[SPW, 64], [16, CL // 16], [1, 16]])),
                        [t_qbd0, prev_chunk_tail[0], prev_amm[0]])

                    def em_kk(e, CL=CL):
                        return e.tensor_tensor(
                            KKB[:, :, 0:CL + 32].opt(),
                            AP(KB, 0, [[ACH + 32, 128], [0, 31], [1, CL + 32]]),
                            AP(KPB, 31, [[ACH + 64, 128], [-1, 31], [1, CL + 32]]),
                            ALU.mult)
                    t_kk = p.t("vector", em_kk, [ld_k, ld_kp, prev_amm[0]])
                    prev_kk[0] = t_kk

                    grp_tasks = []
                    prev_soft = [prev_z[0]]
                    for grp in range(NG):
                        def em_amm(e, grp=grp, q=q, CL=CL):
                            for b4 in range(4):
                                blk = grp * 4 + b4
                                lhs = AP(QBD, blk * 32, [[2 * ACH, 128], [1, 32]])
                                rhs1 = AP(KKB, 16 * blk,
                                          [[31 * (ACH + 32), 128], [ACH + 32, 16],
                                           [1, 32]])
                                rhs2 = AP(KKB, 16 * (ACH + 32) + 16 * blk,
                                          [[31 * (ACH + 32), 128], [ACH + 32, 15],
                                           [1, 32]])
                                e.matmul(psa[b4 * 32:b4 * 32 + 32, 0:512], lhs, rhs1,
                                         start=True, stop=True,
                                         tile_position=(0, b4 * 32),
                                         skip_group_check=True)
                                r = e.matmul(psa[b4 * 32:b4 * 32 + 32, 512:992], lhs,
                                             rhs2, start=True, stop=True,
                                             tile_position=(0, b4 * 32),
                                             skip_group_check=True)
                            return r
                        t_amm = p.t("tensor", em_amm,
                                    [t_kk, ld_q1, ld_q2, prev_soft[0]])
                        prev_amm[0] = t_amm
                        gidx = (c0 + grp * 64) // 64
                        msk = MSK0 if gidx == mg_first else MSK1

                        def em_soft1(e, msk=msk):
                            e.tensor_tensor(SC1[:], psa[:, 0:992], msk[:], ALU.add)
                            e.reduce_max(NMX[:, 0:1], SC1[:], AX.X)
                            return e.tensor_scalar(NMX[:], NMX[:], -1.0, None, ALU.mult)
                        t_sf1 = p.t("vector", em_soft1, [t_amm, ld_m0, ld_m1, t_pm])
                        prev_soft[0] = t_sf1
                        t_exp = p.t("scalar", lambda e: e.activation(
                            EB[:], SC1[:], AF.Exp, bias=NMX[:, 0:1],
                            accum_out=SUS[:, 0:1]), [t_sf1])

                        def em_nrm(e):
                            e.reciprocal(RSS[:], SUS[:])
                            return e.tensor_scalar(PB_[:, 512:1504], EB[:],
                                                   RSS[:, 0:1], None, ALU.mult)
                        t_nrm = p.t("vector", em_nrm, [t_exp, t_pb0])

                        def em_fold(e):
                            e.tensor_copy(AP(PSR, 0, [[512, 128], [16, 32], [1, 1]]),
                                          AP(PB_, 512 + 480, [[1504, 128], [1, 32], [32, 1]]))
                            e.tensor_tensor(
                                AP(PSR, 1, [[512, 128], [16, 32], [1, 15]]),
                                AP(PB_, 512 + 512, [[1504, 128], [1, 32], [32, 15]]),
                                AP(PB_, 512 + 447, [[1504, 128], [1, 32], [-33, 15]]),
                                ALU.add)
                            return e.tensor_tensor(PSR[:], PSR[:], PSM[:], ALU.mult)
                        t_fold = p.t("vector", em_fold, [t_nrm, t_pm])

                        def em_pst(e):
                            for qh in range(4):
                                r = e.transpose(pstb[:, qh * 128:(qh + 1) * 128],
                                                PSR[:, qh * 128:(qh + 1) * 128],
                                                IDB[:])
                            return r
                        t_pst = p.t("tensor", em_pst, [t_fold, t_idb])
                        t_pse = p.t("vector", lambda e, grp=grp: e.tensor_copy(
                            AP(PSTS, grp * 4 * 128,
                               [[(ACH // 64) * 4 * 128, 128], [1, 512]]),
                            pstb[:, 0:512]), [t_pst])
                        grp_tasks.append(t_pse)
                        if cfg.get("dbg") and c0 == 0 and q == 0 and grp == 2:
                            dbf = p.t("vector", lambda e: e.tensor_copy(
                                DBF[:], SC1[:]), [t_sf1, t_exp])
                            p.dma("sync", lambda e: e.dma_start(
                                out=dbg_d[0][:, 1024:2016], in_=DBF[:]), [dbf])
                            p.dma("sync", lambda e: e.dma_start(
                                out=dbg_d[0][:, 0:992], in_=EB[:]), [t_exp, t_nrm])
                            p.dma("sync", lambda e: e.dma_start(
                                out=dbg_d[1][:, 0:1504], in_=PB_[:]),
                                [t_nrm, t_fold])
                            p.dma("sync", lambda e: e.dma_start(
                                out=dbg_d[1][:, 2048:2560], in_=PSM[:]), [t_pm])
                            p.dma("sync", lambda e: e.dma_start(
                                out=dbg_d[1][:, 3000:3512], in_=PSR[:]),
                                [t_fold, t_pst])

                    if cfg.get("dbg") and c0 == 0 and q == 0:
                        p.dma("sync", lambda e: e.dma_start(
                            out=dbg_d[2][:, 0:4096],
                            in_=AP(PSTS, 0, [[(ACH // 64) * 4 * 128, 128],
                                             [1, 4096]])), grp_tasks[-1:])
                        p.dma("sync", lambda e: e.dma_start(
                            out=dbg_d[5][:, 0:2 * ACH],
                            in_=AP(QBD, 0, [[2 * ACH, 128], [1, 2 * ACH]])),
                            [ld_q1, ld_q2])
                        p.dma("sync", lambda e: e.dma_start(
                            out=dbg_d[6][:, 0:544],
                            in_=AP(KKB, 15 * (ACH + 32),
                                   [[31 * (ACH + 32), 128], [1, 544]])), [t_kk])
                        p.dma("sync", lambda e: e.dma_start(
                            out=dbg_d[6][:, 1024:1024 + ACH + 32],
                            in_=KB[:, 0:ACH + 32]), [t_kk])
                        p.dma("sync", lambda e: e.dma_start(
                            out=dbg_d[6][:, 2048:2048 + ACH + 64],
                            in_=KPB[:, 0:ACH + 64]), [t_kk])

                    ld_v = p.dma("sync", lambda e, q=q, c0=c0, CL=CL: e.dma_start(
                        out=VB[:, 0:CL + 64], in_=spl["v"][q][:, c0:c0 + CL + 64]),
                        [prev_chunk_tail[0], prev_utt[0]])

                    def em_uf(e, CL=CL):
                        return e.tensor_tensor(
                            AP(UF, 0, [[16 * (ACH + 32), 128], [1, 16], [16, CL + 32]]),
                            AP(VB, 16, [[ACH + 64, 128], [0, 16], [1, CL + 32]]),
                            AP(VB, 16, [[ACH + 64, 128], [-1, 16], [1, CL + 32]]),
                            ALU.mult)
                    t_uf = p.t("vector", em_uf, [ld_v, prev_utt[0]])
                    ut_ev = []
                    for kt0 in range(0, NUTT, 8):
                        nk = min(8, NUTT - kt0)

                        def em_utt(e, kt0=kt0, nk=nk, CL=CL):
                            for kk_ in range(nk):
                                r = e.transpose(
                                    pstb[:, kk_ * 128:(kk_ + 1) * 128],
                                    AP(UF, (kt0 + kk_) * 128,
                                       [[16 * (ACH + 32), 128], [1, 128]]),
                                    IDB[:])
                            return r
                        t_utt = p.t("tensor", em_utt,
                                    [t_uf, t_idb, prev_z[0]] +
                                    ([ut_ev[-1]] if ut_ev else grp_tasks[-1:]))
                        t_ute = p.t("vector", lambda e, kt0=kt0, nk=nk: e.tensor_copy(
                            AP(UT, kt0 * 128,
                            [[((ACH + 32) // 8) * 128, 128], [1, nk * 128]]),
                            pstb[:, 0:nk * 128]), [t_utt])
                        ut_ev.append(t_ute)
                    prev_utt[0] = ut_ev[-1]
                    if cfg.get("dbg") and c0 == 0 and q == 0:
                        p.dma("sync", lambda e: e.dma_start(
                            out=dbg_d[3][:, 0:4096],
                            in_=AP(UT, 0, [[((ACH + 32) // 8) * 128, 128],
                                           [1, 4096]])), [ut_ev[-1]])
                        p.dma("sync", lambda e: e.dma_start(
                            out=dbg_d[4][:, 0:NUTT * 128 - 4096],
                            in_=AP(UT, 4096, [[((ACH + 32) // 8) * 128, 128],
                                              [1, NUTT * 128 - 4096]])),
                            [ut_ev[-1]])
                        p.dma("sync", lambda e: e.dma_start(
                            out=dbg_d[7][:, 1024:1024 + ACH + 64],
                            in_=VB[:, 0:ACH + 64]), [t_uf])

                    psz = psb if q % 2 == 0 else psc
                    def em_z(e, q=q, CL=CL, psz=psz, NBLK=NBLK):
                        for blk in range(NBLK):
                            grp, sub = blk // 4, blk % 4
                            for h in range(2):
                                for qh in range(4):
                                    r = e.matmul(
                                        psz[64 * h:64 * h + 64,
                                            16 * blk:16 * blk + 16],
                                        AP(UT, (2 * blk + qh) * 128 + 64 * h,
                                           [[((ACH + 32) // 8) * 128, 128],
                                            [1, 64]]),
                                        AP(PSTS, (grp * 4 + qh) * 128 + 32 * sub
                                           + 16 * h,
                                           [[(ACH // 64) * 4 * 128, 128],
                                            [1, 16]]),
                                        start=(blk == 0 and qh == 0),
                                        stop=(blk == NBLK - 1 and qh == 3),
                                        tile_position=(0, 64 * h),
                                        skip_group_check=True)
                        return r
                    t_z = p.t("tensor", em_z,
                              ut_ev + grp_tasks + [t_zevac[q % 2]])
                    if c0 == 0 and q == 0:
                        t_z.label = "t_z c0 q0"
                        for _i, _t in enumerate(ut_ev):
                            _t.label = f"ut_ev[{_i}] c0 q0" 
                    prev_z[0] = t_z
                    t_ze = p.t("vector", lambda e, q=q, psz=psz, CL=CL: e.tensor_copy(
                        ZB[:, q, 0:CL], psz[:, 0:CL]), [t_z])
                    t_zevac[q % 2] = t_ze
                    t_zb[q] = t_ze
                    q_score_done.append(t_ze)
                    if c0 == 0 and q == 0:
                        t_ze.label = "t_ze c0 q0"
                    if cfg.get("dbg") and c0 == 0 and q == 0:
                        p.dma("sync", lambda e, CL=CL: e.dma_start(
                            out=dbg_d[7][:, 0:CL], in_=ZB[:, 0, 0:CL]), [t_ze])

                wo_ev = []
                for pp in range(4):
                    def em_wo(e, pp=pp, CL=CL):
                        for dd in range(2):
                            d8 = pp * 2 + dd
                            for di in range(8):
                                r = e.matmul(pst2[:, dd * 512:dd * 512 + CL],
                                             WOT[:, di, d8 * 128:(d8 + 1) * 128],
                                             ZB[:, di, 0:CL],
                                             start=(di == 0), stop=(di == 7))
                        return r
                    mmw = p.t("tensor", em_wo,
                              list(t_zb.values()) + [ld_wo] + wo_ev[-1:])
                    t_we = p.t("vector", lambda e, pp=pp, CL=CL: (
                        e.tensor_copy(WOU[:, pp * 2, 0:CL], pst2[:, 0:CL]),
                        e.tensor_copy(WOU[:, pp * 2 + 1, 0:CL],
                                      pst2[:, 512:512 + CL]))[-1], [mmw])
                    wo_ev.append(t_we)

                prev_w = [None]
                prev_fin = [None, None]
                for si in range(CL // 128):
                    sub = (c0 // 128) + si
                    buf = X0 if sub % 2 == 0 else X1
                    ldx = p.dma("sync", lambda e, sub=sub, b=buf: e.dma_start(
                        out=b[:], in_=xA_d[sub * 128:(sub + 1) * 128, :]),
                        wo_ev + [prev_fin[sub % 2]])
                    ldh = p.dma("sync", lambda e, sub=sub: e.dma_start(
                        out=XH2[:], in_=xh2_d[sub * 128:(sub + 1) * 128, :]),
                        wo_ev + [prev_chunk_tail[0], prev_w[0]])

                    def em_trw(e, si=si, CL=CL):
                        for d8 in range(8):
                            r = e.transpose(pst2[:, d8 * 128:(d8 + 1) * 128],
                                            WOU[:, d8, si * 128:si * 128 + 128],
                                            IDF[:])
                        return r
                    t_trw = p.t("tensor", em_trw, wo_ev + [ld_id, prev_w[0]])
                    t_w = p.t("vector", lambda e: e.tensor_tensor(
                        WS[:], pst2[:, 0:1024], XH2[:], ALU.add), [t_trw, ldh])
                    prev_w[0] = t_w
                    t_sq = p.t("scalar", lambda e: e.activation(
                        TT_[:], WS[:], AF.Square, accum_out=C["SUMQ"][:, 0:1]), [t_w])

                    def em_st1(e):
                        e.tensor_reduce(C["SUMX"][:], WS[:], AX.X, ALU.add)
                        e.tensor_scalar(C["MU"][:], C["SUMX"][:], 1.0 / D, None,
                                        ALU.mult)
                        e.tensor_scalar(C["TS0"][:], C["SUMQ"][:], 1.0 / D, None,
                                        ALU.mult)
                        e.tensor_tensor(C["TS1"][:], C["MU"][:], C["MU"][:], ALU.mult)
                        e.tensor_tensor(C["TS0"][:], C["TS0"][:], C["TS1"][:],
                                        ALU.subtract)
                        return e.tensor_scalar(C["TS0"][:], C["TS0"][:], 1e-5, None,
                                               ALU.add)
                    t_st1 = p.t("vector", em_st1, [t_w, t_sq])
                    t_sqr = p.t("scalar", lambda e: e.activation(
                        C["TS2"][:], C["TS0"][:], AF.Sqrt), [t_st1])

                    def em_st2(e):
                        e.reciprocal(C["RST"][:], C["TS2"][:])
                        for _ in range(2):
                            e.tensor_tensor(C["TS1"][:], C["RST"][:], C["RST"][:],
                                            ALU.mult)
                            e.tensor_tensor(C["TS1"][:], C["TS1"][:], C["TS0"][:],
                                            ALU.mult)
                            e.tensor_scalar(C["TS1"][:], C["TS1"][:], -0.5, 1.5,
                                            ALU.mult, ALU.add)
                            e.tensor_tensor(C["RST"][:], C["RST"][:], C["TS1"][:],
                                            ALU.mult)
                        e.tensor_tensor(C["BCC"][:], C["MU"][:], C["RST"][:], ALU.mult)
                        return e.tensor_scalar(C["BCC"][:], C["BCC"][:], -1.0, None,
                                               ALU.mult)
                    t_st2 = p.t("vector", em_st2, [t_sqr])
                    t_v = p.t("scalar", lambda e: e.activation(
                        VV_[:], WS[:], AF.Identity, bias=C["BCC"][:, 0:1],
                        scale=C["RST"][:, 0:1]), [t_st2])

                    def em_fin(e, b=buf):
                        e.tensor_tensor(VV_[:], VV_[:], GB[:], ALU.mult)
                        e.tensor_tensor(VV_[:], VV_[:], BB[:], ALU.add)
                        return e.tensor_tensor(WS[:], b[:], VV_[:], ALU.add)
                    t_fin = p.t("vector", em_fin, [t_v, ld_gb, ld_bb, ldx])
                    st = p.dma("sync", lambda e, sub=sub: e.dma_start(
                        out=xB_d[sub * 128:(sub + 1) * 128, :], in_=WS[:]), [t_fin])
                    prev_fin[sub % 2] = st
                    st_tasks.append(st)
                prev_chunk_tail[0] = st_tasks[-1]
            barrier = p.tails()
        return st_tasks, barrier

    if dev_stop == "p0":
        bar = p.tails()
        d1 = p.dma("sync", lambda e: e.dma_start(
            out=AP(out.tensor, 0, [[D, 128], [1, 32]]), in_=CORR1[:]), bar)
        d2 = p.dma("sync", lambda e: e.dma_start(
            out=AP(out.tensor, 32, [[D, 128], [1, 8]]), in_=CORR2[:]), [d1])
        d3 = p.dma("sync", lambda e: e.dma_start(
            out=AP(out.tensor, 40, [[D, 128], [1, 1]]), in_=SW1[:],
            allow_slow_non_contiguous=True), [d2])
        d4 = p.dma("sync", lambda e: e.dma_start(
            out=AP(out.tensor, 41, [[D, 128], [1, 1]]), in_=ISW1[:],
            allow_slow_non_contiguous=True), [d3])
        d5 = p.dma("sync", lambda e: e.dma_start(
            out=AP(out.tensor, 42, [[D, 128], [1, 1]]), in_=SW2[:],
            allow_slow_non_contiguous=True), [d4])
        d6 = p.dma("sync", lambda e: e.dma_start(
            out=AP(out.tensor, 43, [[D, 128], [1, 1]]), in_=ISW2[:],
            allow_slow_non_contiguous=True), [d5])
        d7 = p.dma("sync", lambda e: e.dma_start(
            out=AP(out.tensor, 128 * D, [[D, 128], [1, D]]), in_=GB[:]), [d6])
        d8 = p.dma("sync", lambda e: e.dma_start(
            out=AP(out.tensor, 256 * D, [[D, 128], [1, D]]), in_=BB[:]), [d7])
        d9 = p.dma("sync", lambda e: e.dma_start(
            out=AP(out.tensor, 384 * D, [[D, 128], [1, 128]]),
            in_=AP(WCV, 0, [[128, 128], [1, 128]])), [d8])
        p.t("gpsimd", lambda e: e.engine_nop(), [d9])
        p.finalize()
        return nc
    def _dump_dram(src_ap, ncols, deps):
        th = src_ap.tensor
        rowlen = th.shape[2]
        DBG16x = nc.alloc_sbuf_tensor("DBG16x", [128, ncols], th.dtype)
        DBGFx = nc.alloc_sbuf_tensor("DBGFx", [128, ncols], F32)
        nsub = ncols // D
        prev = None
        for di in range(8):
            ld = p.dma("sync", lambda e, di=di: e.dma_start(
                out=DBG16x.ap()[:],
                in_=AP(th, di * 128 * rowlen, [[rowlen, 128], [1, ncols]])),
                deps + ([prev] if prev else []))
            cv = p.t("vector", lambda e: e.tensor_copy(DBGFx.ap()[:],
                                                       DBG16x.ap()[:]), [ld])
            prev = p.dma("sync", lambda e, di=di: e.dma_start(
                out=AP(out.tensor, di * 128 * nsub * D,
                       [[nsub * D, 128], [1, nsub * D]]),
                in_=DBGFx.ap()[:]), [cv])
        p.t("gpsimd", lambda e: e.engine_nop(), [prev])
        p.finalize()
        return nc

    def _dump_srcs():
        return dict(wq1T=wq1T_d, wq2T=wq2T_d, xh2T=xh2T_d,
                    **{f"w{n}T": wT_d[n] for n in wT_d},
                    **{f"spl{n}": spl[n] for n in spl})

    if dev_stop and dev_stop.startswith("dump0:"):
        nm, ncols = dev_stop.split(":")[1], int(dev_stop.split(":")[2])
        return _dump_dram(_dump_srcs()[nm], ncols, p.tails())
    if dev_stop in ("p0w", "p0v"):
        DBG16 = nc.alloc_sbuf_tensor("DBG16", [128, DH], FP16)
        DBGF = nc.alloc_sbuf_tensor("DBGF", [128, DH], F32)
        bar = p.tails()
        prev = None
        if dev_stop == "p0w":
            for di in range(8):
                ld = p.dma("sync", lambda e, di=di: e.dma_start(
                    out=DBG16[:], in_=wq1T_d[di]),
                    bar + ([prev] if prev else []))
                cv = p.t("vector", lambda e: e.tensor_copy(DBGF[:], DBG16[:]), [ld])
                prev = p.dma("sync", lambda e, di=di: e.dma_start(
                    out=AP(out.tensor, di * 512 * D,
                           [[4 * D, 128], [D, 4], [1, D]]),
                    in_=DBGF[:]), [cv])
        else:
            for s in range(32):
                ld = p.dma("sync", lambda e, s=s: e.dma_start(
                    out=DBG16[:, 0:D], in_=wq2T_d[s]),
                    bar + ([prev] if prev else []))
                cv = p.t("vector", lambda e: e.tensor_copy(DBGF[:, 0:D],
                                                           DBG16[:, 0:D]), [ld])
                prev = p.dma("sync", lambda e, s=s: e.dma_start(
                    out=AP(out.tensor, s * 128 * D, [[D, 128], [1, D]]),
                    in_=DBGF[:, 0:D]), [cv])
        p.t("gpsimd", lambda e: e.engine_nop(), [prev])
        p.finalize()
        return nc
    p1_st, bar1 = ffn_phase(x_in, xA_d, own_only=False, entry_deps=p0_done, tag="a")
    if dev_stop == "p1":
        fin = p.dma("sync", lambda e: e.dma_start(
            out=out[:, :], in_=xA_d[OWN0:OWN0 + OWN, :]), p1_st)
        p.t("gpsimd", lambda e: e.engine_nop(), [fin])
        p.finalize()
        return nc

    if dev_stop and dev_stop.startswith("dump1:"):
        nm, ncols = dev_stop.split(":")[1], int(dev_stop.split(":")[2])
        return _dump_dram(_dump_srcs()[nm], ncols, p.tails())
    p2_st, bar2, zh = proj_phase(bar1)
    if dev_stop and dev_stop.startswith("dump2:"):
        nm, ncols = dev_stop.split(":")[1], int(dev_stop.split(":")[2])
        return _dump_dram(_dump_srcs()[nm], ncols, p.tails())
    if dev_stop == "p2":
        fin = p.dma("sync", lambda e: e.dma_start(
            out=out[:, :], in_=xh2_d[OWN0:OWN0 + OWN, :]), p2_st)
        p.t("gpsimd", lambda e: e.engine_nop(), [fin])
        p.finalize()
        return nc

    p3_st, bar3 = attn_phase(bar2 + zh)
    if dev_stop and dev_stop.startswith("dumpA:"):
        nm, ncols = dev_stop.split(":")[1], int(dev_stop.split(":")[2])
        return _dump_dram(dbg_d, ncols, p.tails())
    if dev_stop == "p3":
        fin = p.dma("sync", lambda e: e.dma_start(
            out=out[:, :], in_=xB_d[OWN0:OWN0 + OWN, :]), p3_st)
        p.t("gpsimd", lambda e: e.engine_nop(), [fin])
        p.finalize()
        return nc

    p4_st, bar4 = ffn_phase(xB_d, out, own_only=True, entry_deps=bar3, tag="b")
    p.t("gpsimd", lambda e: e.engine_nop(), p4_st)
    p.finalize()
    return nc


T_EXT_FULL = 4352
OWN_FULL = 4096
OWN0_FULL = 128


def _build_masks():
    mask = np.zeros((2, 128, 992), np.float32)
    p = np.arange(128)
    blk4 = p // 32
    u = (p % 32) % 16
    gi = np.arange(31)
    mh = np.arange(32)
    g = gi - 15
    MH = mh[None, None, :]
    U = u[:, None, None]
    G = g[None, :, None]
    B4 = blk4[:, None, None]
    valid = ((U + 1 <= MH) & (MH <= U + 16) &
             (U + 1 <= MH - G) & (MH - G <= U + 16))
    kkey = 16 * B4 + MH - 16
    kpkey = kkey - G
    row0 = valid & (kkey >= 0) & (kpkey >= 0)
    mask[1] = np.where(valid, 0.0, -30000.0).reshape(128, 992)
    mask[0] = np.where(row0, 0.0, -30000.0).reshape(128, 992)
    psmask = np.zeros((128, 512), np.float32)
    mm2, gg2 = np.meshgrid(np.arange(32), np.arange(16), indexing="ij")
    psmask[:, :] = (mm2 >= gg2).astype(np.float32).reshape(1, 512)
    return mask, psmask


_NC_CACHE = {}


def _get_nc():
    key = "full"
    if key not in _NC_CACHE:
        cfg = dict(T_EXT=T_EXT_FULL, OWN0=OWN0_FULL, OWN=OWN_FULL,
                   maskgrp_first=OWN0_FULL // 64, dev_stop=None)
        _NC_CACHE[key] = (build(cfg), cfg)
    return _NC_CACHE[key]


def kernel(L0_attn_x, L0_attn_Wq, L0_attn_Wk, L0_attn_Wv, L0_attn_Wkp,
           L0_attn_Wo, L0_attn_gamma, L0_attn_beta, L0_ffn_w1, L0_ffn_conv_w,
           L0_ffn_conv_b, L0_ffn_w2):
    from concourse.bass_utils import run_bass_kernel_spmd
    x = np.ascontiguousarray(np.asarray(L0_attn_x, dtype=np.float32))
    B, S, Dm = x.shape
    assert (B, S, Dm) == (4, 8192, 1024)
    mask, psmask = _build_masks()
    mask_noseq = mask.copy()
    mask_noseq[0] = mask_noseq[1]
    ident = np.eye(128, dtype=np.float32)
    common = {
        "w_q": np.ascontiguousarray(L0_attn_Wq, dtype=np.float32),
        "w_k": np.ascontiguousarray(L0_attn_Wk, dtype=np.float32),
        "w_kp": np.ascontiguousarray(L0_attn_Wkp, dtype=np.float32),
        "w_v": np.ascontiguousarray(L0_attn_Wv, dtype=np.float32),
        "w_o": np.ascontiguousarray(L0_attn_Wo, dtype=np.float32),
        "gamma": np.ascontiguousarray(L0_attn_gamma, dtype=np.float32),
        "beta": np.ascontiguousarray(L0_attn_beta, dtype=np.float32),
        "w1": np.ascontiguousarray(L0_ffn_w1, dtype=np.float32),
        "conv_w": np.ascontiguousarray(
            np.asarray(L0_ffn_conv_w, dtype=np.float32).reshape(4096, 3)),
        "conv_b": np.ascontiguousarray(L0_ffn_conv_b, dtype=np.float32),
        "w2": np.ascontiguousarray(L0_ffn_w2, dtype=np.float32),
        "ident": ident,
        "psmask": psmask,
    }
    in_maps = []
    for core in range(8):
        b, half = core // 2, core % 2
        lo = half * 4096 - 128
        hi = lo + T_EXT_FULL
        xe = np.zeros((T_EXT_FULL, Dm), np.float32)
        s0, s1 = max(lo, 0), min(hi, S)
        xe[s0 - lo:s1 - lo] = x[b, s0:s1]
        m = dict(common)
        m["x_in"] = xe
        m["mask"] = mask if half == 0 else mask_noseq
        in_maps.append(m)
    nc, cfg = _get_nc()
    res = run_bass_kernel_spmd(nc, in_maps, core_ids=list(range(8)))
    outp = np.empty((4, 8192, 1024), np.float32)
    for core in range(8):
        b, half = core // 2, core % 2
        outp[b, half * 4096:(half + 1) * 4096] = np.asarray(
            res.results[core]["out"], dtype=np.float32)
    return outp


# version 10
# speedup vs baseline: 1.0370x; 1.0010x over previous
import contextlib
import numpy as np
import concourse.bass as bass
import concourse.mybir as mybir
from concourse.ap import AP

F32 = mybir.dt.float32
BF16 = mybir.dt.bfloat16
FP16 = mybir.dt.float16
I16 = mybir.dt.int16
AF = mybir.ActivationFunctionType
ALU = mybir.AluOpType
AX = mybir.AxisListType
ENGS = ("sync", "tensor", "vector", "scalar", "gpsimd")
D = 1024
DH = 4096


class Task:
    __slots__ = ("eng", "emit", "deps", "dma", "token", "do_inc", "label")

    def __init__(self, eng, emit, deps, dma):
        self.label = None
        self.eng = eng
        self.emit = emit
        self.deps = [d for d in deps if d is not None]
        self.dma = dma
        self.token = None
        self.do_inc = False


class _HazEng:
    _PASS = ("wait_ge", "dma_start", "engine_nop")

    def __init__(self, e, sem, state, dummy, has_memset, src):
        self._e = e
        self._sem = sem
        self._st = state
        self._dummy = dummy
        self._mset = has_memset
        self._src = src
        self.last_r = None
        self.last_short = False

    def __getattr__(self, name):
        real = getattr(self._e, name)
        if name in self._PASS or not callable(real):
            return real

        def wrap(*a, **k):
            r = real(*a, **k)
            short = True
            try:
                short = (a[0].free_size < 512) or (k.get("accum_out") is not None)
            except Exception:
                short = True
            if short:
                if self._mset:
                    self._e.tensor_reduce(self._dummy, self._src(),
                                          AX.X, ALU.add)
                else:
                    self._st[0] += 1
                    r.then_inc(self._sem, 1)
                    self._e.wait_ge(self._sem, self._st[0])
            self.last_r = r
            self.last_short = short
            return r
        return wrap


class Prog:
    def __init__(self, nc, serial=False):
        self.nc = nc
        self.per_eng = {e: [] for e in ENGS}
        self.esem = {e: nc.alloc_semaphore(f"se_{e}") for e in ENGS}
        self.dsem = {e: nc.alloc_semaphore(f"sd_{e}") for e in ENGS}
        self.hsem = {e: nc.alloc_semaphore(f"hz_{e}")
                     for e in ("vector", "scalar", "gpsimd")}
        self.dummy = nc.alloc_sbuf_tensor("hz_dummy", [128, 1],
                                          mybir.dt.float32).ap()
        self.spacer_src = None
        self.serial = serial
        self.all = []

    def _add(self, task):
        if self.serial and self.all:
            task.deps = task.deps + [self.all[-1]]
        self.all.append(task)
        self.per_eng[task.eng].append(task)
        return task

    def t(self, eng, emit, deps=()):
        return self._add(Task(eng, emit, deps, dma=False))

    def dma(self, eng, emit, deps=()):
        return self._add(Task(eng, emit, deps, dma=True))

    def tails(self):
        return [self.per_eng[e][-1] for e in ENGS if self.per_eng[e]]

    def finalize(self):
        needed = set()
        for e in ENGS:
            for task in self.per_eng[e]:
                for d in task.deps:
                    if d.dma or d.eng != task.eng:
                        needed.add(id(d))
        ccount = {e: 0 for e in ENGS}
        dcount = {e: 0 for e in ENGS}
        for e in ENGS:
            for task in self.per_eng[e]:
                if task.dma:
                    dcount[e] += 16
                    task.token = (self.dsem[e], dcount[e])
                    task.do_inc = True
                elif id(task) in needed:
                    ccount[e] += 1
                    task.token = (self.esem[e], ccount[e])
                    task.do_inc = True
        with self.nc.Block() as block:
            for ename in ENGS:
                tasks = self.per_eng[ename]
                if not tasks:
                    continue

                def body(eng, tasks=tasks, ename=ename):
                    maxw = {}
                    if ename in self.hsem:
                        heng = _HazEng(eng, self.hsem[ename], [0],
                                       self.dummy[0:128, 0:1],
                                       ename == "vector",
                                       lambda: self.spacer_src[0:128, 0:128])
                    else:
                        heng = eng
                    for task in tasks:
                        for d in task.deps:
                            if (not d.dma) and d.eng == ename:
                                continue
                            semh, val = d.token
                            key = id(semh)
                            if maxw.get(key, 0) >= val:
                                continue
                            eng.wait_ge(semh, val)
                            maxw[key] = val
                        last = task.emit(eng if task.dma else heng)
                        if task.do_inc and last is not None:
                            if (heng is not eng and heng.last_r is last
                                    and heng.last_short):
                                if ename == "gpsimd":
                                    last = eng.engine_nop()
                                elif ename == "scalar":
                                    last = eng.copy(self.dummy[0:1, 0:1],
                                                    self.dummy[0:1, 0:1])
                                else:
                                    last = eng.memset(self.dummy[0:1, 0:1], 0.0)
                            semh, _ = task.token
                            last.then_inc(semh, 16 if task.dma else 1)
                getattr(block, ename)(body)


def build(cfg):
    T_EXT = cfg["T_EXT"]
    NT = T_EXT // 128
    SPW = 32 + T_EXT + 32
    ACH = min(512, T_EXT)
    CHUNKS = []
    cc = 0
    while cc < T_EXT:
        CHUNKS.append((cc, min(ACH, T_EXT - cc)))
        cc += ACH
    OWN0 = cfg["OWN0"]
    OWN = cfg["OWN"]
    OSUB0 = OWN0 // 128
    NOSUB = OWN // 128
    mg_first = cfg["maskgrp_first"]
    dev_stop = cfg.get("dev_stop")

    nc = bass.Bass("TRN2", target_bir_lowering=False, detect_race_conditions=False)
    p = Prog(nc, serial=cfg.get("serial", False))
    stack = contextlib.ExitStack()

    x_in = nc.dram_tensor("x_in", [T_EXT, D], F32, kind="ExternalInput").ap()
    w_att = {n: nc.dram_tensor(f"w_{n}", [D, D], F32, kind="ExternalInput").ap()
             for n in ("q", "k", "kp", "v", "o")}
    gamma_i = nc.dram_tensor("gamma", [D], F32, kind="ExternalInput").ap()
    beta_i = nc.dram_tensor("beta", [D], F32, kind="ExternalInput").ap()
    w1_i = nc.dram_tensor("w1", [DH, D], F32, kind="ExternalInput").ap()
    cw_i = nc.dram_tensor("conv_w", [DH, 3], F32, kind="ExternalInput").ap()
    cb_i = nc.dram_tensor("conv_b", [DH], F32, kind="ExternalInput").ap()
    w2_i = nc.dram_tensor("w2", [D, DH], F32, kind="ExternalInput").ap()
    ident_i = nc.dram_tensor("ident", [128, 128], F32, kind="ExternalInput").ap()
    mask_i = nc.dram_tensor("mask", [2, 128, 992], F32, kind="ExternalInput").ap()
    psmask_i = nc.dram_tensor("psmask", [128, 512], F32, kind="ExternalInput").ap()
    out = nc.dram_tensor("out", [OWN, D], F32, kind="ExternalOutput").ap()

    dr = lambda n, s, dt: nc.dram_tensor(n, s, dt, kind="Internal").ap()
    wq1T_d = dr("wq1T_d", [8, 128, DH], FP16)
    wq2T_d = dr("wq2T_d", [32, 128, D], FP16)
    wT_d = {n: dr(f"w{n}T_d", [8, 128, D], BF16) for n in ("q", "k", "kp", "v", "o")}
    xA_d = dr("xA_d", [T_EXT, D], F32)
    xB_d = dr("xB_d", [T_EXT, D], F32)
    spl = {n: dr(f"{n}_d", [8, 128, SPW], BF16) for n in ("q", "k", "kp", "v")}
    xh2_d = dr("xh2_d", [T_EXT, D], F32)
    xh2T_d = dr("xh2T_d", [8, 128, T_EXT], BF16)
    dbg_d = dr("dbg_d", [8, 128, 4096], BF16)

    sb = lambda n, s, dt=F32: nc.alloc_sbuf_tensor(n, s, dt)
    IDF = sb("IDF", [128, 128], F32)
    IDB = sb("IDB", [128, 128], BF16)
    IDH = sb("IDH", [128, 128], FP16)
    X0 = sb("X0", [128, D])
    X1 = sb("X1", [128, D])
    TT_ = sb("TT_", [128, D])
    XQI = sb("XQI", [128, D], I16)
    EE_ = sb("EE_", [128, D])
    XQH = sb("XQH", [128, D], FP16)
    WTMP = sb("WTMP", [128, D], BF16)
    WTMO = sb("WTMO", [128, D], BF16)
    ROWT = sb("ROWT", [1, 132])
    GB = sb("GB", [128, D])
    p.spacer_src = GB
    BB = sb("BB", [128, D])
    WCV = sb("WCV", [128, 32, 4])
    CORR1 = sb("CORR1", [128, 32])
    CORR2 = sb("CORR2", [128, 8])
    SW1 = sb("SW1", [128, 1])
    ISW1 = sb("ISW1", [128, 1])
    SW2 = sb("SW2", [128, 1])
    ISW2 = sb("ISW2", [128, 1])
    ONESC = sb("ONESC", [128, 1], F32)
    ONESR = sb("ONESR", [1, 128], F32)
    ZEROB = sb("ZEROB", [128, 32], BF16)
    names = "SUMX SUMQ MU RST TS0 TS1 TS2 MXC MNC RSC BCC ALC BE2".split()
    C = {n: sb(n, [128, 1]) for n in names}

    PSA = nc.alloc_psum_tensor("PSA", [128, 1024], F32)
    PSB = nc.alloc_psum_tensor("PSB", [128, 512], F32)
    PSC = nc.alloc_psum_tensor("PSC", [128, 512], F32)
    PSD = nc.alloc_psum_tensor("PSD", [128, 512], F32)
    PST2 = nc.alloc_psum_tensor("PST2", [128, 1024], F32)
    PSTB = nc.alloc_psum_tensor("PSTB", [128, 1024], BF16)
    psa, psb, psc, psd = PSA.ap(), PSB.ap(), PSC.ap(), PSD.ap()
    pst2, pstb = PST2.ap(), PSTB.ap()
    psth = pstb.bitcast(FP16)
    psth3 = psth.rearrange("p (a b) -> p a b", b=128)

    def mset(eng, tile_ap, val, deps=()):
        return p.t(eng, lambda e, t=tile_ap, v=val: e.memset(t, v), deps)

    t_ones_c = mset("gpsimd", ONESC[:], 1.0)
    t_ones_r = mset("gpsimd", ONESR[:], 1.0)
    t_zerob = mset("gpsimd", ZEROB[:], 0.0)
    ld_id = p.dma("sync", lambda e: e.dma_start(out=IDF[:], in_=ident_i[:, :]))
    t_idb = p.t("vector", lambda e: e.tensor_copy(IDB[:], IDF[:]), [ld_id])
    t_idh = p.t("vector", lambda e: e.tensor_copy(IDH[:], IDF[:]), [ld_id])
    ld_gb = p.dma("sync", lambda e: e.dma_start(
        out=GB[:], in_=AP(gamma_i.tensor, 0, [[0, 128], [1, D]])))
    ld_bb = p.dma("sync", lambda e: e.dma_start(
        out=BB[:], in_=AP(beta_i.tensor, 0, [[0, 128], [1, D]])))
    ld_cw = p.dma("sync", lambda e: e.dma_start(
        out=AP(WCV, 0, [[128, 128], [4, 32], [1, 3]]),
        in_=AP(cw_i.tensor, 0, [[3, 128], [3 * 128, 32], [1, 3]])))
    ld_cb = p.dma("sync", lambda e: e.dma_start(
        out=AP(WCV, 3, [[128, 128], [4, 32], [1, 1]]),
        in_=AP(cb_i.tensor, 0, [[1, 128], [128, 32], [0, 1]]),
        allow_slow_non_contiguous=True))

    t_ai1 = mset("vector", C["SUMX"][:], 0.0)
    t_ai2 = mset("vector", C["SUMQ"][:], 0.0)
    prev_buf = [None, None]
    w2_flat = w2_i.rearrange("a (b c) -> (a b) c", c=D)
    red31 = red63 = None
    for i in range(64):
        buf = X0 if i % 2 == 0 else X1
        src = w1_i[i * 128:(i + 1) * 128, :] if i < 32 else \
            w2_flat[(i - 32) * 128:(i - 31) * 128, :]
        ld = p.dma("sync", lambda e, b=buf, s=src: e.dma_start(out=b[:], in_=s),
                   [prev_buf[i % 2]])
        acc = C["SUMX"] if i < 32 else C["SUMQ"]
        dep0 = t_ai1 if i < 32 else t_ai2

        def em(e, b=buf, a=acc):
            e.tensor_reduce(C["TS0"][:], b[:], AX.X, ALU.add, apply_absolute_value=True)
            return e.tensor_tensor(a[:], a[:], C["TS0"][:], ALU.add)
        tk = p.t("vector", em, [ld, dep0])
        prev_buf[i % 2] = tk
        if i == 31:
            red31 = tk
        if i == 63:
            red63 = tk

    def em_tot(e):
        e.matmul(psd[0:1, 0:1], ONESC[:, 0:1], C["SUMX"][:, 0:1], start=True, stop=True)
        return e.matmul(psd[0:1, 1:2], ONESC[:, 0:1], C["SUMQ"][:, 0:1],
                        start=True, stop=True, skip_group_check=True)
    t_tot = p.t("tensor", em_tot, [red31, red63, t_ones_c])
    t_mean = p.t("vector", lambda e: e.tensor_scalar(
        ROWT[0:1, 0:2], psd[0:1, 0:2], 1.0 / (DH * D), 1e-5, ALU.mult, ALU.max), [t_tot])
    t_bc = p.t("tensor", lambda e: e.matmul(
        psd[:, 2:4], ONESR[0:1, :], ROWT[0:1, 0:2], start=True, stop=True,
        skip_group_check=True), [t_mean, t_ones_r])

    def em_sw(e):
        e.tensor_copy(ISW1[:], psd[:, 2:3])
        e.tensor_copy(ISW2[:], psd[:, 3:4])
        e.reciprocal(SW1[:], ISW1[:])
        return e.reciprocal(SW2[:], ISW2[:])
    t_sw = p.t("vector", em_sw, [t_bc])

    with nc.sbuf_tensor("P0EV", [128, 1024], FP16) as P0EV:
        prev_buf = [t_sw, t_sw]
        t_ci1 = mset("vector", CORR1[:], 0.0, [t_sw])
        t_ci2 = mset("vector", CORR2[:], 0.0, [t_sw])
        p0_tail = []
        prev_ev = t_idh
        t_cs_last = None
        for i in range(64):
            buf = X0 if i % 2 == 0 else X1
            if i < 32:
                src = w1_i[i * 128:(i + 1) * 128, :]
                sw = SW1
            else:
                r2, cc2 = (i - 32) // 4, (i - 32) % 4
                src = w2_i[128 * r2:128 * (r2 + 1), 1024 * cc2:1024 * (cc2 + 1)]
                sw = SW2
            ld = p.dma("sync", lambda e, b=buf, s=src: e.dma_start(out=b[:], in_=s),
                       [prev_buf[i % 2]])
            t_t = p.t("scalar", lambda e, b=buf, s=sw: e.mul(TT_[:], b[:], s[:, 0:1]),
                      [ld, t_sw])
            prev_buf[i % 2] = t_t

            def em_q(e):
                e.tensor_scalar(EE_[:], TT_[:], 128.0, None, ALU.add)
                e.tensor_copy(XQI[:], EE_[:])
                e.tensor_tensor(EE_[:], EE_[:], XQI[:], ALU.subtract)
                e.scalar_tensor_tensor(TT_[:], EE_[:], 0.5, XQI[:], ALU.is_ge, ALU.add)
                e.tensor_scalar(TT_[:], TT_[:], 129.0, 127.0, ALU.min, ALU.max)
                return e.tensor_scalar(XQH[:], TT_[:], 128.0, None, ALU.subtract)
            t_q = p.t("vector", em_q, [t_t])
            if i < 32:
                def em_cs(e, col=i):
                    e.tensor_reduce(C["TS0"][:], XQH[:], AX.X, ALU.add)
                    return e.tensor_tensor(CORR1[:, col:col + 1], CORR1[:, col:col + 1],
                                           C["TS0"][:], ALU.add)
                t_cs = p.t("vector", em_cs, [t_q, t_ci1])
            else:
                r2 = (i - 32) // 4

                def em_cs(e, col=r2):
                    e.tensor_reduce(C["TS0"][:], XQH[:], AX.X, ALU.add)
                    return e.tensor_tensor(CORR2[:, col:col + 1], CORR2[:, col:col + 1],
                                           C["TS0"][:], ALU.add)
                t_cs = p.t("vector", em_cs, [t_q, t_ci2])
            t_cs_last = t_cs

            def em_tr(e):
                for j in range(8):
                    r = e.transpose(psth[:, j * 128:(j + 1) * 128],
                                    XQH[:, j * 128:(j + 1) * 128], IDH[:])
                return r
            t_tr = p.t("tensor", em_tr, [t_q, t_idh, prev_ev])
            t_ev = p.t("vector", lambda e: e.tensor_copy(P0EV[:], psth[:, 0:1024]),
                       [t_tr])
            prev_ev = t_ev
            if i < 32:
                sp = lambda e, ii=i: e.dma_start(
                    out=AP(wq1T_d.tensor, ii * 128, [[DH, 128], [128 * DH, 8], [1, 128]]),
                    in_=AP(P0EV, 0, [[1024, 128], [128, 8], [1, 128]]))
            else:
                r2, cc2 = (i - 32) // 4, (i - 32) % 4
                sp = lambda e, r2=r2, cc2=cc2: e.dma_start(
                    out=AP(wq2T_d.tensor, (cc2 * 8) * 128 * D + r2 * 128,
                           [[D, 128], [128 * D, 8], [1, 128]]),
                    in_=AP(P0EV, 0, [[1024, 128], [128, 8], [1, 128]]))
            t_sp = p.dma("sync", sp, [t_ev])
            p0_tail.append(t_sp)

        t_corr = p.t("vector", lambda e: (
            e.tensor_scalar(CORR1[:], CORR1[:], 128.0, None, ALU.mult),
            e.tensor_scalar(CORR2[:], CORR2[:], 128.0, None, ALU.mult))[-1],
            [t_cs_last])

        for name in ("q", "k", "kp", "v", "o"):
            w_src = w_att[name]
            for i in range(8):
                buf = X0 if i % 2 == 0 else X1
                ld = p.dma("sync", lambda e, b=buf, s=w_src[i * 128:(i + 1) * 128, :]:
                           e.dma_start(out=b[:], in_=s), [prev_buf[i % 2]])
                t_cast = p.t("vector", lambda e, b=buf: e.tensor_copy(WTMP[:], b[:]),
                             [ld, prev_ev])
                prev_buf[i % 2] = t_cast

                def em_tr(e):
                    for j in range(8):
                        r = e.transpose(pstb[:, j * 128:(j + 1) * 128],
                                        WTMP[:, j * 128:(j + 1) * 128], IDB[:])
                    return r
                t_tr = p.t("tensor", em_tr, [t_cast, t_idb])
                t_ev2 = p.t("vector", lambda e: e.tensor_copy(WTMO[:], pstb[:, 0:1024]),
                            [t_tr])
                prev_ev = t_ev2
                t_sp = p.dma("sync", lambda e, n=name, ii=i: e.dma_start(
                    out=AP(wT_d[n].tensor, ii * 128, [[D, 128], [128 * D, 8], [1, 128]]),
                    in_=AP(WTMO, 0, [[1024, 128], [128, 8], [1, 128]])), [t_ev2])
                p0_tail.append(t_sp)
        p0_done = p0_tail + [t_corr]

    def ffn_phase(x_src, x_dst, own_only, entry_deps, tag=""):
        with contextlib.ExitStack() as fs:
            WQ1T = fs.enter_context(nc.sbuf_tensor("WQ1T" + tag, [128, 8, DH], FP16))
            WQ2T = fs.enter_context(nc.sbuf_tensor("WQ2T" + tag, [128, 32, D], FP16))
            XQT = fs.enter_context(nc.sbuf_tensor("XQT" + tag, [128, 8, 3, 130], FP16))
            YS = fs.enter_context(nc.sbuf_tensor("YS" + tag, [128, 32, 128], F32))
            XQ2 = fs.enter_context(nc.sbuf_tensor("XQ2" + tag, [128, 32, 128], FP16))
            OUT2 = fs.enter_context(nc.sbuf_tensor("OUT2" + tag, [128, D], F32))
            XN = fs.enter_context(nc.sbuf_tensor("XN" + tag, [128, D], F32))
            ALPH = fs.enter_context(nc.sbuf_tensor("ALPH" + tag, [128, 130], F32))
            S2B = fs.enter_context(nc.sbuf_tensor("S2B" + tag, [128, 128], F32))
            H_ = fs.enter_context(nc.sbuf_tensor("H_" + tag, [128, 130], F32))
            TC = fs.enter_context(nc.sbuf_tensor("TC" + tag, [128, 128], F32))
            T2 = fs.enter_context(nc.sbuf_tensor("T2" + tag, [128, 128], F32))
            YY = fs.enter_context(nc.sbuf_tensor("YY" + tag, [128, 128], F32))
            M128 = fs.enter_context(nc.sbuf_tensor("M128" + tag, [128, 128], F32))
            A2 = fs.enter_context(nc.sbuf_tensor("A2" + tag, [128, 128], F32))
            I2 = fs.enter_context(nc.sbuf_tensor("I2" + tag, [128, 128], I16))
            E2 = fs.enter_context(nc.sbuf_tensor("E2" + tag, [128, 128], F32))
            ARW = fs.enter_context(nc.sbuf_tensor("ARW" + tag, [1, 3 * 130], F32))

            ld_wq1 = p.dma("sync", lambda e: e.dma_start(
                out=AP(WQ1T, 0, [[8 * DH, 128], [DH, 8], [1, DH]]),
                in_=AP(wq1T_d.tensor, 0, [[DH, 128], [128 * DH, 8], [1, DH]])),
                entry_deps)
            ld_wq2 = p.dma("sync", lambda e: e.dma_start(
                out=AP(WQ2T, 0, [[32 * D, 128], [D, 32], [1, D]]),
                in_=AP(wq2T_d.tensor, 0, [[D, 128], [128 * D, 32], [1, D]])),
                entry_deps)
            t_xqt0 = mset("gpsimd", XQT[:], 0.0, entry_deps)
            t_arw = mset("gpsimd", ARW[:], 1.0, entry_deps)

            KSLOTS = [(psa, 0), (psa, 512), (psb, 0)]
            t_join = p.t("gpsimd", lambda e: e.engine_nop(), entry_deps)
            deps_x = [t_join, t_join]
            ring_task = [t_xqt0, t_xqt0, t_xqt0]
            tr_prev = [None]
            stat_prev = [None]
            prev_rows = [None]
            prev_tra = [None]
            mm_done_prev = [None]
            last_bcs_read = [None]
            st_tasks = []
            tasks_q = {}
            for j in range(NT + 1):
                if j < NT:
                    buf = X0 if j % 2 == 0 else X1
                    ld = p.dma("sync", lambda e, b=buf,
                               s=x_src[j * 128:(j + 1) * 128, :]:
                               e.dma_start(out=b[:], in_=s), [deps_x[j % 2]])
                    t_sq = p.t("scalar", lambda e, b=buf: e.activation(
                        TT_[:], b[:], AF.Square, accum_out=C["SUMQ"][:, 0:1]),
                        [ld, stat_prev[0]])

                    def em_st1(e, b=buf):
                        e.tensor_reduce(C["SUMX"][:], b[:], AX.X, ALU.add)
                        e.tensor_reduce(C["MXC"][:], b[:], AX.X, ALU.max)
                        e.tensor_reduce(C["MNC"][:], b[:], AX.X, ALU.min)
                        e.tensor_scalar(C["MU"][:], C["SUMX"][:], 1.0 / D, None, ALU.mult)
                        e.tensor_scalar(C["TS0"][:], C["SUMQ"][:], 1.0 / D, None, ALU.mult)
                        e.tensor_tensor(C["TS1"][:], C["MU"][:], C["MU"][:], ALU.mult)
                        e.tensor_tensor(C["TS0"][:], C["TS0"][:], C["TS1"][:], ALU.subtract)
                        return e.tensor_scalar(C["TS0"][:], C["TS0"][:], 1e-5, None, ALU.add)
                    t_st1 = p.t("vector", em_st1, [ld, t_sq])
                    t_sqr = p.t("scalar", lambda e: e.activation(
                        C["TS2"][:], C["TS0"][:], AF.Sqrt), [t_st1])

                    def em_st2(e):
                        e.reciprocal(C["RST"][:], C["TS2"][:])
                        for _ in range(2):
                            e.tensor_tensor(C["TS1"][:], C["RST"][:], C["RST"][:], ALU.mult)
                            e.tensor_tensor(C["TS1"][:], C["TS1"][:], C["TS0"][:], ALU.mult)
                            e.tensor_scalar(C["TS1"][:], C["TS1"][:], -0.5, 1.5,
                                            ALU.mult, ALU.add)
                            e.tensor_tensor(C["RST"][:], C["RST"][:], C["TS1"][:], ALU.mult)
                        e.tensor_tensor(C["TS1"][:], C["MXC"][:], C["MU"][:], ALU.subtract)
                        e.tensor_tensor(C["TS2"][:], C["MU"][:], C["MNC"][:], ALU.subtract)
                        e.tensor_tensor(C["TS1"][:], C["TS1"][:], C["TS2"][:], ALU.max)
                        e.tensor_tensor(C["TS1"][:], C["TS1"][:], C["RST"][:], ALU.mult)
                        e.tensor_scalar(C["TS1"][:], C["TS1"][:], 1e-5, None, ALU.max)
                        e.tensor_scalar(C["ALC"][:], C["TS1"][:], 1.0 / 127.0, None, ALU.mult)
                        e.tensor_tensor(C["ALC"][:], C["ALC"][:], ISW1[:], ALU.mult)
                        e.reciprocal(C["TS2"][:], C["TS1"][:])
                        e.tensor_scalar(C["TS2"][:], C["TS2"][:], 127.0, None, ALU.mult)
                        e.tensor_tensor(C["RSC"][:], C["RST"][:], C["TS2"][:], ALU.mult)
                        e.tensor_tensor(C["BCC"][:], C["MU"][:], C["RSC"][:], ALU.mult)
                        return e.tensor_scalar(C["BCC"][:], C["BCC"][:], -1.0, 128.0,
                                               ALU.mult, ALU.add)
                    t_st2 = p.t("vector", em_st2, [t_sqr, t_sw])
                    t_tq = p.t("scalar", lambda e, b=buf: e.activation(
                        TT_[:], b[:], AF.Identity, bias=C["BCC"][:, 0:1],
                        scale=C["RSC"][:, 0:1]), [t_st2])

                    def em_rnd(e):
                        e.tensor_copy(XQI[:], TT_[:])
                        e.tensor_tensor(EE_[:], TT_[:], XQI[:], ALU.subtract)
                        return e.scalar_tensor_tensor(XQH[:], EE_[:], 0.5, XQI[:],
                                                      ALU.is_ge, ALU.add)
                    t_rnd = p.t("vector", em_rnd, [t_tq])
                    stat_prev[0] = t_rnd

                    def em_trq(e):
                        for jj in range(8):
                            r = e.transpose(psth[:, jj * 128:(jj + 1) * 128],
                                            XQH[:, jj * 128:(jj + 1) * 128], IDH[:])
                        return r
                    t_trq = p.t("tensor", em_trq, [t_rnd, t_idh, tr_prev[0]])

                    def em_evq(e, slot=j % 3, sl=(j - 1) % 3, sr=(j + 1) % 3):
                        e.tensor_copy(AP(XQT, slot * 130 + 1,
                                         [[8 * 3 * 130, 128], [3 * 130, 8], [1, 128]]),
                                      psth3)
                        e.tensor_copy(AP(XQT, sl * 130 + 129,
                                         [[8 * 3 * 130, 128], [3 * 130, 8], [1, 1]]),
                                      psth3[:, :, 0:1])
                        return e.tensor_copy(
                            AP(XQT, sr * 130 + 0,
                               [[8 * 3 * 130, 128], [3 * 130, 8], [1, 1]]),
                            psth3[:, :, 127:128])
                    t_evq = p.t("vector", em_evq, [t_trq, t_xqt0])
                    ring_task[j % 3] = t_evq
                    tr_prev[0] = t_evq
                    t_tra = p.t("tensor", lambda e: e.transpose(
                        psc[0:1, 0:128], C["ALC"][:, 0:1], IDF[:]),
                        [t_st2, ld_id, prev_rows[0]])
                    prev_tra[0] = t_tra

                    def em_eva(e, slot=j % 3, sl=(j - 1) % 3, sr=(j + 1) % 3):
                        e.tensor_copy(ARW[0:1, slot * 130 + 1:slot * 130 + 129],
                                      psc[0:1, 0:128])
                        e.tensor_copy(ARW[0:1, sl * 130 + 129:sl * 130 + 130],
                                      psc[0:1, 0:1])
                        return e.tensor_copy(ARW[0:1, sr * 130:sr * 130 + 1],
                                             psc[0:1, 127:128])
                    t_eva = p.t("vector", em_eva, [t_tra, t_arw])
                    tasks_q[j] = (t_evq, t_eva, ld, buf)

                i = j - 1
                if i < 0 or i >= NT:
                    continue
                buf_i = tasks_q[i][3]
                dep_ring = [ring_task[0], ring_task[1], ring_task[2]]
                deps_bca = [tasks_q[i][1]]
                if i + 1 in tasks_q:
                    deps_bca.append(tasks_q[i + 1][1])
                t_bca = p.t("tensor", lambda e, slot=i % 3: e.matmul(
                    psd[:, 0:130], ONESR[0:1, :], ARW[0:1, slot * 130:slot * 130 + 130],
                    start=True, stop=True, skip_group_check=True),
                    deps_bca + [last_bcs_read[0]])
                t_alph = p.t("vector", lambda e: e.tensor_copy(ALPH[:], psd[:, 0:130]),
                             [t_bca])

                t_ys_last = None
                t_h = None
                t_tp = None
                for cg in range(0, 32, 3):
                    ncg = min(3, 32 - cg)

                    def em_mm1(e, cg=cg, ncg=ncg, ii=i):
                        for ci in range(ncg):
                            c = cg + ci
                            pk, off = KSLOTS[ci]
                            for d8 in range(8):
                                r = e.matmul(
                                    pk[:, off:off + 130],
                                    AP(WQ1T, d8 * DH + c * 128,
                                       [[8 * DH, 128], [1, 128]]),
                                    AP(XQT, d8 * 3 * 130 + (ii % 3) * 130,
                                       [[8 * 3 * 130, 128], [1, 130]]),
                                    start=(d8 == 0), stop=(d8 == 7))
                        return r
                    mm1 = p.t("tensor", em_mm1, dep_ring + [ld_wq1, mm_done_prev[0]])
                    for ci in range(ncg):
                        c = cg + ci
                        pk, off = KSLOTS[ci]
                        t_h = p.t("vector", lambda e, pk=pk, off=off, c=c:
                                  e.scalar_tensor_tensor(H_[:], pk[:, off:off + 130],
                                                         CORR1[:, c:c + 1], ALPH[:],
                                                         ALU.subtract, ALU.mult),
                                  [mm1, t_alph, t_corr, t_tp])
                        t_tc = p.t("scalar", lambda e, c=c: e.activation(
                            TC[:], H_[:, 1:129], AF.Identity, bias=WCV[:, c, 3:4],
                            scale=WCV[:, c, 1:2]), [t_h, ld_cw, ld_cb])

                        def em_taps(e, c=c):
                            e.scalar_tensor_tensor(T2[:], H_[:, 0:128], WCV[:, c, 0:1],
                                                   TC[:], ALU.mult, ALU.add)
                            return e.scalar_tensor_tensor(YY[:], H_[:, 2:130],
                                                          WCV[:, c, 2:3], T2[:],
                                                          ALU.mult, ALU.add)
                        t_tp = p.t("vector", em_taps, [t_tc])
                        t_ys = p.t("scalar", lambda e, c=c: e.activation(
                            YS[:, c, :], YY[:], AF.Silu), [t_tp])
                        t_ys_last = t_ys
                    mm_done_prev[0] = t_h

                t_m = p.t("vector", lambda e: e.tensor_reduce(
                    M128[:], AP(YS, 0, [[32 * 128, 128], [1, 128], [128, 32]]),
                    AX.X, ALU.max, apply_absolute_value=True), [t_ys_last])
                t_trm = p.t("tensor", lambda e: e.transpose(
                    pst2[:, 0:128], M128[:], IDF[:]), [t_m, ld_id, mm_done_prev[0]])

                def em_s2(e):
                    e.tensor_reduce(C["TS0"][:], pst2[:, 0:128], AX.X, ALU.max)
                    e.tensor_scalar(C["TS0"][:], C["TS0"][:], 1e-5, None, ALU.max)
                    e.reciprocal(C["TS1"][:], C["TS0"][:])
                    e.tensor_scalar(C["TS1"][:], C["TS1"][:], 127.0, None, ALU.mult)
                    e.tensor_scalar(C["BE2"][:], C["TS0"][:], 0.5 / 127.0, None, ALU.mult)
                    return e.tensor_tensor(C["BE2"][:], C["BE2"][:], ISW2[:], ALU.mult)
                t_s2 = p.t("vector", em_s2, [t_trm, t_sw])
                t_trs = p.t("tensor", lambda e: e.transpose(
                    psc[0:1, 128:256], C["TS1"][:, 0:1], IDF[:]),
                    [t_s2, ld_id, prev_tra[0]])
                t_rows = p.t("vector", lambda e: e.tensor_copy(
                    ROWT[0:1, 0:128], psc[0:1, 128:256]), [t_trs])
                prev_rows[0] = t_rows
                t_bcs = p.t("tensor", lambda e: e.matmul(
                    psd[:, 256:384], ONESR[0:1, :], ROWT[0:1, 0:128],
                    start=True, stop=True, skip_group_check=True),
                    [t_rows, t_alph])
                t_s2b = p.t("vector", lambda e: e.tensor_copy(S2B[:], psd[:, 256:384]),
                            [t_bcs])
                last_bcs_read[0] = t_s2b

                t_q2_last = t_s2b
                for c in range(32):
                    def em_q2(e, c=c):
                        e.tensor_tensor(A2[:], YS[:, c, :], S2B[:], ALU.mult)
                        e.tensor_scalar(I2[:], A2[:], 128.0, None, ALU.add)
                        e.scalar_tensor_tensor(E2[:], A2[:], 128.0, I2[:],
                                               ALU.add, ALU.subtract)
                        return e.scalar_tensor_tensor(XQ2[:, c, :], E2[:], 0.5, I2[:],
                                                      ALU.is_ge, ALU.add)
                    t_q2_last = p.t("vector", em_q2, [t_q2_last])

                def em_mm2(e):
                    for bank in range(2):
                        first = True
                        for d8 in range(bank * 4, bank * 4 + 4):
                            for c in range(32):
                                r = e.matmul(pst2[:, d8 * 128:(d8 + 1) * 128],
                                             AP(WQ2T, c * D + d8 * 128,
                                                [[32 * D, 128], [1, 128]]),
                                             XQ2[:, c, :],
                                             start=first,
                                             stop=(d8 == bank * 4 + 3 and c == 31),
                                             skip_group_check=True)
                                first = False
                    return r
                mm2 = p.t("tensor", em_mm2, [t_q2_last, ld_wq2, t_trm])

                def em_o2(e):
                    for d8 in range(8):
                        r = e.tensor_scalar(OUT2[:, d8 * 128:(d8 + 1) * 128],
                                            pst2[:, d8 * 128:(d8 + 1) * 128],
                                            CORR2[:, d8:d8 + 1], None, ALU.subtract)
                    return r
                t_o2 = p.t("vector", em_o2, [mm2, t_corr])

                def em_tro(e):
                    for jj in range(8):
                        r = e.transpose(pst2[:, jj * 128:(jj + 1) * 128],
                                        OUT2[:, jj * 128:(jj + 1) * 128], IDF[:])
                    return r
                t_tro = p.t("tensor", em_tro, [t_o2, ld_id])
                t_xn = p.t("vector", lambda e, b=buf_i: e.scalar_tensor_tensor(
                    XN[:], pst2[:, 0:1024], C["BE2"][:, 0:1], b[:], ALU.mult, ALU.add),
                    [t_tro, t_s2])
                mm_done_prev[0] = t_xn
                if (not own_only) or (OSUB0 <= i < OSUB0 + NOSUB):
                    r0 = (i - OSUB0) if own_only else i
                    dst = x_dst[r0 * 128:(r0 + 1) * 128, :]
                    st = p.dma("sync", lambda e, d=dst: e.dma_start(out=d, in_=XN[:]),
                               [t_xn])
                    st_tasks.append(st)
                    deps_x[i % 2] = st
                else:
                    deps_x[i % 2] = t_xn
            barrier = p.tails()
        return st_tasks, barrier

    def proj_phase(entry_deps):
        with contextlib.ExitStack() as fs:
            PW = fs.enter_context(nc.sbuf_tensor("PW", [128, 8, D], BF16))
            XHT = fs.enter_context(nc.sbuf_tensor("XHT", [128, 8, 128], BF16))
            QF = fs.enter_context(nc.sbuf_tensor("QF", [128, D], BF16))
            XHB = fs.enter_context(nc.sbuf_tensor("XHB", [128, D], BF16))
            t_join = p.t("gpsimd", lambda e: e.engine_nop(), entry_deps)
            deps_x = [t_join, t_join]
            stat_prev = [None]
            tr_prev = [t_join]
            sp_tasks = []
            for j in range(NT):
                buf = X0 if j % 2 == 0 else X1
                ld = p.dma("sync", lambda e, b=buf, s=xA_d[j * 128:(j + 1) * 128, :]:
                           e.dma_start(out=b[:], in_=s), [deps_x[j % 2]])
                t_sq = p.t("scalar", lambda e, b=buf: e.activation(
                    TT_[:], b[:], AF.Square, accum_out=C["SUMQ"][:, 0:1]),
                    [ld, stat_prev[0]])

                def em_st1(e, b=buf):
                    e.tensor_reduce(C["SUMX"][:], b[:], AX.X, ALU.add)
                    e.tensor_scalar(C["MU"][:], C["SUMX"][:], 1.0 / D, None, ALU.mult)
                    e.tensor_scalar(C["TS0"][:], C["SUMQ"][:], 1.0 / D, None, ALU.mult)
                    e.tensor_tensor(C["TS1"][:], C["MU"][:], C["MU"][:], ALU.mult)
                    e.tensor_tensor(C["TS0"][:], C["TS0"][:], C["TS1"][:], ALU.subtract)
                    return e.tensor_scalar(C["TS0"][:], C["TS0"][:], 1e-5, None, ALU.add)
                t_st1 = p.t("vector", em_st1, [ld, t_sq])
                t_sqr = p.t("scalar", lambda e: e.activation(
                    C["TS2"][:], C["TS0"][:], AF.Sqrt), [t_st1])

                def em_st2(e):
                    e.reciprocal(C["RST"][:], C["TS2"][:])
                    for _ in range(2):
                        e.tensor_tensor(C["TS1"][:], C["RST"][:], C["RST"][:], ALU.mult)
                        e.tensor_tensor(C["TS1"][:], C["TS1"][:], C["TS0"][:], ALU.mult)
                        e.tensor_scalar(C["TS1"][:], C["TS1"][:], -0.5, 1.5,
                                        ALU.mult, ALU.add)
                        e.tensor_tensor(C["RST"][:], C["RST"][:], C["TS1"][:], ALU.mult)
                    e.tensor_tensor(C["BCC"][:], C["MU"][:], C["RST"][:], ALU.mult)
                    return e.tensor_scalar(C["BCC"][:], C["BCC"][:], -1.0, None, ALU.mult)
                t_st2 = p.t("vector", em_st2, [t_sqr])
                stat_prev[0] = t_st2
                t_ap = p.t("scalar", lambda e, b=buf: e.activation(
                    EE_[:], b[:], AF.Identity, bias=C["BCC"][:, 0:1],
                    scale=C["RST"][:, 0:1]), [t_st2])
                t_cb = p.t("vector", lambda e: e.tensor_copy(XHB[:], EE_[:]), [t_ap])
                st1 = p.dma("sync", lambda e, jj=j: e.dma_start(
                    out=xh2_d[jj * 128:(jj + 1) * 128, :], in_=EE_[:]), [t_cb])
                deps_x[j % 2] = st1

                def em_tr(e):
                    for jj in range(8):
                        r = e.transpose(pstb[:, jj * 128:(jj + 1) * 128],
                                        XHB[:, jj * 128:(jj + 1) * 128], IDB[:])
                    return r
                t_tr = p.t("tensor", em_tr, [t_cb, t_idb, tr_prev[0]])
                t_ev = p.t("vector", lambda e: e.tensor_copy(QF[:], pstb[:, 0:1024]),
                           [t_tr])
                tr_prev[0] = t_ev
                st2 = p.dma("sync", lambda e, jj=j: e.dma_start(
                    out=AP(xh2T_d.tensor, jj * 128, [[T_EXT, 128], [128 * T_EXT, 8],
                                                     [1, 128]]),
                    in_=AP(QF, 0, [[1024, 128], [128, 8], [1, 128]])), [t_ev])
                sp_tasks.append(st1)
                sp_tasks.append(st2)

            def zero_halo(dst_t, width):
                tk = []
                for q8 in range(8):
                    for side in (0, width - 32):
                        tk.append(p.dma("gpsimd", lambda e, q8=q8, s=side, d=dst_t:
                                        e.dma_start(out=AP(d.tensor, q8 * 128 * width + s,
                                                           [[width, 128], [1, 32]]),
                                                    in_=ZEROB[:]), [t_zerob]))
                return tk
            zh = []
            for n in ("q", "k", "kp", "v"):
                zh += zero_halo(spl[n], SPW)

            for n in ("q", "k", "kp", "v"):
                ldw = p.dma("sync", lambda e, n=n: e.dma_start(
                    out=AP(PW, 0, [[8 * D, 128], [D, 8], [1, D]]),
                    in_=AP(wT_d[n].tensor, 0, [[D, 128], [128 * D, 8], [1, D]])),
                    sp_tasks + entry_deps)
                prev_mm = [None]
                for j in range(NT):
                    ldx = p.dma("sync", lambda e, jj=j: e.dma_start(
                        out=AP(XHT, 0, [[8 * 128, 128], [128, 8], [1, 128]]),
                        in_=AP(xh2T_d.tensor, jj * 128, [[T_EXT, 128],
                                                         [128 * T_EXT, 8], [1, 128]])),
                        [sp_tasks[2 * j + 1], prev_mm[0]])

                    def em_mm(e):
                        for do8 in range(8):
                            for di in range(8):
                                r = e.matmul(pst2[:, do8 * 128:(do8 + 1) * 128],
                                             PW[:, di, do8 * 128:(do8 + 1) * 128],
                                             XHT[:, di, :],
                                             start=(di == 0), stop=(di == 7))
                        return r
                    mm = p.t("tensor", em_mm, [ldx, ldw, prev_mm[0]])
                    scl = 0.125 if n == "q" else 1.0
                    t_ev = p.t("vector", lambda e, s=scl: e.tensor_scalar(
                        QF[:], pst2[:, 0:1024], s, None, ALU.mult), [mm])
                    prev_mm[0] = t_ev
                    st = p.dma("sync", lambda e, n=n, jj=j: e.dma_start(
                        out=AP(spl[n].tensor, 32 + jj * 128,
                               [[SPW, 128], [128 * SPW, 8], [1, 128]]),
                        in_=AP(QF, 0, [[1024, 128], [128, 8], [1, 128]])), [t_ev])
                    sp_tasks.append(st)
            barrier = p.tails()
        return sp_tasks, barrier, zh

    def attn_phase(entry_deps):
        with contextlib.ExitStack() as fs:
            KB = fs.enter_context(nc.sbuf_tensor("KB", [128, ACH + 32], BF16))
            KPB = fs.enter_context(nc.sbuf_tensor("KPB", [128, ACH + 64], BF16))
            VB = fs.enter_context(nc.sbuf_tensor("VB", [128, ACH + 64], BF16))
            QBD = fs.enter_context(nc.sbuf_tensor("QBD", [128, 2 * ACH], BF16))
            KKB = fs.enter_context(nc.sbuf_tensor("KKB", [128, 31, ACH + 32], BF16))
            SC1 = fs.enter_context(nc.sbuf_tensor("SC1", [128, 992], F32))
            MSK0 = fs.enter_context(nc.sbuf_tensor("MSK0", [128, 992], F32))
            MSK1 = fs.enter_context(nc.sbuf_tensor("MSK1", [128, 992], F32))
            PSM = fs.enter_context(nc.sbuf_tensor("PSM", [128, 512], BF16))
            DBF = fs.enter_context(nc.sbuf_tensor("DBF", [128, 992], BF16)) \
                if cfg.get("dbg") else None
            EB = fs.enter_context(nc.sbuf_tensor("EB", [128, 992], BF16))
            PB_ = fs.enter_context(nc.sbuf_tensor("PB_", [128, 1504], BF16))
            PSR = fs.enter_context(nc.sbuf_tensor("PSR", [128, 512], BF16))
            PSTS = fs.enter_context(nc.sbuf_tensor("PSTS",
                                                   [128, ACH // 64, 4, 128], BF16))
            UF = fs.enter_context(nc.sbuf_tensor("UF", [128, 16, ACH + 32], BF16))
            UT = fs.enter_context(nc.sbuf_tensor("UT",
                                                 [128, (ACH + 32) // 8, 128], BF16))
            ZB = fs.enter_context(nc.sbuf_tensor("ZB", [128, 8, ACH], BF16))
            WOT = fs.enter_context(nc.sbuf_tensor("WOT", [128, 8, D], BF16))
            WOU = fs.enter_context(nc.sbuf_tensor("WOU", [128, 8, ACH], F32))
            XH2 = fs.enter_context(nc.sbuf_tensor("XH2", [128, D], F32))
            WS = fs.enter_context(nc.sbuf_tensor("WS", [128, D], F32))
            VV_ = fs.enter_context(nc.sbuf_tensor("VV_", [128, D], F32))
            NMX = fs.enter_context(nc.sbuf_tensor("NMX", [128, 1], F32))
            SUS = fs.enter_context(nc.sbuf_tensor("SUS", [128, 1], F32))
            RSS = fs.enter_context(nc.sbuf_tensor("RSS", [128, 1], F32))

            ld_m0 = p.dma("sync", lambda e: e.dma_start(out=MSK0[:], in_=mask_i[0]),
                          entry_deps)
            ld_m1 = p.dma("sync", lambda e: e.dma_start(out=MSK1[:], in_=mask_i[1]),
                          entry_deps)
            ld_pmf = p.dma("sync", lambda e: e.dma_start(out=SC1[:, 0:512],
                                                         in_=psmask_i[:, :]), entry_deps)
            t_pm = p.t("vector", lambda e: e.tensor_copy(PSM[:], SC1[:, 0:512]),
                       [ld_pmf])
            ld_wo = p.dma("sync", lambda e: e.dma_start(
                out=AP(WOT, 0, [[8 * D, 128], [D, 8], [1, D]]),
                in_=AP(wT_d["o"].tensor, 0, [[D, 128], [128 * D, 8], [1, D]])),
                entry_deps)
            t_qbd0 = mset("gpsimd", QBD[:], 0.0, entry_deps)
            t_pb0 = mset("gpsimd", PB_[:, 0:512], 0.0, entry_deps)

            st_tasks = []
            prev_chunk_tail = [None]
            t_zevac = [None, None]
            prev_kk = [None]
            prev_amm = [None]
            prev_utt = [None]
            prev_z = [None]
            for (c0, CL) in CHUNKS:
                NG = CL // 64
                NBLK = CL // 16
                NUTT = (CL + 32) // 8
                q_score_done = []
                t_zb = {}
                for q in range(8):
                    ld_k = p.dma("gpsimd", lambda e, q=q, c0=c0, CL=CL: e.dma_start(
                        out=KB[:, 0:CL + 32],
                        in_=spl["k"][q][:, 16 + c0:16 + c0 + CL + 32]),
                        [prev_chunk_tail[0], prev_kk[0]])
                    ld_kp = p.dma("gpsimd", lambda e, q=q, c0=c0, CL=CL: e.dma_start(
                        out=KPB[:, 0:CL + 64], in_=spl["kp"][q][:, c0:c0 + CL + 64]),
                        [prev_chunk_tail[0], prev_kk[0]])
                    ld_q1 = p.dma("sync", lambda e, q=q, c0=c0, CL=CL: e.dma_start(
                        out=AP(QBD, 0, [[2 * ACH, 64], [32, CL // 16], [1, 16]]),
                        in_=AP(spl["q"].tensor, q * 128 * SPW + 32 + c0,
                               [[SPW, 64], [16, CL // 16], [1, 16]])),
                        [t_qbd0, prev_chunk_tail[0], prev_amm[0]])
                    ld_q2 = p.dma("sync", lambda e, q=q, c0=c0, CL=CL: e.dma_start(
                        out=AP(QBD, 64 * 2 * ACH + 16,
                               [[2 * ACH, 64], [32, CL // 16], [1, 16]]),
                        in_=AP(spl["q"].tensor, q * 128 * SPW + 64 * SPW + 32 + c0,
                               [[SPW, 64], [16, CL // 16], [1, 16]])),
                        [t_qbd0, prev_chunk_tail[0], prev_amm[0]])

                    def em_kk(e, CL=CL):
                        return e.tensor_tensor(
                            KKB[:, :, 0:CL + 32].opt(),
                            AP(KB, 0, [[ACH + 32, 128], [0, 31], [1, CL + 32]]),
                            AP(KPB, 31, [[ACH + 64, 128], [-1, 31], [1, CL + 32]]),
                            ALU.mult)
                    t_kk = p.t("vector", em_kk, [ld_k, ld_kp, prev_amm[0]])
                    prev_kk[0] = t_kk

                    grp_tasks = []
                    prev_soft = [prev_z[0]]
                    for grp in range(NG):
                        def em_amm(e, grp=grp, q=q, CL=CL):
                            for b4 in range(4):
                                blk = grp * 4 + b4
                                lhs = AP(QBD, blk * 32, [[2 * ACH, 128], [1, 32]])
                                rhs1 = AP(KKB, 16 * blk,
                                          [[31 * (ACH + 32), 128], [ACH + 32, 16],
                                           [1, 32]])
                                rhs2 = AP(KKB, 16 * (ACH + 32) + 16 * blk,
                                          [[31 * (ACH + 32), 128], [ACH + 32, 15],
                                           [1, 32]])
                                e.matmul(psa[b4 * 32:b4 * 32 + 32, 0:512], lhs, rhs1,
                                         start=True, stop=True,
                                         tile_position=(0, b4 * 32),
                                         skip_group_check=True)
                                r = e.matmul(psa[b4 * 32:b4 * 32 + 32, 512:992], lhs,
                                             rhs2, start=True, stop=True,
                                             tile_position=(0, b4 * 32),
                                             skip_group_check=True)
                            return r
                        t_amm = p.t("tensor", em_amm,
                                    [t_kk, ld_q1, ld_q2, prev_soft[0]])
                        prev_amm[0] = t_amm
                        gidx = (c0 + grp * 64) // 64
                        msk = MSK0 if gidx == mg_first else MSK1

                        def em_soft1(e, msk=msk):
                            e.tensor_tensor(SC1[:], psa[:, 0:992], msk[:], ALU.add)
                            e.reduce_max(NMX[:, 0:1], SC1[:], AX.X)
                            return e.tensor_scalar(NMX[:], NMX[:], -1.0, None, ALU.mult)
                        t_sf1 = p.t("vector", em_soft1, [t_amm, ld_m0, ld_m1, t_pm])
                        prev_soft[0] = t_sf1
                        t_exp = p.t("scalar", lambda e: e.activation(
                            EB[:], SC1[:], AF.Exp, bias=NMX[:, 0:1],
                            accum_out=SUS[:, 0:1]), [t_sf1])

                        def em_nrm(e):
                            e.reciprocal(RSS[:], SUS[:])
                            return e.tensor_scalar(PB_[:, 512:1504], EB[:],
                                                   RSS[:, 0:1], None, ALU.mult)
                        t_nrm = p.t("vector", em_nrm, [t_exp, t_pb0])

                        def em_fold(e):
                            e.tensor_copy(AP(PSR, 0, [[512, 128], [16, 32], [1, 1]]),
                                          AP(PB_, 512 + 480, [[1504, 128], [1, 32], [32, 1]]))
                            e.tensor_tensor(
                                AP(PSR, 1, [[512, 128], [16, 32], [1, 15]]),
                                AP(PB_, 512 + 512, [[1504, 128], [1, 32], [32, 15]]),
                                AP(PB_, 512 + 447, [[1504, 128], [1, 32], [-33, 15]]),
                                ALU.add)
                            return e.tensor_tensor(PSR[:], PSR[:], PSM[:], ALU.mult)
                        t_fold = p.t("vector", em_fold, [t_nrm, t_pm])

                        def em_pst(e):
                            for qh in range(4):
                                r = e.transpose(pstb[:, qh * 128:(qh + 1) * 128],
                                                PSR[:, qh * 128:(qh + 1) * 128],
                                                IDB[:])
                            return r
                        t_pst = p.t("tensor", em_pst, [t_fold, t_idb])
                        t_pse = p.t("vector", lambda e, grp=grp: e.tensor_copy(
                            AP(PSTS, grp * 4 * 128,
                               [[(ACH // 64) * 4 * 128, 128], [1, 512]]),
                            pstb[:, 0:512]), [t_pst])
                        grp_tasks.append(t_pse)
                        if cfg.get("dbg") and c0 == 0 and q == 0 and grp == 2:
                            dbf = p.t("vector", lambda e: e.tensor_copy(
                                DBF[:], SC1[:]), [t_sf1, t_exp])
                            p.dma("sync", lambda e: e.dma_start(
                                out=dbg_d[0][:, 1024:2016], in_=DBF[:]), [dbf])
                            p.dma("sync", lambda e: e.dma_start(
                                out=dbg_d[0][:, 0:992], in_=EB[:]), [t_exp, t_nrm])
                            p.dma("sync", lambda e: e.dma_start(
                                out=dbg_d[1][:, 0:1504], in_=PB_[:]),
                                [t_nrm, t_fold])
                            p.dma("sync", lambda e: e.dma_start(
                                out=dbg_d[1][:, 2048:2560], in_=PSM[:]), [t_pm])
                            p.dma("sync", lambda e: e.dma_start(
                                out=dbg_d[1][:, 3000:3512], in_=PSR[:]),
                                [t_fold, t_pst])

                    if cfg.get("dbg") and c0 == 0 and q == 0:
                        p.dma("sync", lambda e: e.dma_start(
                            out=dbg_d[2][:, 0:4096],
                            in_=AP(PSTS, 0, [[(ACH // 64) * 4 * 128, 128],
                                             [1, 4096]])), grp_tasks[-1:])
                        p.dma("sync", lambda e: e.dma_start(
                            out=dbg_d[5][:, 0:2 * ACH],
                            in_=AP(QBD, 0, [[2 * ACH, 128], [1, 2 * ACH]])),
                            [ld_q1, ld_q2])
                        p.dma("sync", lambda e: e.dma_start(
                            out=dbg_d[6][:, 0:544],
                            in_=AP(KKB, 15 * (ACH + 32),
                                   [[31 * (ACH + 32), 128], [1, 544]])), [t_kk])
                        p.dma("sync", lambda e: e.dma_start(
                            out=dbg_d[6][:, 1024:1024 + ACH + 32],
                            in_=KB[:, 0:ACH + 32]), [t_kk])
                        p.dma("sync", lambda e: e.dma_start(
                            out=dbg_d[6][:, 2048:2048 + ACH + 64],
                            in_=KPB[:, 0:ACH + 64]), [t_kk])

                    ld_v = p.dma("gpsimd", lambda e, q=q, c0=c0, CL=CL: e.dma_start(
                        out=VB[:, 0:CL + 64], in_=spl["v"][q][:, c0:c0 + CL + 64]),
                        [prev_chunk_tail[0], prev_utt[0]])

                    def em_uf(e, CL=CL):
                        return e.tensor_tensor(
                            AP(UF, 0, [[16 * (ACH + 32), 128], [1, 16], [16, CL + 32]]),
                            AP(VB, 16, [[ACH + 64, 128], [0, 16], [1, CL + 32]]),
                            AP(VB, 16, [[ACH + 64, 128], [-1, 16], [1, CL + 32]]),
                            ALU.mult)
                    t_uf = p.t("vector", em_uf, [ld_v, prev_utt[0]])
                    ut_ev = []
                    for kt0 in range(0, NUTT, 8):
                        nk = min(8, NUTT - kt0)

                        def em_utt(e, kt0=kt0, nk=nk, CL=CL):
                            for kk_ in range(nk):
                                r = e.transpose(
                                    pstb[:, kk_ * 128:(kk_ + 1) * 128],
                                    AP(UF, (kt0 + kk_) * 128,
                                       [[16 * (ACH + 32), 128], [1, 128]]),
                                    IDB[:])
                            return r
                        t_utt = p.t("tensor", em_utt,
                                    [t_uf, t_idb, prev_z[0]] +
                                    ([ut_ev[-1]] if ut_ev else grp_tasks[-1:]))
                        t_ute = p.t("vector", lambda e, kt0=kt0, nk=nk: e.tensor_copy(
                            AP(UT, kt0 * 128,
                            [[((ACH + 32) // 8) * 128, 128], [1, nk * 128]]),
                            pstb[:, 0:nk * 128]), [t_utt])
                        ut_ev.append(t_ute)
                    prev_utt[0] = ut_ev[-1]
                    if cfg.get("dbg") and c0 == 0 and q == 0:
                        p.dma("sync", lambda e: e.dma_start(
                            out=dbg_d[3][:, 0:4096],
                            in_=AP(UT, 0, [[((ACH + 32) // 8) * 128, 128],
                                           [1, 4096]])), [ut_ev[-1]])
                        p.dma("sync", lambda e: e.dma_start(
                            out=dbg_d[4][:, 0:NUTT * 128 - 4096],
                            in_=AP(UT, 4096, [[((ACH + 32) // 8) * 128, 128],
                                              [1, NUTT * 128 - 4096]])),
                            [ut_ev[-1]])
                        p.dma("sync", lambda e: e.dma_start(
                            out=dbg_d[7][:, 1024:1024 + ACH + 64],
                            in_=VB[:, 0:ACH + 64]), [t_uf])

                    psz = psb if q % 2 == 0 else psc
                    def em_z(e, q=q, CL=CL, psz=psz, NBLK=NBLK):
                        for blk in range(NBLK):
                            grp, sub = blk // 4, blk % 4
                            for h in range(2):
                                for qh in range(4):
                                    r = e.matmul(
                                        psz[64 * h:64 * h + 64,
                                            16 * blk:16 * blk + 16],
                                        AP(UT, (2 * blk + qh) * 128 + 64 * h,
                                           [[((ACH + 32) // 8) * 128, 128],
                                            [1, 64]]),
                                        AP(PSTS, (grp * 4 + qh) * 128 + 32 * sub
                                           + 16 * h,
                                           [[(ACH // 64) * 4 * 128, 128],
                                            [1, 16]]),
                                        start=(blk == 0 and qh == 0),
                                        stop=(blk == NBLK - 1 and qh == 3),
                                        tile_position=(0, 64 * h),
                                        skip_group_check=True)
                        return r
                    t_z = p.t("tensor", em_z,
                              ut_ev + grp_tasks + [t_zevac[q % 2]])
                    if c0 == 0 and q == 0:
                        t_z.label = "t_z c0 q0"
                        for _i, _t in enumerate(ut_ev):
                            _t.label = f"ut_ev[{_i}] c0 q0" 
                    prev_z[0] = t_z
                    t_ze = p.t("vector", lambda e, q=q, psz=psz, CL=CL: e.tensor_copy(
                        ZB[:, q, 0:CL], psz[:, 0:CL]), [t_z])
                    t_zevac[q % 2] = t_ze
                    t_zb[q] = t_ze
                    q_score_done.append(t_ze)
                    if c0 == 0 and q == 0:
                        t_ze.label = "t_ze c0 q0"
                    if cfg.get("dbg") and c0 == 0 and q == 0:
                        p.dma("sync", lambda e, CL=CL: e.dma_start(
                            out=dbg_d[7][:, 0:CL], in_=ZB[:, 0, 0:CL]), [t_ze])

                wo_ev = []
                for pp in range(4):
                    def em_wo(e, pp=pp, CL=CL):
                        for dd in range(2):
                            d8 = pp * 2 + dd
                            for di in range(8):
                                r = e.matmul(pst2[:, dd * 512:dd * 512 + CL],
                                             WOT[:, di, d8 * 128:(d8 + 1) * 128],
                                             ZB[:, di, 0:CL],
                                             start=(di == 0), stop=(di == 7))
                        return r
                    mmw = p.t("tensor", em_wo,
                              list(t_zb.values()) + [ld_wo] + wo_ev[-1:])
                    t_we = p.t("vector", lambda e, pp=pp, CL=CL: (
                        e.tensor_copy(WOU[:, pp * 2, 0:CL], pst2[:, 0:CL]),
                        e.tensor_copy(WOU[:, pp * 2 + 1, 0:CL],
                                      pst2[:, 512:512 + CL]))[-1], [mmw])
                    wo_ev.append(t_we)

                prev_w = [None]
                prev_fin = [None, None]
                for si in range(CL // 128):
                    sub = (c0 // 128) + si
                    buf = X0 if sub % 2 == 0 else X1
                    ldx = p.dma("sync", lambda e, sub=sub, b=buf: e.dma_start(
                        out=b[:], in_=xA_d[sub * 128:(sub + 1) * 128, :]),
                        wo_ev + [prev_fin[sub % 2]])
                    ldh = p.dma("sync", lambda e, sub=sub: e.dma_start(
                        out=XH2[:], in_=xh2_d[sub * 128:(sub + 1) * 128, :]),
                        wo_ev + [prev_chunk_tail[0], prev_w[0]])

                    def em_trw(e, si=si, CL=CL):
                        for d8 in range(8):
                            r = e.transpose(pst2[:, d8 * 128:(d8 + 1) * 128],
                                            WOU[:, d8, si * 128:si * 128 + 128],
                                            IDF[:])
                        return r
                    t_trw = p.t("tensor", em_trw, wo_ev + [ld_id, prev_w[0]])
                    t_w = p.t("vector", lambda e: e.tensor_tensor(
                        WS[:], pst2[:, 0:1024], XH2[:], ALU.add), [t_trw, ldh])
                    prev_w[0] = t_w
                    t_sq = p.t("scalar", lambda e: e.activation(
                        TT_[:], WS[:], AF.Square, accum_out=C["SUMQ"][:, 0:1]), [t_w])

                    def em_st1(e):
                        e.tensor_reduce(C["SUMX"][:], WS[:], AX.X, ALU.add)
                        e.tensor_scalar(C["MU"][:], C["SUMX"][:], 1.0 / D, None,
                                        ALU.mult)
                        e.tensor_scalar(C["TS0"][:], C["SUMQ"][:], 1.0 / D, None,
                                        ALU.mult)
                        e.tensor_tensor(C["TS1"][:], C["MU"][:], C["MU"][:], ALU.mult)
                        e.tensor_tensor(C["TS0"][:], C["TS0"][:], C["TS1"][:],
                                        ALU.subtract)
                        return e.tensor_scalar(C["TS0"][:], C["TS0"][:], 1e-5, None,
                                               ALU.add)
                    t_st1 = p.t("vector", em_st1, [t_w, t_sq])
                    t_sqr = p.t("scalar", lambda e: e.activation(
                        C["TS2"][:], C["TS0"][:], AF.Sqrt), [t_st1])

                    def em_st2(e):
                        e.reciprocal(C["RST"][:], C["TS2"][:])
                        for _ in range(2):
                            e.tensor_tensor(C["TS1"][:], C["RST"][:], C["RST"][:],
                                            ALU.mult)
                            e.tensor_tensor(C["TS1"][:], C["TS1"][:], C["TS0"][:],
                                            ALU.mult)
                            e.tensor_scalar(C["TS1"][:], C["TS1"][:], -0.5, 1.5,
                                            ALU.mult, ALU.add)
                            e.tensor_tensor(C["RST"][:], C["RST"][:], C["TS1"][:],
                                            ALU.mult)
                        e.tensor_tensor(C["BCC"][:], C["MU"][:], C["RST"][:], ALU.mult)
                        return e.tensor_scalar(C["BCC"][:], C["BCC"][:], -1.0, None,
                                               ALU.mult)
                    t_st2 = p.t("vector", em_st2, [t_sqr])
                    t_v = p.t("scalar", lambda e: e.activation(
                        VV_[:], WS[:], AF.Identity, bias=C["BCC"][:, 0:1],
                        scale=C["RST"][:, 0:1]), [t_st2])

                    def em_fin(e, b=buf):
                        e.tensor_tensor(VV_[:], VV_[:], GB[:], ALU.mult)
                        e.tensor_tensor(VV_[:], VV_[:], BB[:], ALU.add)
                        return e.tensor_tensor(WS[:], b[:], VV_[:], ALU.add)
                    t_fin = p.t("vector", em_fin, [t_v, ld_gb, ld_bb, ldx])
                    st = p.dma("sync", lambda e, sub=sub: e.dma_start(
                        out=xB_d[sub * 128:(sub + 1) * 128, :], in_=WS[:]), [t_fin])
                    prev_fin[sub % 2] = st
                    st_tasks.append(st)
                prev_chunk_tail[0] = st_tasks[-1]
            barrier = p.tails()
        return st_tasks, barrier

    if dev_stop == "p0":
        bar = p.tails()
        d1 = p.dma("sync", lambda e: e.dma_start(
            out=AP(out.tensor, 0, [[D, 128], [1, 32]]), in_=CORR1[:]), bar)
        d2 = p.dma("sync", lambda e: e.dma_start(
            out=AP(out.tensor, 32, [[D, 128], [1, 8]]), in_=CORR2[:]), [d1])
        d3 = p.dma("sync", lambda e: e.dma_start(
            out=AP(out.tensor, 40, [[D, 128], [1, 1]]), in_=SW1[:],
            allow_slow_non_contiguous=True), [d2])
        d4 = p.dma("sync", lambda e: e.dma_start(
            out=AP(out.tensor, 41, [[D, 128], [1, 1]]), in_=ISW1[:],
            allow_slow_non_contiguous=True), [d3])
        d5 = p.dma("sync", lambda e: e.dma_start(
            out=AP(out.tensor, 42, [[D, 128], [1, 1]]), in_=SW2[:],
            allow_slow_non_contiguous=True), [d4])
        d6 = p.dma("sync", lambda e: e.dma_start(
            out=AP(out.tensor, 43, [[D, 128], [1, 1]]), in_=ISW2[:],
            allow_slow_non_contiguous=True), [d5])
        d7 = p.dma("sync", lambda e: e.dma_start(
            out=AP(out.tensor, 128 * D, [[D, 128], [1, D]]), in_=GB[:]), [d6])
        d8 = p.dma("sync", lambda e: e.dma_start(
            out=AP(out.tensor, 256 * D, [[D, 128], [1, D]]), in_=BB[:]), [d7])
        d9 = p.dma("sync", lambda e: e.dma_start(
            out=AP(out.tensor, 384 * D, [[D, 128], [1, 128]]),
            in_=AP(WCV, 0, [[128, 128], [1, 128]])), [d8])
        p.t("gpsimd", lambda e: e.engine_nop(), [d9])
        p.finalize()
        return nc
    def _dump_dram(src_ap, ncols, deps):
        th = src_ap.tensor
        rowlen = th.shape[2]
        DBG16x = nc.alloc_sbuf_tensor("DBG16x", [128, ncols], th.dtype)
        DBGFx = nc.alloc_sbuf_tensor("DBGFx", [128, ncols], F32)
        nsub = ncols // D
        prev = None
        for di in range(8):
            ld = p.dma("sync", lambda e, di=di: e.dma_start(
                out=DBG16x.ap()[:],
                in_=AP(th, di * 128 * rowlen, [[rowlen, 128], [1, ncols]])),
                deps + ([prev] if prev else []))
            cv = p.t("vector", lambda e: e.tensor_copy(DBGFx.ap()[:],
                                                       DBG16x.ap()[:]), [ld])
            prev = p.dma("sync", lambda e, di=di: e.dma_start(
                out=AP(out.tensor, di * 128 * nsub * D,
                       [[nsub * D, 128], [1, nsub * D]]),
                in_=DBGFx.ap()[:]), [cv])
        p.t("gpsimd", lambda e: e.engine_nop(), [prev])
        p.finalize()
        return nc

    def _dump_srcs():
        return dict(wq1T=wq1T_d, wq2T=wq2T_d, xh2T=xh2T_d,
                    **{f"w{n}T": wT_d[n] for n in wT_d},
                    **{f"spl{n}": spl[n] for n in spl})

    if dev_stop and dev_stop.startswith("dump0:"):
        nm, ncols = dev_stop.split(":")[1], int(dev_stop.split(":")[2])
        return _dump_dram(_dump_srcs()[nm], ncols, p.tails())
    if dev_stop in ("p0w", "p0v"):
        DBG16 = nc.alloc_sbuf_tensor("DBG16", [128, DH], FP16)
        DBGF = nc.alloc_sbuf_tensor("DBGF", [128, DH], F32)
        bar = p.tails()
        prev = None
        if dev_stop == "p0w":
            for di in range(8):
                ld = p.dma("sync", lambda e, di=di: e.dma_start(
                    out=DBG16[:], in_=wq1T_d[di]),
                    bar + ([prev] if prev else []))
                cv = p.t("vector", lambda e: e.tensor_copy(DBGF[:], DBG16[:]), [ld])
                prev = p.dma("sync", lambda e, di=di: e.dma_start(
                    out=AP(out.tensor, di * 512 * D,
                           [[4 * D, 128], [D, 4], [1, D]]),
                    in_=DBGF[:]), [cv])
        else:
            for s in range(32):
                ld = p.dma("sync", lambda e, s=s: e.dma_start(
                    out=DBG16[:, 0:D], in_=wq2T_d[s]),
                    bar + ([prev] if prev else []))
                cv = p.t("vector", lambda e: e.tensor_copy(DBGF[:, 0:D],
                                                           DBG16[:, 0:D]), [ld])
                prev = p.dma("sync", lambda e, s=s: e.dma_start(
                    out=AP(out.tensor, s * 128 * D, [[D, 128], [1, D]]),
                    in_=DBGF[:, 0:D]), [cv])
        p.t("gpsimd", lambda e: e.engine_nop(), [prev])
        p.finalize()
        return nc
    p1_st, bar1 = ffn_phase(x_in, xA_d, own_only=False, entry_deps=p0_done, tag="a")
    if dev_stop == "p1":
        fin = p.dma("sync", lambda e: e.dma_start(
            out=out[:, :], in_=xA_d[OWN0:OWN0 + OWN, :]), p1_st)
        p.t("gpsimd", lambda e: e.engine_nop(), [fin])
        p.finalize()
        return nc

    if dev_stop and dev_stop.startswith("dump1:"):
        nm, ncols = dev_stop.split(":")[1], int(dev_stop.split(":")[2])
        return _dump_dram(_dump_srcs()[nm], ncols, p.tails())
    p2_st, bar2, zh = proj_phase(bar1)
    if dev_stop and dev_stop.startswith("dump2:"):
        nm, ncols = dev_stop.split(":")[1], int(dev_stop.split(":")[2])
        return _dump_dram(_dump_srcs()[nm], ncols, p.tails())
    if dev_stop == "p2":
        fin = p.dma("sync", lambda e: e.dma_start(
            out=out[:, :], in_=xh2_d[OWN0:OWN0 + OWN, :]), p2_st)
        p.t("gpsimd", lambda e: e.engine_nop(), [fin])
        p.finalize()
        return nc

    p3_st, bar3 = attn_phase(bar2 + zh)
    if dev_stop and dev_stop.startswith("dumpA:"):
        nm, ncols = dev_stop.split(":")[1], int(dev_stop.split(":")[2])
        return _dump_dram(dbg_d, ncols, p.tails())
    if dev_stop == "p3":
        fin = p.dma("sync", lambda e: e.dma_start(
            out=out[:, :], in_=xB_d[OWN0:OWN0 + OWN, :]), p3_st)
        p.t("gpsimd", lambda e: e.engine_nop(), [fin])
        p.finalize()
        return nc

    p4_st, bar4 = ffn_phase(xB_d, out, own_only=True, entry_deps=bar3, tag="b")
    p.t("gpsimd", lambda e: e.engine_nop(), p4_st)
    p.finalize()
    return nc


T_EXT_FULL = 4352
OWN_FULL = 4096
OWN0_FULL = 128


def _build_masks():
    mask = np.zeros((2, 128, 992), np.float32)
    p = np.arange(128)
    blk4 = p // 32
    u = (p % 32) % 16
    gi = np.arange(31)
    mh = np.arange(32)
    g = gi - 15
    MH = mh[None, None, :]
    U = u[:, None, None]
    G = g[None, :, None]
    B4 = blk4[:, None, None]
    valid = ((U + 1 <= MH) & (MH <= U + 16) &
             (U + 1 <= MH - G) & (MH - G <= U + 16))
    kkey = 16 * B4 + MH - 16
    kpkey = kkey - G
    row0 = valid & (kkey >= 0) & (kpkey >= 0)
    mask[1] = np.where(valid, 0.0, -30000.0).reshape(128, 992)
    mask[0] = np.where(row0, 0.0, -30000.0).reshape(128, 992)
    psmask = np.zeros((128, 512), np.float32)
    mm2, gg2 = np.meshgrid(np.arange(32), np.arange(16), indexing="ij")
    psmask[:, :] = (mm2 >= gg2).astype(np.float32).reshape(1, 512)
    return mask, psmask


_NC_CACHE = {}


def _get_nc():
    key = "full"
    if key not in _NC_CACHE:
        cfg = dict(T_EXT=T_EXT_FULL, OWN0=OWN0_FULL, OWN=OWN_FULL,
                   maskgrp_first=OWN0_FULL // 64, dev_stop=None)
        _NC_CACHE[key] = (build(cfg), cfg)
    return _NC_CACHE[key]


def kernel(L0_attn_x, L0_attn_Wq, L0_attn_Wk, L0_attn_Wv, L0_attn_Wkp,
           L0_attn_Wo, L0_attn_gamma, L0_attn_beta, L0_ffn_w1, L0_ffn_conv_w,
           L0_ffn_conv_b, L0_ffn_w2):
    from concourse.bass_utils import run_bass_kernel_spmd
    x = np.ascontiguousarray(np.asarray(L0_attn_x, dtype=np.float32))
    B, S, Dm = x.shape
    assert (B, S, Dm) == (4, 8192, 1024)
    mask, psmask = _build_masks()
    mask_noseq = mask.copy()
    mask_noseq[0] = mask_noseq[1]
    ident = np.eye(128, dtype=np.float32)
    common = {
        "w_q": np.ascontiguousarray(L0_attn_Wq, dtype=np.float32),
        "w_k": np.ascontiguousarray(L0_attn_Wk, dtype=np.float32),
        "w_kp": np.ascontiguousarray(L0_attn_Wkp, dtype=np.float32),
        "w_v": np.ascontiguousarray(L0_attn_Wv, dtype=np.float32),
        "w_o": np.ascontiguousarray(L0_attn_Wo, dtype=np.float32),
        "gamma": np.ascontiguousarray(L0_attn_gamma, dtype=np.float32),
        "beta": np.ascontiguousarray(L0_attn_beta, dtype=np.float32),
        "w1": np.ascontiguousarray(L0_ffn_w1, dtype=np.float32),
        "conv_w": np.ascontiguousarray(
            np.asarray(L0_ffn_conv_w, dtype=np.float32).reshape(4096, 3)),
        "conv_b": np.ascontiguousarray(L0_ffn_conv_b, dtype=np.float32),
        "w2": np.ascontiguousarray(L0_ffn_w2, dtype=np.float32),
        "ident": ident,
        "psmask": psmask,
    }
    in_maps = []
    for core in range(8):
        b, half = core // 2, core % 2
        lo = half * 4096 - 128
        hi = lo + T_EXT_FULL
        xe = np.zeros((T_EXT_FULL, Dm), np.float32)
        s0, s1 = max(lo, 0), min(hi, S)
        xe[s0 - lo:s1 - lo] = x[b, s0:s1]
        m = dict(common)
        m["x_in"] = xe
        m["mask"] = mask if half == 0 else mask_noseq
        in_maps.append(m)
    nc, cfg = _get_nc()
    res = run_bass_kernel_spmd(nc, in_maps, core_ids=list(range(8)))
    outp = np.empty((4, 8192, 1024), np.float32)
    for core in range(8):
        b, half = core // 2, core % 2
        outp[b, half * 4096:(half + 1) * 4096] = np.asarray(
            res.results[core]["out"], dtype=np.float32)
    return outp
